# version 05
# baseline (speedup 1.0000x reference)
import math
import numpy as np
import ml_dtypes
import concourse.bass as bass
import concourse.mybir as mybir
from concourse.bass_utils import run_bass_kernel_spmd

F32 = mybir.dt.float32
BF16 = mybir.dt.bfloat16
AF = mybir.ActivationFunctionType
ALU = mybir.AluOpType
AXX = mybir.AxisListType.X

B, T, D = 4, 4096, 1024
G = 8
H, HD = 16, 64
TC = 128
NSUB = TC // 128
SL = 512 // TC
RND = G // SL
EPS_LN = 1e-5
EPS_A = 1e-6
BETA = 0.3
THR = 0.2
MAGIC = 12582912.0
E10 = float(np.exp(np.float32(10.0)))
LN1E4 = float(np.log(10000.0))
TWO_PI = 2.0 * math.pi
INV_2PI = float(1.0 / TWO_PI)
CW1 = float(np.float32(6.28125))
CW2 = float(np.float32(1.9353628e-3))
CW3 = float(TWO_PI - (np.float64(np.float32(CW1)) + np.float64(np.float32(CW2))))

ENGS = ("sync", "tensor", "vector", "scalar", "gpsimd")


class Plan:

    AUTOSYNC = ("vector", "scalar", "gpsimd")

    def __init__(self):
        self.q = {e: [] for e in ENGS}
        self.cnt = {e: 0 for e in ENGS}
        self.store = 0
        self.pending = {e: 0 for e in ENGS}
        self.pend_dma = {e: 0 for e in ENGS}
        self.prev2 = {e: [0, 0] for e in ENGS}
        self.cnt["gdma"] = 0
        self.bank_read = {}

    def wbank(self, bank):
        br = self.bank_read.get(bank)
        return [br] if br else []

    def rbank(self, bank, eng, mark):
        self.bank_read[bank] = (eng, mark)
        return mark

    def op(self, eng, fn, waits=(), inc=1, bare=False):
        waits = list(waits)
        if bare:
            if self.pend_dma.get(eng):
                waits.append(("gdma", self.pend_dma[eng]))
                self.pend_dma[eng] = 0
            self.q[eng].append((fn, tuple(w for w in waits if w[1] > 0), 0, eng))
            return self.cnt[eng]
        if eng in self.AUTOSYNC:
            dep = max(self.prev2[eng][0], self.pending[eng])
            self.pending[eng] = 0
            if dep > 0:
                waits.append((eng, dep))
            if self.pend_dma.get(eng):
                waits.append(("gdma", self.pend_dma[eng]))
                self.pend_dma[eng] = 0
        self.q[eng].append((fn, tuple(w for w in waits if w[1] > 0), inc, eng))
        self.cnt[eng] += inc
        if inc:
            self.prev2[eng] = [self.prev2[eng][1], self.cnt[eng]]
        return self.cnt[eng]

    def dma(self, eng, out, in_, waits=(), no_pend=False):
        def fn(e):
            return e.dma_start(out=out, in_=in_)
        key = "gdma" if eng == "gpsimd" else eng
        self.q[eng].append((fn, tuple(w for w in waits if w[1] > 0), 16, key))
        self.cnt[key] += 16
        if eng in self.AUTOSYNC and not no_pend:
            self.pend_dma[eng] = self.cnt[key]
        return self.cnt[key]

    def store_dma(self, out, in_, waits=()):
        def fn(e):
            return e.dma_start(out=out, in_=in_)
        self.q["sync"].append((fn, tuple(w for w in waits if w[1] > 0), 16, "store"))
        self.store += 16
        return self.store

    def fence(self, eng):
        val = self.cnt[eng]
        self.q[eng].append(((lambda e: None), ((eng, val),), 0, eng))

    def guard(self, eng, n=2):
        self.pending[eng] = self.cnt[eng]

    def mark(self, eng):
        return self.cnt[eng]


def build_program(n_chunks, debug=False, bass_debug=False, stages=3):
    TT = n_chunks * TC
    AJ = TC + 1
    nc = bass.Bass("TRN2", target_bir_lowering=False, debug=bass_debug)

    x0_h = nc.dram_tensor("x0", [TT, D], F32, kind="ExternalInput").ap()
    w_h = {}
    for nm in ("wrt", "wkt", "wvt", "wtt", "wq", "wk", "wv", "wo"):
        w_h[nm] = nc.dram_tensor(nm, [D, D], F32, kind="ExternalInput").ap()
    wg1_h = nc.dram_tensor("wg1", [D, HD], F32, kind="ExternalInput").ap()
    wg2_h = nc.dram_tensor("wg2", [HD, D], F32, kind="ExternalInput").ap()
    rhsT_h = nc.dram_tensor("rhsT", [D, D], F32, kind="ExternalInput").ap()
    vecs_h = nc.dram_tensor("vecs", [128, 48], F32, kind="ExternalInput").ap()
    ident_h = nc.dram_tensor("ident", [128, 128], F32, kind="ExternalInput").ap()
    tri_h = nc.dram_tensor("trimask", [128, 128], F32, kind="ExternalInput").ap()
    spk_h = nc.dram_tensor("spk_scr", [n_chunks, 128, G * TC], BF16, kind="Internal").ap()
    out_h = nc.dram_tensor("out", [TT, D], F32, kind="ExternalOutput").ap()

    dbg = {}
    if debug:
        for name, dt in [("d_r", F32), ("d_ek", F32), ("d_v", F32), ("d_td", F32),
                         ("d_a", F32), ("d_b", F32), ("d_c5", F32), ("d_mem", F32),
                         ("d_spk", BF16), ("d_x1", F32), ("d_f1", F32), ("d_x2", F32),
                         ("d_qr", BF16), ("d_kr", BF16), ("d_ao", F32),
                         ("d_gate", F32), ("d_x3", F32), ("d_ln2", BF16)]:
            dbg[name] = nc.dram_tensor(name, [128, G * TC], dt, kind="ExternalOutput").ap()

    P = Plan()

    import contextlib
    with contextlib.ExitStack() as stk:
        E = stk.enter_context

        WA = E(nc.sbuf_tensor("WA", [128, 4 * 8 * 1024], BF16))
        WB = WA
        QR = E(nc.sbuf_tensor("QR", [128, 8 * 1024], BF16))
        WGT = E(nc.sbuf_tensor("WGT", [128, 8 * 64 + 1024], BF16))
        AUGW = E(nc.sbuf_tensor("AUGW", [1, 9 * 1024], BF16))
        ROWB = E(nc.sbuf_tensor("ROWB", [1, 6 * (TC + 1)], BF16))
        CARY = E(nc.sbuf_tensor("CARY", [128, 24], F32))
        CS = E(nc.sbuf_tensor("CS", [128, 2 * TC], BF16))
        VEC = E(nc.sbuf_tensor("VEC", [128, 96], F32))
        VECB = E(nc.sbuf_tensor("VECB", [128, 32], BF16))
        VGD = E(nc.sbuf_tensor("VGD", [128, 16], F32))
        VRQ = E(nc.sbuf_tensor("VRQ", [128, 24], F32))
        VIF = E(nc.sbuf_tensor("VIF", [128, 1], F32))
        ONER = E(nc.sbuf_tensor("ONER", [1, 128], F32))
        IDENT = E(nc.sbuf_tensor("IDENT", [128, 128], F32))
        IDENTB = E(nc.sbuf_tensor("IDENTB", [128, 128], BF16))
        TRI = E(nc.sbuf_tensor("TRI", [128, 128], F32))
        ROWS = E(nc.sbuf_tensor("ROWS", [1, 15 * AJ], F32))
        X0TOK = E(nc.sbuf_tensor("x0tok", [128, NSUB * 1024], F32))
        XIN = E(nc.sbuf_tensor("xin", [128, 2 * NSUB * 1024], F32))
        REP = E(nc.sbuf_tensor("REP", [128, 4 * TC], F32))
        NB = G * TC
        fb = {}
        for nm in ("x0cm", "r", "ek", "v", "td", "aA", "bA",
                   "s1", "s2", "mhA", "x1", "x2"):
            fb[nm] = E(nc.sbuf_tensor("f_" + nm, [128, NB], F32))
        fb["gate"] = fb["r"]
        bb = {}
        for nm in ("xa", "spk", "lnb", "qr", "kr", "ktok",
                   "vtok", "esb", "q3"):
            bb[nm] = E(nc.sbuf_tensor("b_" + nm, [128, NB], BF16))
        XRSF = E(nc.sbuf_tensor("f_xrsA", [128, G * (TC + 1)], F32))
        SQB = E(nc.sbuf_tensor("f_sqb", [128, G * TC], F32))
        SSB = E(nc.sbuf_tensor("ssb", [128, 512], BF16))
        SACC = E(nc.sbuf_tensor("sacc", [128, 512], F32))
        G1B = E(nc.sbuf_tensor("g1b", [64, TC], BF16))

        PP0 = E(nc.psum_tensor("PP0", [128, 512], F32))
        PP1 = E(nc.psum_tensor("PP1", [128, 512], F32))
        PP2 = E(nc.psum_tensor("PP2", [128, 512], F32))
        PP3 = E(nc.psum_tensor("PP3", [128, 512], F32))
        PTF = E(nc.psum_tensor("PTF", [128, 512], F32))
        PTB = E(nc.psum_tensor("PTB", [128, 512], F32))
        PX = E(nc.psum_tensor("PX", [128, 512], F32))
        PS = E(nc.psum_tensor("PS", [128, 512], F32))

        sems = {e: E(nc.semaphore("s_" + e)) for e in ENGS}
        sems["gdma"] = E(nc.semaphore("s_gdma"))
        s_store = E(nc.semaphore("s_st"))
        block = E(nc.Block())


        def _finish():
            P.q["gpsimd"].append(((lambda e: e.wait_ge(s_store, P.store)), (), 0, "gpsimd"))
            def replay(name):
                def run(eng):
                    wmax = {}
                    for fn, waits, inc, semkey in P.q[name]:
                        for weng, val in waits:
                            if wmax.get(weng, -1) >= val:
                                continue
                            wmax[weng] = val
                            sem = s_store if weng == "store" else sems[weng]
                            eng.wait_ge(sem, val)
                        inst = fn(eng)
                        if inc and inst is not None:
                            inst.then_inc(s_store if semkey == "store" else sems[semkey], inc)
                return run
            block.sync(replay("sync"))
            block.tensor(replay("tensor"))
            block.vector(replay("vector"))
            block.scalar(replay("scalar"))
            block.gpsimd(replay("gpsimd"))
            return nc
        def vo(fn, waits=(), inc=1, rb=None, bare=False):
            m = P.op("vector", fn, waits, inc, bare=bare)
            if rb:
                P.rbank(rb, "vector", m)
            return m

        def ao(fn, waits=(), inc=1, rb=None, bare=False):
            m = P.op("scalar", fn, waits, inc, bare=bare)
            if rb:
                P.rbank(rb, "scalar", m)
            return m

        def po(fn, waits=(), inc=1, rb=None, bare=False):
            m = P.op("gpsimd", fn, waits, inc, bare=bare)
            if rb:
                P.rbank(rb, "gpsimd", m)
            return m

        def to(fn, waits=(), inc=1, wb=None):
            w = list(waits)
            if wb:
                w += P.wbank(wb)
            return P.op("tensor", fn, w, inc)
        ld = lambda out, in_, waits=(): P.dma("sync", out, in_, waits)
        gld = lambda out, in_, waits=(): P.dma("gpsimd", out, in_, waits)

        def dstore(out, in_, waits=(), flush=True):
            v = P.store_dma(out, in_, waits)
            if flush:
                for eng in ("vector", "scalar", "gpsimd", "tensor", "sync"):
                    P.q[eng].append(((lambda e: None), (("store", v),), 0, eng))
            return v

        def dflush():
            for eng in ("vector", "scalar", "gpsimd", "tensor", "sync"):
                P.q[eng].append(((lambda e: None), (("store", P.store),), 0, eng))

        def row(slot, n=TC, off=0):
            return ROWS[0:1, slot * AJ + off: slot * AJ + off + n]

        def rowb(slot, n=TC, off=0):
            return ROWB[0:1, slot * AJ + off: slot * AJ + off + n]

        import os as _os
        _kcut = int(_os.environ.get("KCUT", "0"))
        cons_ld = ld(VEC[:, 0:48], vecs_h)
        cons_ld = ld(IDENT[:, :], ident_h)
        cons_ld = ld(TRI[:, :], tri_h)

        idb = vo(lambda e: e.tensor_copy(out=IDENTB[:, :], in_=IDENT[:, :]),
                 waits=[("sync", cons_ld)])
        vo(lambda e: e.tensor_copy(out=VECB[:, 0:1], in_=VEC[:, 40:41]),
           waits=[("sync", cons_ld)])
        vo(lambda e: e.tensor_copy(out=VECB[:, 1:9], in_=VEC[:, 32:40]),
           waits=[("sync", cons_ld)])
        vcb = vo(lambda e: e.tensor_copy(out=VECB[:, 9:25], in_=VEC[:, 0:16]),
                 waits=[("sync", cons_ld)])
        if not (_kcut & 4):
            po(lambda e: e.memset(CARY[:, :], 0.0))
            onr = po(lambda e: e.memset(ONER[0:1, :], 1.0))
            po(lambda e: e.memset(SACC[:, :], 0.0))
            po(lambda e: e.memset(XRSF[:, 0:G * AJ:AJ], 0.0))
            pz = po(lambda e: e.memset(ROWS[0:1, :], 0.0))
            pz = po(lambda e: e.memset(ROWB[0:1, :], 0.0))
            P.guard("gpsimd")
            pz = po(lambda e: e.memset(ROWB[0:1, 4 * AJ + 1:4 * AJ + 1 + TC], 1.0))
        else:
            onr = pz = 0
        if not (_kcut & 2):
            gde = ao(lambda e: e.activation(out=VGD[:, 0:8], in_=VEC[:, 16:24],
                                            func=AF.Exp), waits=[("sync", cons_ld)])
            gd5 = vo(lambda e: e.tensor_scalar(out=VGD[:, 8:16], in0=VGD[:, 0:8],
                                               scalar1=THR, scalar2=None,
                                               op0=ALU.mult, op1=ALU.bypass),
                     waits=[("scalar", gde)])
        else:
            gde = gd5 = 0

        if not (_kcut & 32):
            P.fence("sync")
        _sub = int(_os.environ.get("KSUB", "9"))
        if _sub < 1:
            return _finish()
        _kq = int(_os.environ.get("KQ", "9"))
        tq_ev = 0
        rql = 0
        for i in range(int(_os.environ.get("KQI", str(G)))):
            if not (_kcut & 16):
                rql = P.dma("gpsimd", fb["x1"][:, 0:1024],
                            rhsT_h[i * 128:(i + 1) * 128, :],
                            waits=[("scalar", tq_ev)] if (i > 0 and tq_ev) else (),
                            no_pend=True)
            rq = 0
            if not (_kcut & 8):
                if _kq < 2:
                    continue
                am = vo(lambda e, i=i: e.tensor_reduce(
                    out=VRQ[:, 0 + i:1 + i], in_=fb["x1"][:, 0:1024],
                    axis=AXX, op=ALU.max, apply_absolute_value=True),
                    waits=[("gdma", rql)])
                if _kq < 3:
                    continue
                P.guard("vector")
                am = vo(lambda e, i=i: e.tensor_scalar(
                    out=VRQ[:, 8 + i:9 + i], in0=VRQ[:, 0 + i:1 + i], scalar1=1e-12,
                    scalar2=1.0 / 127.0, op0=ALU.max, op1=ALU.mult))
                P.guard("vector")
                am = vo(lambda e, i=i: e.reciprocal(out=VRQ[:, 16 + i:17 + i],
                                                    in_=VRQ[:, 8 + i:9 + i]))
                if _kq < 4:
                    continue
                qm = ao(lambda e, i=i: e.activation(
                    out=fb["x2"][:, 0:1024], in_=fb["x1"][:, 0:1024],
                    func=AF.Identity, scale=VRQ[:, 16 + i:17 + i]),
                    waits=[("vector", am)])
                if _kq < 5:
                    continue
                rq = vo(lambda e, i=i: e.tensor_scalar(
                    out=fb["x2"][:, 0:1024], in0=fb["x2"][:, 0:1024],
                    scalar1=MAGIC, scalar2=MAGIC, op0=ALU.add, op1=ALU.subtract),
                    waits=[("scalar", qm)])
                P.guard("vector")
                rq = vo(lambda e, i=i: e.tensor_scalar(
                    out=bb["q3"][:, 0:1024], in0=fb["x2"][:, 0:1024],
                    scalar1=127.0, scalar2=-127.0, op0=ALU.min, op1=ALU.max))
                if _kq < 6:
                    continue
            for j in range(int(_os.environ.get("KQT", str(G)))):
                slot = j % 4
                w = []
                if j == 0:
                    if rq:
                        w.append(("vector", rq))
                    w.append(("vector", idb))
                if j >= 4 and tq_ev > 3:
                    w.append(("scalar", tq_ev - 3))
                tqm = to(lambda e, j=j, slot=slot: e.matmul(
                    PTB[:, slot * 128:slot * 128 + 128],
                    bb["q3"][:, j * 128:(j + 1) * 128],
                    IDENTB[:, :], start=True, stop=True), waits=w, wb="PTB")
                tq_ev = ao(lambda e, i=i, j=j, slot=slot: e.activation(
                    out=QR[:, (j * 8 + i) * 128:(j * 8 + i + 1) * 128],
                    in_=PTB[:, slot * 128:slot * 128 + 128], func=AF.Identity),
                    waits=[("tensor", tqm)], rb="PTB")

        wa_done = 0
        if not (_kcut & 1):
            for mi, nm in enumerate(("wrt", "wkt", "wvt", "wtt")):
                for i in range(G):
                    wa_done = gld(WA[:, (mi * 8 + i) * 1024:(mi * 8 + i + 1) * 1024],
                                  w_h[nm][i * 128:(i + 1) * 128, :])
            for i in range(G):
                gld(WGT[:, i * 64:(i + 1) * 64], wg1_h[i * 128:(i + 1) * 128, :])
            wg_done = gld(WGT[0:64, 512:512 + 1024], wg2_h[:, :])
            wa_done = wg_done
        else:
            wg_done = 0

        if _sub < 2:
            return _finish()
        def emit_wsum(w_sb, base, mixcol, slot, ncols=1024, extra_w=()):
            ev = 0
            for half in range(2 if ncols > 512 else 1):
                hw = min(512, ncols)
                mm = 0
                for i in range(G):
                    w = (list(extra_w) + [("sync", cons_ld), ("vector", vcb)]) if i == 0 else []
                    w = list(w) + [("tensor", P.wsum_t)]
                    if i == 0:
                        w.append(("vector", P.wsum_v))
                    lcol = VECB[:, 0:1] if mixcol is None else VECB[:, mixcol + i:mixcol + i + 1]
                    mm = to(lambda e, i=i, half=half, hw=hw, lcol=lcol: e.matmul(
                        PX[0:1, 0:hw], lcol,
                        w_sb[:, base + i * ncols + half * 512:base + i * ncols + half * 512 + hw],
                        start=(i == 0), stop=(i == G - 1)),
                        waits=w, wb="PX")
                    P.wsum_t = mm
                ev = vo(lambda e, half=half, hw=hw: e.tensor_copy(
                    out=AUGW[0:1, slot * 1024 + half * 512:slot * 1024 + half * 512 + hw],
                    in_=PX[0:1, 0:hw]), waits=[("tensor", mm)], rb="PX")
                P.wsum_v = ev
            return ev

        lw = 0
        P.wsum_t = 0
        P.wsum_v = 0
        for m_i in range(4):
            lw = emit_wsum(WA, m_i * 8192, 9, 2 * m_i,
                           extra_w=[("gdma", wa_done)])
            lw = emit_wsum(WA, m_i * 8192, 17, 2 * m_i + 1,
                           extra_w=[("gdma", wa_done)])
        mm = 0
        for i in range(G):
            mm = to(lambda e, i=i: e.matmul(
                PX[0:1, 0:64], VECB[:, 0:1], WGT[:, i * 64:(i + 1) * 64],
                start=(i == 0), stop=(i == G - 1)),
                waits=[("gdma", wg_done), ("vector", lw), ("vector", vcb),
                       ("vector", P.wsum_v)] if i == 0 else (), wb="PX")
        lw = vo(lambda e: e.tensor_copy(out=AUGW[0:1, 8 * 1024:8 * 1024 + 64],
                                        in_=PX[0:1, 0:64]), waits=[("tensor", mm)],
                rb="PX")
        P.wsum_v = lw
        if _sub < 3:
            return _finish()
        invf = ao(lambda e: e.activation(out=VIF[:, 0:1], in_=VEC[:, 42:43],
                                         func=AF.Exp, scale=-LN1E4 / 32.0),
                  waits=[("sync", cons_ld)])

        def emit_trig_pair(tok_base):
            s2 = fb["s2"]
            wn = [s2[:, k * TC:(k + 1) * TC] for k in range(8)]
            pm = po(lambda e: e.iota(wn[0], [[1, TC]], base=tok_base,
                                     channel_multiplier=0,
                                     allow_small_or_imprecise_dtypes=True))
            angm = ao(lambda e: e.activation(out=wn[1], in_=wn[0],
                                             func=AF.Identity,
                                             scale=VIF[:, 0:1]),
                      waits=[("gpsimd", pm), ("scalar", invf)], inc=1)
            marks = []
            for quarter, base_w, dst in ((True, 2, 0), (False, 5, TC)):
                qadd = 0.25 if quarter else 0.0
                wk, wa, wb = wn[base_w], wn[base_w + 1], wn[base_w + 2]
                m = vo(lambda e, wk=wk, qadd=qadd: e.tensor_scalar(
                    out=wk, in0=wn[1], scalar1=INV_2PI, scalar2=qadd,
                    op0=ALU.mult, op1=ALU.add), waits=[("scalar", angm)])
                P.guard("vector")
                m = vo(lambda e, wk=wk: e.tensor_scalar(
                    out=wk, in0=wk, scalar1=MAGIC, scalar2=MAGIC,
                    op0=ALU.add, op1=ALU.subtract))
                P.guard("vector")
                m = vo(lambda e, wk=wk, wa=wa: e.scalar_tensor_tensor(
                    out=wa, in0=wk, scalar=-CW1, in1=wn[1],
                    op0=ALU.mult, op1=ALU.add))
                P.guard("vector")
                m = vo(lambda e, wk=wk, wa=wa, wb=wb: e.scalar_tensor_tensor(
                    out=wb, in0=wk, scalar=-CW2, in1=wa,
                    op0=ALU.mult, op1=ALU.add))
                P.guard("vector")
                m = vo(lambda e, wk=wk, wa=wa, wb=wb: e.scalar_tensor_tensor(
                    out=wa, in0=wk, scalar=-CW3, in1=wb,
                    op0=ALU.mult, op1=ALU.add))
                if quarter:
                    P.guard("vector")
                    m = vo(lambda e, wa=wa: e.tensor_scalar(
                        out=wa, in0=wa, scalar1=math.pi / 2.0, scalar2=None,
                        op0=ALU.add, op1=ALU.bypass))
                marks.append(ao(lambda e, wa=wa, dst=dst: e.activation(
                    out=CS[:, dst:dst + TC], in_=wa, func=AF.Sin),
                    waits=[("vector", m)]))
            return marks[-1]

        def ln_stats(xbuf, slot_m, slot_t, slot_r, rep_slot, first_waits, sqb=None,
                     bslot=5, mt_extra=()):
            sqb = sqb if sqb is not None else fb["s2"]
            mm = 0
            for i in range(G):
                mm = to(lambda e, i=i: e.matmul(
                    PX[0:1, 0:TC], VEC[:, 41:42], xbuf[:, i * TC:(i + 1) * TC],
                    start=(i == 0), stop=(i == G - 1)),
                    waits=first_waits if i == 0 else (), wb="PX")
            sq = 0
            for i in range(G):
                sq = ao(lambda e, i=i, sqb=sqb: e.activation(
                    out=sqb[:, i * TC:(i + 1) * TC],
                    in_=xbuf[:, i * TC:(i + 1) * TC], func=AF.Square),
                    waits=(list(first_waits) + [("tensor", mm)]) if i == 0 else ())
            mm2 = 0
            for i in range(G):
                mm2 = to(lambda e, i=i, sqb=sqb: e.matmul(
                    PX[0:1, 256:256 + TC], VEC[:, 41:42],
                    sqb[:, i * TC:(i + 1) * TC],
                    start=(i == 0), stop=(i == G - 1)),
                    waits=[("scalar", sq)] if i == 0 else (), wb="PX")
            rv = vo(lambda e: e.tensor_copy(out=row(slot_m), in_=PX[0:1, 0:TC]),
                    waits=[("tensor", mm2), ("tensor", mm)], rb="PX")
            P.guard("vector")
            rv = vo(lambda e: e.tensor_tensor(out=row(slot_r), in0=row(slot_m),
                                              in1=row(slot_m), op=ALU.mult))
            P.guard("vector")
            rv = vo(lambda e: e.scalar_tensor_tensor(
                out=row(slot_r), in0=row(slot_r), scalar=-1.0,
                in1=PX[0:1, 256:256 + TC], op0=ALU.mult, op1=ALU.add), rb="PX")
            P.guard("vector")
            rv = vo(lambda e: e.tensor_scalar(
                out=row(slot_r), in0=row(slot_r), scalar1=EPS_LN, scalar2=None,
                op0=ALU.add, op1=ALU.bypass))
            ra = ao(lambda e: e.activation(out=row(slot_r), in_=row(slot_r),
                                           func=AF.Ln), waits=[("vector", rv)])
            P.guard("scalar")
            ra = ao(lambda e: e.activation(out=row(slot_r), in_=row(slot_r),
                                           func=AF.Exp, scale=-0.5))
            mt = vo(lambda e: e.scalar_tensor_tensor(
                out=row(slot_t, off=1), in0=row(slot_m), scalar=-1.0,
                in1=row(slot_r), op0=ALU.mult, op1=ALU.mult),
                waits=[("scalar", ra)])
            P.guard("vector")
            mt = vo(lambda e: e.tensor_copy(out=rowb(bslot, off=1),
                                            in_=row(slot_t, off=1)),
                    waits=list(mt_extra))
            rp = to(lambda e: e.matmul(
                PX[:, 256:256 + TC], ONER[0:1, :], row(slot_r),
                start=True, stop=True),
                waits=[("scalar", ra), ("vector", rv), ("gpsimd", onr),
                       ("vector", P.rep_v)], wb="PX")
            re = vo(lambda e: e.tensor_copy(
                out=REP[:, rep_slot * TC:(rep_slot + 1) * TC],
                in_=PX[:, 256:256 + TC]), waits=[("tensor", rp)], rb="PX")
            P.rep_v = re
            return mt, re

        def replicate(src_row_ap, rep_slot, waits):
            rp = to(lambda e: e.matmul(PX[:, 256:256 + TC], ONER[0:1, :],
                                       src_row_ap, start=True, stop=True),
                    waits=list(waits) + [("gpsimd", onr), ("vector", P.rep_v)],
                    wb="PX")
            re = vo(lambda e: e.tensor_copy(
                out=REP[:, rep_slot * TC:(rep_slot + 1) * TC],
                in_=PX[:, 256:256 + TC]), waits=[("tensor", rp)], rb="PX")
            P.rep_v = re
            return re

        P.fence("sync")
        if stages < 1:
            return _finish()
        spk_stored = []
        P.rep_v = 0
        P.scand_prev = 0
        P.x0ev = 0
        P.last_proj_t = 0
        P.cm_prev = 0
        P.e_evac = 0
        P.o_evac = 0
        P.s_upd = 0
        P.sacc_v = 0
        P.x0cm_free = []
        P.x0ev_h = [0, 0]

        def emit_x0_load(c):
            hb = (c % 2) * NSUB * 1024
            lm = 0
            for s in range(NSUB):
                lm = ld(XIN[:, hb + s * 1024:hb + (s + 1) * 1024],
                        x0_h[c * TC + s * 128:c * TC + (s + 1) * 128, :],
                        waits=[("scalar", P.x0ev_h[c % 2])] if s == 0 else ())
            return lm

        P.lm_pref = {}
        for c in range(n_chunks):
            x0cm = fb["x0cm"]
            hb = (c % 2) * NSUB * 1024
            lm = P.lm_pref.pop(c, None)
            if lm is None:
                lm = emit_x0_load(c)
            if c + 1 < n_chunks:
                P.lm_pref[c + 1] = emit_x0_load(c + 1)
            ev = 0
            for s in range(NSUB):
                for i in range(G):
                    idx = s * G + i
                    w = list(P.x0cm_free) + [("sync", lm)] if idx == 0 else []
                    pb = (PTF, PTB)[idx % 2]
                    off = 128 * ((idx // 2) % 4)
                    tm = to(lambda e, s=s, i=i, pb=pb, off=off, hb=hb: e.transpose(
                        pb[:, off:off + 128],
                        XIN[:, hb + s * 1024 + i * 128:hb + s * 1024 + (i + 1) * 128],
                        IDENT[:, :]), waits=w, wb=("PTF", "PTB")[idx % 2])
                    ev = ao(lambda e, s=s, i=i, pb=pb, off=off, x0cm=x0cm: e.activation(
                        out=x0cm[:, i * TC + s * 128:i * TC + s * 128 + 128],
                        in_=pb[:, off:off + 128],
                        func=AF.Identity), waits=[("tensor", tm)],
                        rb=("PTF", "PTB")[idx % 2])
            P.x0ev_h[c % 2] = ev
            P.x0ev = ev
            x0w = []
            if c > 0:
                xbnd = po(lambda e: e.tensor_copy(
                    out=XRSF[:, 0:G * AJ:AJ],
                    in_=XRSF[:, TC:TC + (G - 1) * AJ + 1:AJ]),
                    waits=[("scalar", P.xam_prev)])
                mbnd = po(lambda e: e.tensor_copy(out=rowb(0, n=1, off=0),
                                                  in_=rowb(0, n=1, off=TC)),
                          waits=[("tensor", P.last_proj_t)])
            else:
                xbnd = P.mark("gpsimd")
                mbnd = xbnd

            mt0, rs0rep = ln_stats(x0cm, 0, 1, 2, 0,
                                   [("scalar", ev)], sqb=SQB,
                                   bslot=0,
                                   mt_extra=[("gpsimd", mbnd),
                                             ("tensor", P.last_proj_t)])
            xrs = XRSF
            xm = 0
            for i in range(G):
                xm = vo(lambda e, i=i, xrs=xrs: e.tensor_tensor(
                    out=xrs[:, i * AJ + 1:i * AJ + 1 + TC],
                    in0=x0cm[:, i * TC:(i + 1) * TC], in1=REP[:, 0:TC],
                    op=ALU.mult),
                    waits=[("vector", rs0rep), ("gpsimd", xbnd),
                           ] if i == 0 else (), bare=(0 < i < G - 1))
            bm = xbnd
            mtb = mbnd
            xam = 0
            for i in range(G):
                xam = ao(lambda e, i=i, xrs=xrs: e.activation(
                    out=bb["lnb"][:, i * TC:(i + 1) * TC],
                    in_=xrs[:, i * AJ + 1:i * AJ + 1 + TC],
                    func=AF.Identity, scale=VEC[:, 0 + i:1 + i]),
                    waits=[("vector", xm), ("gpsimd", bm)] if i == 0 else (),
                    bare=(i > 0))
            for i in range(G):
                xam = ao(lambda e, i=i, xrs=xrs: e.activation(
                    out=bb["q3"][:, i * TC:(i + 1) * TC],
                    in_=xrs[:, i * AJ:i * AJ + TC],
                    func=AF.Identity, scale=VEC[:, 8 + i:9 + i]),
                    bare=(i < G - 1))
            xa2 = 0
            for i in range(G):
                xa2 = po(lambda e, i=i: e.tensor_tensor(
                    out=bb["xa"][:, i * TC:(i + 1) * TC],
                    in0=bb["lnb"][:, i * TC:(i + 1) * TC],
                    in1=bb["q3"][:, i * TC:(i + 1) * TC], op=ALU.add),
                    waits=[("scalar", xam)] if i == 0 else (),
                    bare=(0 < i < G - 1))
            P.xam_prev = xam
            P.x0cm_free = [("vector", xm)]

            projw = [("gpsimd", xa2), ("gpsimd", mtb), ("vector", lw),
                     ("gdma", wa_done), ("vector", mt0)]
            pev = 0
            for m_i, (outb, fn_a) in enumerate((("r", AF.Sigmoid), ("ek", AF.Exp),
                                                ("v", AF.Identity), ("td", AF.Exp))):
                pp = (PP0, PP1, PP2, PP3)[m_i]
                pn = ("PP0", "PP1", "PP2", "PP3")[m_i]
                for j2 in range(RND):
                    mmk = 0
                    for jj in range(SL):
                        j = SL * j2 + jj
                        for i in range(G):
                            w = projw if (m_i == 0 and j2 == 0 and jj == 0 and i == 0) else ()
                            if i == 0 and jj == 0 and j2 > 0:
                                w = [("scalar", pev)]
                            mmk = to(lambda e, m_i=m_i, i=i, j=j, jj=jj, pp=pp: e.matmul(
                                pp[:, jj * TC:jj * TC + TC],
                                WA[:, (m_i * 8 + i) * 1024 + j * 128:(m_i * 8 + i) * 1024 + (j + 1) * 128],
                                bb["xa"][:, i * TC:(i + 1) * TC],
                                start=(i == 0), stop=False), waits=w, wb=pn)
                        mmk = to(lambda e, m_i=m_i, j=j, jj=jj, pp=pp: e.matmul(
                            pp[:, jj * TC:jj * TC + TC],
                            AUGW[0:1, (2 * m_i) * 1024 + j * 128:(2 * m_i) * 1024 + (j + 1) * 128],
                            rowb(0, off=1), start=False, stop=False))
                        mmk = to(lambda e, m_i=m_i, j=j, jj=jj, pp=pp: e.matmul(
                            pp[:, jj * TC:jj * TC + TC],
                            AUGW[0:1, (2 * m_i + 1) * 1024 + j * 128:(2 * m_i + 1) * 1024 + (j + 1) * 128],
                            rowb(0, off=0), start=False, stop=True))
                    for jj in range(SL):
                        j = SL * j2 + jj
                        w = [("tensor", mmk)]
                        if m_i == 0 and j2 == 0 and jj == 0:
                            w.append(("gpsimd", P.cm_prev))
                        pev = ao(lambda e, outb=outb, fn_a=fn_a, j=j, jj=jj, pp=pp: e.activation(
                            out=fb[outb][:, j * TC:(j + 1) * TC],
                            in_=pp[:, jj * TC:jj * TC + TC], func=fn_a),
                            waits=w, rb=pn)
            pdone = pev
            P.last_proj_t = P.mark("tensor")

            ekm = 0
            for i in range(G):
                ekm = vo(lambda e, i=i: e.tensor_scalar(
                    out=fb["ek"][:, i * TC:(i + 1) * TC],
                    in0=fb["ek"][:, i * TC:(i + 1) * TC],
                    scalar1=E10, scalar2=None, op0=ALU.min, op1=ALU.bypass),
                    waits=[("scalar", pdone)] if i == 0 else (),
                    bare=(0 < i < G - 1))
            ekv = 0
            for i in range(G):
                ekv = po(lambda e, i=i: e.tensor_tensor(
                    out=fb["s2"][:, i * TC:(i + 1) * TC],
                    in0=fb["ek"][:, i * TC:(i + 1) * TC],
                    in1=fb["v"][:, i * TC:(i + 1) * TC], op=ALU.mult),
                    waits=[("vector", ekm), ("scalar", pdone),
                           ("vector", P.scand_prev)] if i == 0 else (),
                    bare=(0 < i < G - 1))
            a_cur = fb["aA"]
            b_cur = fb["bA"]
            scm = 0
            for i in range(G):
                scm = vo(lambda e, i=i: e.tensor_tensor_scan(
                    out=fb["aA"][:, i * TC:(i + 1) * TC],
                    data0=fb["td"][:, i * TC:(i + 1) * TC],
                    data1=fb["s2"][:, i * TC:(i + 1) * TC],
                    initial=CARY[:, i:i + 1], op0=ALU.mult, op1=ALU.add),
                    waits=[("gpsimd", ekv)] if i == 0 else (), bare=(i > 0))
            for i in range(G):
                scm = vo(lambda e, i=i: e.tensor_tensor_scan(
                    out=fb["bA"][:, i * TC:(i + 1) * TC],
                    data0=fb["td"][:, i * TC:(i + 1) * TC],
                    data1=fb["ek"][:, i * TC:(i + 1) * TC],
                    initial=CARY[:, 8 + i:9 + i], op0=ALU.mult, op1=ALU.add),
                    bare=(i < G - 1))
            acar = po(lambda e: e.tensor_copy(
                out=CARY[:, 0:8], in_=fb["aA"][:, TC - 1:TC - 1 + (G - 1) * TC + 1:TC]),
                waits=[("vector", scm)])
            acar = po(lambda e: e.tensor_copy(
                out=CARY[:, 8:16], in_=fb["bA"][:, TC - 1:TC - 1 + (G - 1) * TC + 1:TC]))
            g5 = 0
            for i in range(G):
                g5 = ao(lambda e, i=i: e.activation(
                    out=fb["x1"][:, i * TC:(i + 1) * TC],
                    in_=fb["ek"][:, i * TC:(i + 1) * TC],
                    func=AF.Identity, scale=VGD[:, 8 + i:9 + i]),
                    waits=[("vector", ekm), ("vector", gd5),
                           ("gpsimd", P.cm_prev)] if i == 0 else (),
                    bare=(0 < i < G - 1))
            dn = 0
            for i in range(G):
                dn = vo(lambda e, i=i, b_cur=b_cur: e.scalar_tensor_tensor(
                    out=fb["x2"][:, i * TC:(i + 1) * TC],
                    in0=b_cur[:, i * TC:(i + 1) * TC], scalar=THR,
                    in1=fb["x1"][:, i * TC:(i + 1) * TC],
                    op0=ALU.mult, op1=ALU.add),
                    waits=[("scalar", g5), ("vector", scm)] if i == 0 else (),
                    bare=(i > 0))
            for i in range(G):
                dn = vo(lambda e, i=i: e.tensor_scalar(
                    out=fb["x2"][:, i * TC:(i + 1) * TC],
                    in0=fb["x2"][:, i * TC:(i + 1) * TC],
                    scalar1=THR * EPS_A, scalar2=None, op0=ALU.add, op1=ALU.bypass),
                    bare=(i < G - 1))
            nm = 0
            for i in range(G):
                nm = vo(lambda e, i=i: e.tensor_scalar(
                    out=fb["x1"][:, i * TC:(i + 1) * TC],
                    in0=fb["x1"][:, i * TC:(i + 1) * TC],
                    scalar1=5.0, scalar2=-EPS_A, op0=ALU.mult, op1=ALU.add),
                    bare=(i < G - 1))
            n2 = 0
            for i in range(G):
                n2 = po(lambda e, i=i: e.tensor_tensor(
                    out=fb["x1"][:, i * TC:(i + 1) * TC],
                    in0=fb["x1"][:, i * TC:(i + 1) * TC],
                    in1=fb["v"][:, i * TC:(i + 1) * TC], op=ALU.mult),
                    waits=[("vector", nm)] if i == 0 else (),
                    bare=(0 < i < G - 1))
            n3 = 0
            for i in range(G):
                n3 = po(lambda e, i=i, a_cur=a_cur: e.tensor_tensor(
                    out=fb["x1"][:, i * TC:(i + 1) * TC],
                    in0=fb["x1"][:, i * TC:(i + 1) * TC],
                    in1=a_cur[:, i * TC:(i + 1) * TC], op=ALU.add),
                    bare=(i < G - 1))
            rc = 0
            for i in range(G):
                rc = vo(lambda e, i=i: e.reciprocal(
                    out=fb["s1"][:, i * TC:(i + 1) * TC],
                    in_=fb["x2"][:, i * TC:(i + 1) * TC]),
                    waits=[("gpsimd", n3)] if i == 0 else (),
                    bare=(0 < i < G - 1))
            wkm = 0
            for i in range(G):
                wkm = po(lambda e, i=i: e.tensor_tensor(
                    out=fb["x1"][:, i * TC:(i + 1) * TC],
                    in0=fb["x1"][:, i * TC:(i + 1) * TC],
                    in1=fb["s1"][:, i * TC:(i + 1) * TC], op=ALU.mult),
                    waits=[("vector", rc)] if i == 0 else (),
                    bare=(0 < i < G - 1))
            cm = 0
            for i in range(G):
                cm = po(lambda e, i=i: e.tensor_tensor(
                    out=fb["s2"][:, i * TC:(i + 1) * TC],
                    in0=fb["x1"][:, i * TC:(i + 1) * TC],
                    in1=fb["r"][:, i * TC:(i + 1) * TC], op=ALU.mult),
                    bare=(i < G - 1))
            P.cm_prev = cm

            mh_c = fb["mhA"]

            def sl(buf, t, half):
                return buf[:, half * 4 * TC + t:half * 4 * TC + t + 3 * TC + 1:TC]

            for t in range(TC):
                if t == 0:
                    pa = CARY[:, 16:20]
                    pb = CARY[:, 20:24]
                else:
                    pa = sl(mh_c, t - 1, 0)
                    pb = sl(mh_c, t - 1, 1)
                vo(lambda e, t=t, pa=pa: e.scalar_tensor_tensor(
                    out=sl(fb["s1"], t, 0), in0=pa, scalar=1.0,
                    in1=sl(fb["s2"], t, 0), op0=ALU.is_gt, op1=ALU.subtract),
                    waits=[("gpsimd", cm)] if t == 0 else (), bare=True)
                vo(lambda e, t=t, pb=pb: e.scalar_tensor_tensor(
                    out=sl(fb["s1"], t, 1), in0=pb, scalar=1.0,
                    in1=sl(fb["s2"], t, 1), op0=ALU.is_gt, op1=ALU.subtract),
                    bare=True)
                vo(lambda e, t=t, pa=pa, mh_c=mh_c: e.scalar_tensor_tensor(
                    out=sl(mh_c, t, 0), in0=pa, scalar=BETA,
                    in1=sl(fb["s1"], t, 0), op0=ALU.mult, op1=ALU.subtract),
                    bare=True)
                vo(lambda e, t=t, pb=pb, mh_c=mh_c: e.scalar_tensor_tensor(
                    out=sl(mh_c, t, 1), in0=pb, scalar=BETA,
                    in1=sl(fb["s1"], t, 1), op0=ALU.mult, op1=ALU.subtract),
                    bare=(t < TC - 1))
            scand = P.mark("vector")
            P.scand_prev = scand
            mcar = po(lambda e, mh_c=mh_c: e.tensor_copy(
                out=CARY[:, 16:24],
                in_=mh_c[:, TC - 1:TC - 1 + 7 * TC + 1:TC]),
                waits=[("vector", scand)])
            P.guard("vector")
            spm = vo(lambda e, mh_c=mh_c: e.tensor_single_scalar(
                out=bb["spk"][:, :], in_=mh_c[:, :], scalar=1.0, op=ALU.is_gt),
                waits=[("gdma", spk_stored[-1])] if spk_stored else ())
            sst = P.dma("gpsimd", spk_h[c, :, :], bb["spk"][:, :],
                        waits=[("vector", spm)], no_pend=True)
            spk_stored.append(sst)
            if debug and c == 0:
                dstore(dbg["d_r"], fb["r"][:, :], waits=[("gpsimd", cm)], flush=False)
                dstore(dbg["d_ek"], fb["ek"][:, :], waits=[("gpsimd", cm)], flush=False)
                dstore(dbg["d_v"], fb["v"][:, :], waits=[("gpsimd", cm)], flush=False)
                dstore(dbg["d_td"], fb["td"][:, :], waits=[("gpsimd", cm)], flush=False)
                dstore(dbg["d_a"], a_cur[:, :], waits=[("gpsimd", cm)], flush=False)
                dstore(dbg["d_b"], b_cur[:, :], waits=[("gpsimd", cm)], flush=False)
                dstore(dbg["d_c5"], fb["s2"][:, :], waits=[("vector", scand)], flush=False)
                dstore(dbg["d_mem"], mh_c[:, :], waits=[("vector", scand)], flush=False)
                dstore(dbg["d_spk"], bb["spk"][:, :], waits=[("vector", spm)])
            if debug and c == 0:
                for eng in ("vector", "scalar", "gpsimd", "tensor"):
                    P.q[eng].append(((lambda e: None), (("store", P.store),), 0, eng))

        ph1_v = P.mark("vector")
        ph1_p = P.mark("gpsimd")
        if stages < 2:
            return _finish()
        P.wsum_v = P.mark("vector")
        P.wsum_t = P.mark("tensor")
        wb_done = 0
        for mi, nm in enumerate(("wq", "wk", "wv", "wo")):
            for i in range(G):
                wb_done = gld(WB[:, (mi * 8 + i) * 1024:(mi * 8 + i + 1) * 1024],
                              w_h[nm][i * 128:(i + 1) * 128, :],
                              waits=[("tensor", P.last_proj_t)] if (mi == 0 and i == 0) else ())
        lw = emit_wsum(WB, 0 * 8192, None, 0, extra_w=[("gdma", wb_done), ("vector", ph1_v)])
        lw = emit_wsum(WB, 1 * 8192, None, 1, extra_w=[("gdma", wb_done)])
        lw = emit_wsum(WB, 2 * 8192, None, 2, extra_w=[("gdma", wb_done)])
        bwe = 0
        for half in range(2):
            bw = 0
            for i in range(G):
                bw = to(lambda e, i=i, half=half: e.matmul(
                    PX[0:1, 0:512], VECB[:, 1 + i:2 + i],
                    WB[:, (3 * 8 + i) * 1024 + half * 512:(3 * 8 + i) * 1024 + half * 512 + 512],
                    start=(i == 0), stop=(i == G - 1)),
                    waits=[("gdma", wb_done), ("vector", lw),
                           ("vector", P.wsum_v)] if (i == 0) else (), wb="PX")
            bwe = vo(lambda e, half=half: e.tensor_copy(
                out=AUGW[0:1, 4 * 1024 + half * 512:4 * 1024 + half * 512 + 512],
                in_=PX[0:1, 0:512]), waits=[("tensor", bw)], rb="PX")
            P.wsum_v = bwe
        gf = 0
        for i in range(G):
            gf = ao(lambda e, i=i: e.activation(
                out=WB[:, (3 * 8 + i) * 1024:(3 * 8 + i + 1) * 1024],
                in_=WB[:, (3 * 8 + i) * 1024:(3 * 8 + i + 1) * 1024],
                func=AF.Identity, scale=VEC[:, 24 + i:25 + i]),
                waits=[("vector", bwe)])
        lw2 = emit_wsum(WB, 3 * 8192, None, 3, extra_w=[("scalar", gf)])
        wbprep_v = P.mark("vector")
        if stages < 3:
            return _finish()

        P.ffn_fo_prev = 0

        def emit_ffn(xin, xout, first_waits, dbg_key=None, c=0):
            mt, rsre = ln_stats(xin, 4, 5, 6, 1, first_waits, bslot=1)
            mre = replicate(row(5, off=1), 3, [("vector", mt),
                                               ("gpsimd", P.ffn_fo_prev)])
            um = 0
            for i in range(G):
                um = vo(lambda e, i=i, xin=xin: e.tensor_tensor(
                    out=fb["s2"][:, i * TC:(i + 1) * TC],
                    in0=xin[:, i * TC:(i + 1) * TC], in1=REP[:, TC:2 * TC],
                    op=ALU.mult), waits=[("vector", rsre)] if i == 0 else ())
            u2 = 0
            for i in range(G):
                u2 = po(lambda e, i=i: e.tensor_tensor(
                    out=fb["s2"][:, i * TC:(i + 1) * TC],
                    in0=fb["s2"][:, i * TC:(i + 1) * TC],
                    in1=REP[:, 3 * TC:4 * TC], op=ALU.add),
                    waits=[("vector", um), ("vector", mre)] if i == 0 else ())
            ab_ = 0
            for i in range(G):
                ab_ = ao(lambda e, i=i: e.activation(
                    out=fb["s1"][:, i * TC:(i + 1) * TC],
                    in_=fb["s2"][:, i * TC:(i + 1) * TC], func=AF.Abs),
                    waits=[("gpsimd", u2)] if i == 0 else ())
            tm = 0
            for step in (4, 2, 1):
                for i in range(step):
                    tm = vo(lambda e, i=i, step=step: e.tensor_tensor(
                        out=fb["s1"][:, i * TC:(i + 1) * TC],
                        in0=fb["s1"][:, i * TC:(i + 1) * TC],
                        in1=fb["s1"][:, (i + step) * TC:(i + step + 1) * TC],
                        op=ALU.max),
                        waits=[("scalar", ab_)] if (step == 4 and i == 0) else ())
                if step > 1:
                    P.guard("vector")
            tt1 = 0
            for s in range(NSUB):
                tt1 = to(lambda e, s=s: e.transpose(
                    PTF[:, s * 128:(s + 1) * 128],
                    fb["s1"][:, s * 128:(s + 1) * 128], IDENT[:, :]),
                    waits=[("vector", tm)] if s == 0 else (), wb="PTF")
            SC = 8 * TC - 8
            sc = 0
            for s in range(NSUB):
                sc = vo(lambda e, s=s: e.tensor_reduce(
                    out=fb["s1"][:, SC + s:SC + s + 1],
                    in_=PTF[:, s * 128:(s + 1) * 128], axis=AXX, op=ALU.max),
                    waits=[("tensor", tt1)] if s == 0 else (), rb="PTF")
            P.guard("vector")
            sc = vo(lambda e: e.tensor_scalar(
                out=fb["s1"][:, SC + 2:SC + 4], in0=fb["s1"][:, SC:SC + 2],
                scalar1=1e-12, scalar2=1.0 / 127.0, op0=ALU.max, op1=ALU.mult))
            P.guard("vector")
            sc = vo(lambda e: e.reciprocal(
                out=fb["s1"][:, SC + 4:SC + 6], in_=fb["s1"][:, SC + 2:SC + 4]))
            P.guard("vector")
            tb = to(lambda e: e.transpose(PTF[0:4, 256:256 + 128],
                                          fb["s1"][:, SC + 2:SC + 6], IDENT[:, :]),
                    waits=[("vector", sc)], wb="PTF")
            r4 = vo(lambda e: e.tensor_copy(out=fb["s1"][0:4, 0:128],
                                            in_=PTF[0:4, 256:256 + 128]),
                    waits=[("tensor", tb)], rb="PTF")
            lsm = to(lambda e: e.matmul(PX[0:1, 0:128], IDENT[0:4, 0:1],
                                        fb["s1"][0:4, 0:128], start=True, stop=True),
                     waits=[("vector", r4)], wb="PX")
            lsm = to(lambda e: e.matmul(PX[0:1, 128:256], IDENT[0:4, 1:2],
                                        fb["s1"][0:4, 0:128], start=True, stop=True))
            lsm = to(lambda e: e.matmul(PX[0:1, 256:384], IDENT[0:4, 2:3],
                                        fb["s1"][0:4, 0:128], start=True, stop=True))
            lsm = to(lambda e: e.matmul(PX[0:1, 384:512], IDENT[0:4, 3:4],
                                        fb["s1"][0:4, 0:128], start=True, stop=True))
            lse = vo(lambda e: e.tensor_copy(out=row(7, n=TC),
                                             in_=PX[0:1, 0:TC]),
                     waits=[("tensor", lsm)], rb="PX")
            ive = vo(lambda e: e.tensor_copy(out=row(8, n=TC),
                                             in_=PX[0:1, 256:256 + TC]), rb="PX")
            ivr = replicate(row(8, n=TC), 3, [("vector", ive)])
            lsr = replicate(row(7, n=TC), 2, [("vector", lse)])
            q1 = 0
            for i in range(G):
                q1 = vo(lambda e, i=i: e.tensor_tensor(
                    out=fb["s2"][:, i * TC:(i + 1) * TC],
                    in0=fb["s2"][:, i * TC:(i + 1) * TC],
                    in1=REP[:, 3 * TC:4 * TC], op=ALU.mult),
                    waits=[("vector", ivr)] if i == 0 else ())
            for i in range(G):
                q1 = vo(lambda e, i=i: e.tensor_scalar(
                    out=fb["s2"][:, i * TC:(i + 1) * TC],
                    in0=fb["s2"][:, i * TC:(i + 1) * TC],
                    scalar1=MAGIC, scalar2=MAGIC, op0=ALU.add, op1=ALU.subtract))
            for i in range(G):
                q1 = vo(lambda e, i=i: e.tensor_scalar(
                    out=bb["q3"][:, i * TC:(i + 1) * TC],
                    in0=fb["s2"][:, i * TC:(i + 1) * TC],
                    scalar1=127.0, scalar2=-127.0, op0=ALU.min, op1=ALU.max))
            fev = 0
            for j2 in range(RND):
                mmk = 0
                for jj in range(SL):
                    j = SL * j2 + jj
                    for i in range(G):
                        w = ()
                        if j2 == 0 and jj == 0 and i == 0:
                            w = [("vector", q1)]
                        elif i == 0 and jj == 0 and j2 > 0:
                            w = [("scalar", fev)]
                        mmk = to(lambda e, i=i, j=j, jj=jj: e.matmul(
                            PP0[:, jj * TC:jj * TC + TC],
                            QR[:, (i * 8 + j) * 128:(i * 8 + j + 1) * 128],
                            bb["q3"][:, i * TC:(i + 1) * TC],
                            start=(i == 0), stop=(i == G - 1)), waits=w, wb="PP0")
                for jj in range(SL):
                    j = SL * j2 + jj
                    fev = ao(lambda e, j=j, jj=jj: e.activation(
                        out=fb["s2"][:, j * TC:(j + 1) * TC],
                        in_=PP0[:, jj * TC:jj * TC + TC],
                        func=AF.Identity, scale=VRQ[:, 8 + j:9 + j]),
                        waits=[("tensor", mmk)], rb="PP0")
            fo = 0
            for i in range(G):
                fo = po(lambda e, i=i: e.tensor_tensor(
                    out=fb["s2"][:, i * TC:(i + 1) * TC],
                    in0=fb["s2"][:, i * TC:(i + 1) * TC],
                    in1=REP[:, 2 * TC:3 * TC], op=ALU.mult),
                    waits=[("scalar", fev), ("vector", lsr)] if i == 0 else (),
                    bare=(0 < i < G - 1))
            P.ffn_fo_prev = fo
            if debug and dbg_key and c == 0:
                dstore(dbg[dbg_key], fb["s2"][:, :], waits=[("gpsimd", fo)])
            fo2 = 0
            for i in range(G):
                fo2 = po(lambda e, i=i, xin=xin, xout=xout: e.tensor_tensor(
                    out=xout[:, i * TC:(i + 1) * TC],
                    in0=fb["s2"][:, i * TC:(i + 1) * TC],
                    in1=xin[:, i * TC:(i + 1) * TC], op=ALU.add))
            return fo2

        P.ph2_x1m = 0
        P.ph2_ost = 0
        P.lm_pref = {}
        P.spl_pref = {}
        for c in range(n_chunks):
            x0cm = fb["x0cm"]
            hb = (c % 2) * NSUB * 1024
            lm = P.lm_pref.pop(c, None)
            if lm is None:
                lm = emit_x0_load(c)
            spl = P.spl_pref.pop(c, None)
            if spl is None:
                spl = ld(bb["spk"][:, :], spk_h[c, :, :],
                         waits=[("gdma", spk_stored[-1])] +
                               ([("gpsimd", P.ph2_x1m)] if c > 0 else []))
            ev = 0
            for s in range(NSUB):
                for i in range(G):
                    idx = s * G + i
                    w = []
                    if idx == 0:
                        w = [("sync", spl)]
                        if c == 0:
                            w += [("vector", ph1_v), ("gpsimd", ph1_p)]
                        else:
                            w += [("gpsimd", P.ph2_x1m)]
                    pb = (PTF, PTB)[idx % 2]
                    off = 128 * ((idx // 2) % 4)
                    tm = to(lambda e, s=s, i=i, pb=pb, off=off, hb=hb: e.transpose(
                        pb[:, off:off + 128],
                        XIN[:, hb + s * 1024 + i * 128:hb + s * 1024 + (i + 1) * 128],
                        IDENT[:, :]), waits=w, wb=("PTF", "PTB")[idx % 2])
                    ev = ao(lambda e, s=s, i=i, pb=pb, off=off: e.activation(
                        out=x0cm[:, i * TC + s * 128:i * TC + s * 128 + 128],
                        in_=pb[:, off:off + 128],
                        func=AF.Identity), waits=[("tensor", tm)],
                        rb=("PTF", "PTB")[idx % 2])
            P.x0ev_h[c % 2] = ev
            x1m = 0
            for i in range(G):
                x1m = po(lambda e, i=i: e.tensor_tensor(
                    out=fb["x1"][:, i * TC:(i + 1) * TC],
                    in0=x0cm[:, i * TC:(i + 1) * TC],
                    in1=bb["spk"][:, i * TC:(i + 1) * TC], op=ALU.add),
                    waits=([("scalar", ev), ("sync", spl)] +
                           ([("tensor", P.ph2_ffn2t), ("vector", P.ph2_ffn2v)]
                            if c > 0 else [])) if i == 0 else ())
            P.ph2_x1m = x1m
            if c + 1 < n_chunks:
                P.lm_pref[c + 1] = emit_x0_load(c + 1)
                P.spl_pref[c + 1] = ld(bb["spk"][:, :], spk_h[c + 1, :, :],
                                       waits=[("gpsimd", x1m)])
            if debug and c == 0:
                dstore(dbg["d_x1"], fb["x1"][:, :], waits=[("gpsimd", x1m)])

            P.trigc = emit_trig_pair(c * TC)

            x2m = emit_ffn(fb["x1"], fb["x2"],
                           [("gpsimd", x1m), ("vector", wbprep_v)], "d_f1", c)
            if debug and c == 0:
                dstore(dbg["d_x2"], fb["x2"][:, :], waits=[("gpsimd", x2m)])

            mt2, ln2re = ln_stats(fb["x2"], 9, 10, 11, 1, [("gpsimd", x2m)], bslot=2)
            l2c = 0
            for i in range(G):
                l2c = vo(lambda e, i=i: e.tensor_tensor(
                    out=bb["lnb"][:, i * TC:(i + 1) * TC],
                    in0=fb["x2"][:, i * TC:(i + 1) * TC], in1=REP[:, TC:2 * TC],
                    op=ALU.mult), waits=[("vector", ln2re)] if i == 0 else ())
            if debug and c == 0:
                dstore(dbg["d_ln2"], bb["lnb"][:, :], waits=[("vector", l2c)])

            def proj_b(w_base, fn_a, aug_slot, dstb, pp, pn):
                evp = 0
                for j2 in range(RND):
                    mmk = 0
                    for jj in range(SL):
                        j = SL * j2 + jj
                        for i in range(G):
                            w = ()
                            if j2 == 0 and jj == 0 and i == 0:
                                w = [("vector", l2c), ("vector", mt2),
                                     ("vector", wbprep_v)]
                            elif i == 0 and jj == 0 and j2 > 0:
                                w = [("scalar", evp)]
                            mmk = to(lambda e, i=i, j=j, jj=jj, w_base=w_base, pp=pp: e.matmul(
                                pp[:, jj * TC:jj * TC + TC],
                                WB[:, (w_base + i) * 1024 + j * 128:(w_base + i) * 1024 + (j + 1) * 128],
                                bb["lnb"][:, i * TC:(i + 1) * TC],
                                start=(i == 0), stop=False), waits=w, wb=pn)
                        mmk = to(lambda e, j=j, jj=jj, aug_slot=aug_slot, pp=pp: e.matmul(
                            pp[:, jj * TC:jj * TC + TC],
                            AUGW[0:1, aug_slot * 1024 + j * 128:aug_slot * 1024 + (j + 1) * 128],
                            rowb(2, off=1), start=False, stop=True))
                    for jj in range(SL):
                        j = SL * j2 + jj
                        evp = ao(lambda e, j=j, jj=jj, fn_a=fn_a, dstb=dstb, pp=pp: e.activation(
                            out=dstb[:, j * TC:(j + 1) * TC],
                            in_=pp[:, jj * TC:jj * TC + TC], func=fn_a),
                            waits=[("tensor", mmk)], rb=pn)
                return evp

            qev = proj_b(0, AF.Silu, 0, bb["qr"], PP1, "PP1")
            kev = proj_b(8, AF.Silu, 1, bb["kr"], PP2, "PP2")
            vev = proj_b(16, AF.Identity, 2, bb["xa"], PP3, "PP3")

            g1m = 0
            for i in range(G):
                g1m = to(lambda e, i=i: e.matmul(
                    PX[0:64, 256:256 + TC], WGT[:, i * 64:(i + 1) * 64],
                    bb["lnb"][:, i * TC:(i + 1) * TC],
                    start=(i == 0), stop=False),
                    waits=[("vector", l2c), ("vector", mt2)] if i == 0 else (),
                    wb="PX")
            g1m = to(lambda e: e.matmul(
                PX[0:64, 256:256 + TC], AUGW[0:1, 8 * 1024:8 * 1024 + 64],
                rowb(2, off=1), start=False, stop=True))
            g1e = ao(lambda e: e.activation(out=G1B[:, :], in_=PX[0:64, 256:256 + TC],
                                            func=AF.Identity), waits=[("tensor", g1m)],
                     rb="PX")
            g2m = 0
            for j2 in range(RND):
                for jj in range(SL):
                    j = SL * j2 + jj
                    g2m = to(lambda e, j=j, jj=jj: e.matmul(
                        PP0[:, jj * TC:jj * TC + TC],
                        WGT[0:64, 512 + j * 128:512 + (j + 1) * 128], G1B[:, :],
                        start=True, stop=True),
                        waits=[("scalar", g1e)] if j == 0 else (), wb="PP0")
                for jj in range(SL):
                    j = SL * j2 + jj
                    gev = ao(lambda e, j=j, jj=jj: e.activation(
                        out=fb["gate"][:, j * TC:(j + 1) * TC],
                        in_=PP0[:, jj * TC:jj * TC + TC], func=AF.Sigmoid),
                        waits=[("tensor", g2m)], rb="PP0")
                if j2 < RND - 1:
                    P.q["tensor"].append(((lambda e: None), (("scalar", gev),), 0, "tensor"))
            gated_w = gev
            if debug and c == 0:
                dstore(dbg["d_gate"], fb["gate"][:, :], waits=[("scalar", gated_w)])

            def rope(buf, done):
                rm = 0
                cosr = CS[:, 0:TC]
                sinr = CS[:, TC:2 * TC]
                for g in range(4):
                    rm = vo(lambda e, g=g, buf=buf: e.tensor_tensor(
                        out=bb["esb"][:, (g % 2) * 4 * TC + 0:(g % 2) * 4 * TC + TC],
                        in0=buf[:, g * TC:(g + 1) * TC], in1=cosr, op=ALU.mult),
                        waits=[("scalar", done), ("scalar", P.trigc)] if g == 0 else (),
                        bare=(g > 0))
                    rm = vo(lambda e, g=g, buf=buf: e.tensor_tensor(
                        out=bb["esb"][:, (g % 2) * 4 * TC + TC:(g % 2) * 4 * TC + 2 * TC],
                        in0=buf[:, (4 + g) * TC:(5 + g) * TC], in1=sinr, op=ALU.mult),
                        bare=True)
                    rm = vo(lambda e, g=g, buf=buf: e.tensor_tensor(
                        out=bb["esb"][:, (g % 2) * 4 * TC + 2 * TC:(g % 2) * 4 * TC + 3 * TC],
                        in0=buf[:, g * TC:(g + 1) * TC], in1=sinr, op=ALU.mult),
                        bare=True)
                    rm = vo(lambda e, g=g, buf=buf: e.tensor_tensor(
                        out=bb["esb"][:, (g % 2) * 4 * TC + 3 * TC:(g % 2) * 4 * TC + 4 * TC],
                        in0=buf[:, (4 + g) * TC:(5 + g) * TC], in1=cosr, op=ALU.mult),
                        bare=True)
                    rm = vo(lambda e, g=g, buf=buf: e.tensor_tensor(
                        out=buf[:, g * TC:(g + 1) * TC],
                        in0=bb["esb"][:, (g % 2) * 4 * TC + 0:(g % 2) * 4 * TC + TC],
                        in1=bb["esb"][:, (g % 2) * 4 * TC + TC:(g % 2) * 4 * TC + 2 * TC],
                        op=ALU.subtract), bare=True)
                    rm = vo(lambda e, g=g, buf=buf: e.tensor_tensor(
                        out=buf[:, (4 + g) * TC:(5 + g) * TC],
                        in0=bb["esb"][:, (g % 2) * 4 * TC + 2 * TC:(g % 2) * 4 * TC + 3 * TC],
                        in1=bb["esb"][:, (g % 2) * 4 * TC + 3 * TC:(g % 2) * 4 * TC + 4 * TC],
                        op=ALU.add), bare=(g < 3))
                return rm
            qrm = rope(bb["qr"], qev)
            krm = rope(bb["kr"], kev)
            if debug and c == 0:
                dstore(dbg["d_qr"], bb["qr"][:, :], waits=[("vector", qrm)])
                dstore(dbg["d_kr"], bb["kr"][:, :], waits=[("vector", krm)])

            for s in range(NSUB):
                tev = 0
                for i in range(G):
                    w = [("vector", krm)] if i == 0 else []
                    if i >= 2:
                        w.append(("scalar", tev - 1))
                    tm = to(lambda e, i=i, s=s: e.matmul(
                        PTB[:, (i % 2) * 128:(i % 2) * 128 + 128],
                        bb["kr"][:, i * TC + s * 128:i * TC + s * 128 + 128],
                        IDENTB[:, :], start=True, stop=True), waits=w, wb="PTB")
                    tev = ao(lambda e, i=i, s=s: e.activation(
                        out=bb["ktok"][:, s * 1024 + i * 128:s * 1024 + (i + 1) * 128],
                        in_=PTB[:, (i % 2) * 128:(i % 2) * 128 + 128],
                        func=AF.Identity), waits=[("tensor", tm)], rb="PTB")
                for i in range(G):
                    w = [("scalar", vev)] if i == 0 else []
                    if i >= 2:
                        w.append(("scalar", tev - 1))
                    tm = to(lambda e, i=i, s=s: e.matmul(
                        PTB[:, 256 + (i % 2) * 128:256 + (i % 2) * 128 + 128],
                        bb["xa"][:, i * TC + s * 128:i * TC + s * 128 + 128],
                        IDENTB[:, :], start=True, stop=True), waits=w, wb="PTB")
                    tev = ao(lambda e, i=i, s=s: e.activation(
                        out=bb["vtok"][:, s * 1024 + i * 128:s * 1024 + (i + 1) * 128],
                        in_=PTB[:, 256 + (i % 2) * 128:256 + (i % 2) * 128 + 128],
                        func=AF.Identity), waits=[("tensor", tm)], rb="PTB")
                first_sub = (c == 0 and s == 0)
                if not first_sub:
                    scp = vo(lambda e: e.tensor_copy(out=SSB[:, :], in_=SACC[:, :]),
                             waits=[("vector", P.sacc_v)])
                for h in range(H):
                    glo, pos = h // 4, 32 * (h % 4)
                    eslot = h % 4
                    w = []
                    if h == 0:
                        w += [("vector", qrm), ("scalar", tev)]
                    if h >= 4:
                        w.append(("vector", P.e_evac - 3))
                    em = to(lambda e, h=h, s=s, glo=glo, pos=pos, eslot=eslot: e.matmul(
                        PP1[:, eslot * 128:(eslot + 1) * 128],
                        bb["kr"][pos:pos + 32, glo * TC + s * 128:glo * TC + s * 128 + 128],
                        bb["qr"][pos:pos + 32, glo * TC + s * 128:glo * TC + s * 128 + 128],
                        start=True, stop=False,
                        tile_position=(pos, 0)), waits=w, wb="PP1")
                    em = to(lambda e, h=h, s=s, glo=glo, pos=pos, eslot=eslot: e.matmul(
                        PP1[:, eslot * 128:(eslot + 1) * 128],
                        bb["kr"][pos:pos + 32, (4 + glo) * TC + s * 128:(4 + glo) * TC + s * 128 + 128],
                        bb["qr"][pos:pos + 32, (4 + glo) * TC + s * 128:(4 + glo) * TC + s * 128 + 128],
                        start=False, stop=True, tile_position=(pos, 0)))
                    P.e_evac = vo(lambda e, h=h, eslot=eslot: e.tensor_tensor(
                        out=bb["esb"][:, eslot * 128:(eslot + 1) * 128],
                        in0=PP1[:, eslot * 128:(eslot + 1) * 128],
                        in1=TRI[:, :], op=ALU.mult), waits=[("tensor", em)],
                        rb="PP1")
                    oslot = (h // 2) % 4
                    obase = 64 * (h % 2)
                    evw = [("vector", P.e_evac)]
                    if h >= 8:
                        evw.append(("vector", P.o_evac))
                    evm = to(lambda e, h=h, s=s, oslot=oslot, obase=obase, fs=first_sub: e.matmul(
                        PP3[obase:obase + 64, oslot * 128:(oslot + 1) * 128],
                        bb["vtok"][:, s * 1024 + h * 64:s * 1024 + (h + 1) * 64],
                        bb["esb"][:, (h % 4) * 128:((h % 4) + 1) * 128],
                        start=True, stop=fs,
                        tile_position=(0, obase)), waits=evw, wb="PP3")
                    if not first_sub:
                        evm = to(lambda e, h=h, s=s, glo=glo, pos=pos, oslot=oslot, obase=obase: e.matmul(
                            PP3[obase:obase + 64, oslot * 128:(oslot + 1) * 128],
                            SSB[pos:pos + 32, 64 * glo:64 * glo + 64],
                            bb["qr"][pos:pos + 32, glo * TC + s * 128:glo * TC + s * 128 + 128],
                            start=False, stop=False, tile_position=(pos, obase)),
                            waits=[("vector", scp)] if h == 0 else ())
                        evm = to(lambda e, h=h, s=s, glo=glo, pos=pos, oslot=oslot, obase=obase: e.matmul(
                            PP3[obase:obase + 64, oslot * 128:(oslot + 1) * 128],
                            SSB[pos:pos + 32, 256 + 64 * glo:256 + 64 * glo + 64],
                            bb["qr"][pos:pos + 32, (4 + glo) * TC + s * 128:(4 + glo) * TC + s * 128 + 128],
                            start=False, stop=True, tile_position=(pos, obase)))
                    supw = [("scalar", tev)] if h == 0 else []
                    if h == 0:
                        supw.append(("vector", P.sacc_v))
                    sup = to(lambda e, h=h, s=s, glo=glo, pos=pos: e.matmul(
                        PS[pos:pos + 32, 64 * glo:64 * glo + 64],
                        bb["ktok"][:, s * 1024 + glo * 128 + pos:s * 1024 + glo * 128 + pos + 32],
                        bb["vtok"][:, s * 1024 + h * 64:s * 1024 + (h + 1) * 64],
                        start=True, stop=True, tile_position=(0, pos)), waits=supw,
                        wb="PS")
                    sup = to(lambda e, h=h, s=s, glo=glo, pos=pos: e.matmul(
                        PS[pos:pos + 32, 256 + 64 * glo:256 + 64 * glo + 64],
                        bb["ktok"][:, s * 1024 + (4 + glo) * 128 + pos:s * 1024 + (4 + glo) * 128 + pos + 32],
                        bb["vtok"][:, s * 1024 + h * 64:s * 1024 + (h + 1) * 64],
                        start=True, stop=True, tile_position=(0, pos)), wb="PS")
                    if h % 2 == 1 and h % 8 == 7:
                        for hp in range(h - 7, h + 1, 2):
                            os2 = (hp // 2) % 4
                            P.o_evac = vo(lambda e, hp=hp, os2=os2, s=s: e.tensor_tensor(
                                out=fb["s2"][:, (hp // 2) * TC + s * 128:(hp // 2) * TC + s * 128 + 128],
                                in0=PP3[:, os2 * 128:(os2 + 1) * 128],
                                in1=fb["gate"][:, (hp // 2) * TC + s * 128:(hp // 2) * TC + s * 128 + 128],
                                op=ALU.mult),
                                waits=[("tensor", sup), ("scalar", gated_w)],
                                rb="PP3")
                P.s_upd = P.mark("tensor")
                P.sacc_v = vo(lambda e: e.tensor_tensor(
                    out=SACC[:, :], in0=SACC[:, :], in1=PS[:, :], op=ALU.add),
                    waits=[("tensor", P.s_upd)], rb="PS")
            ao_done = P.o_evac
            if debug and c == 0:
                dstore(dbg["d_ao"], fb["s2"][:, :], waits=[("vector", ao_done)])

            mt3, re3 = ln_stats(fb["s2"], 12, 13, 14, 1, [("vector", ao_done)],
                                sqb=fb["s1"], bslot=3)
            c3 = 0
            for i in range(G):
                c3 = vo(lambda e, i=i: e.tensor_tensor(
                    out=bb["lnb"][:, i * TC:(i + 1) * TC],
                    in0=fb["s2"][:, i * TC:(i + 1) * TC], in1=REP[:, TC:2 * TC],
                    op=ALU.mult), waits=[("vector", re3)] if i == 0 else ())
            wev = 0
            for j2 in range(RND):
                mmk = 0
                for jj in range(SL):
                    j = SL * j2 + jj
                    for i in range(G):
                        w = ()
                        if j2 == 0 and jj == 0 and i == 0:
                            w = [("vector", c3), ("vector", mt3)]
                        elif i == 0 and jj == 0 and j2 > 0:
                            w = [("vector", wev)]
                        mmk = to(lambda e, i=i, j=j, jj=jj: e.matmul(
                            PP2[:, jj * TC:jj * TC + TC],
                            WB[:, (3 * 8 + i) * 1024 + j * 128:(3 * 8 + i) * 1024 + (j + 1) * 128],
                            bb["lnb"][:, i * TC:(i + 1) * TC],
                            start=(i == 0), stop=False), waits=w, wb="PP2")
                    mmk = to(lambda e, j=j, jj=jj: e.matmul(
                        PP2[:, jj * TC:jj * TC + TC],
                        AUGW[0:1, 3 * 1024 + j * 128:3 * 1024 + (j + 1) * 128],
                        rowb(3, off=1), start=False, stop=False))
                    mmk = to(lambda e, j=j, jj=jj: e.matmul(
                        PP2[:, jj * TC:jj * TC + TC],
                        AUGW[0:1, 4 * 1024 + j * 128:4 * 1024 + (j + 1) * 128],
                        rowb(4, off=1), start=False, stop=True))
                for jj in range(SL):
                    j = SL * j2 + jj
                    wev = vo(lambda e, j=j, jj=jj: e.tensor_tensor(
                        out=fb["x1"][:, j * TC:(j + 1) * TC],
                        in0=PP2[:, jj * TC:jj * TC + TC],
                        in1=fb["x2"][:, j * TC:(j + 1) * TC], op=ALU.add),
                        waits=[("tensor", mmk)], rb="PP2")
            if debug and c == 0:
                dstore(dbg["d_x3"], fb["x1"][:, :], waits=[("vector", wev)])

            x4m = emit_ffn(fb["x1"], fb["x2"], [("vector", wev)], None, c)

            P.ph2_ffn2v = P.mark("vector")
            oev = 0
            for s in range(NSUB):
                for i in range(G):
                    idx = s * G + i
                    w = [("gpsimd", x4m)] if idx == 0 else []
                    if idx == 0 and c > 0:
                        w.append(("store", P.ph2_ost))
                    pb = (PTF, PTB)[idx % 2]
                    off = 128 * ((idx // 2) % 4)
                    tm = to(lambda e, s=s, i=i, pb=pb, off=off: e.transpose(
                        pb[:, off:off + 128],
                        fb["x2"][:, i * TC + s * 128:i * TC + s * 128 + 128],
                        IDENT[:, :]), waits=w, wb=("PTF", "PTB")[idx % 2])
                    oev = ao(lambda e, s=s, i=i, pb=pb, off=off: e.activation(
                        out=X0TOK[:, s * 1024 + i * 128:s * 1024 + (i + 1) * 128],
                        in_=pb[:, off:off + 128],
                        func=AF.Identity), waits=[("tensor", tm)],
                        rb=("PTF", "PTB")[idx % 2])
            ost = 0
            for s in range(NSUB):
                ost = P.store_dma(out_h[c * TC + s * 128:c * TC + (s + 1) * 128, :],
                                  X0TOK[:, s * 1024:(s + 1) * 1024],
                                  waits=[("scalar", oev)])
            P.ph2_ost = ost
            P.ph2_ffn2t = P.mark("tensor")

        return _finish()

    return nc


def _perm():
    p = np.zeros(D, dtype=np.int64)
    for h in range(H):
        for j in range(32):
            p[32 * h + j] = 64 * h + j
            p[512 + 32 * h + j] = 64 * h + 32 + j
    return p


def _vec_cols(inputs):
    v = np.zeros((128, 48), dtype=np.float32)

    def col8(x):
        return np.ascontiguousarray(x.reshape(8, 128).T)
    mix = np.asarray(inputs["L0_attn_a_mix"], np.float32)
    v[:, 0:8] = col8(mix)
    v[:, 8:16] = col8(np.float32(1.0) - mix)
    v[:, 16:24] = col8(np.asarray(inputs["L0_attn_a_global_decay"], np.float32))
    v[:, 24:32] = col8(np.asarray(inputs["L0_attn_b_gamma"], np.float32))
    v[:, 32:40] = col8(np.asarray(inputs["L0_attn_b_beta"], np.float32))
    v[:, 40] = 1.0
    v[:, 41] = 1.0 / D
    v[:, 42] = (np.arange(128) % 32).astype(np.float32)
    return v


def kernel(**inputs):
    n_chunks = T // TC
    nc = build_program(n_chunks, debug=False)
    perm = _perm()
    tC = lambda a: np.ascontiguousarray(np.asarray(a, dtype=np.float32))
    base = {
        "wrt": tC(np.asarray(inputs["L0_attn_a_W_R"]).T),
        "wkt": tC(np.asarray(inputs["L0_attn_a_W_K"]).T),
        "wvt": tC(np.asarray(inputs["L0_attn_a_W_V"]).T),
        "wtt": tC(np.asarray(inputs["L0_attn_a_W_td"]).T),
        "wq": tC(np.asarray(inputs["L0_attn_b_Wq"])[:, perm]),
        "wk": tC(np.asarray(inputs["L0_attn_b_Wk"])[:, perm]),
        "wv": tC(inputs["L0_attn_b_Wv"]),
        "wo": tC(inputs["L0_attn_b_Wo"]),
        "wg1": tC(inputs["L0_attn_b_Wg1"]),
        "wg2": tC(inputs["L0_attn_b_Wg2"]),
        "rhsT": tC(np.asarray(inputs["L0_ffn_rhs"]).T),
        "vecs": _vec_cols(inputs),
        "ident": np.eye(128, dtype=np.float32),
        "trimask": np.tril(np.ones((128, 128), np.float32)).T.copy(),
    }
    emb = np.asarray(inputs["L0_attn_a_embeddings"], np.float32)
    in_maps = []
    for core in range(8):
        m = dict(base)
        m["x0"] = np.ascontiguousarray(emb[core % B])
        in_maps.append(m)
    res = run_bass_kernel_spmd(nc, in_maps, core_ids=list(range(8)))
    out = np.stack([np.asarray(res.results[b]["out"]) for b in range(B)], axis=0)
    return out.astype(np.float32)


# version 06
# speedup vs baseline: 1.0022x; 1.0022x over previous
import math
import numpy as np
import ml_dtypes
import concourse.bass as bass
import concourse.mybir as mybir
from concourse.bass_utils import run_bass_kernel_spmd

F32 = mybir.dt.float32
BF16 = mybir.dt.bfloat16
AF = mybir.ActivationFunctionType
ALU = mybir.AluOpType
AXX = mybir.AxisListType.X

B, T, D = 4, 4096, 1024
G = 8
H, HD = 16, 64
TC = 128
NSUB = TC // 128
SL = 512 // TC
RND = G // SL
EPS_LN = 1e-5
EPS_A = 1e-6
BETA = 0.3
THR = 0.2
MAGIC = 12582912.0
E10 = float(np.exp(np.float32(10.0)))
LN1E4 = float(np.log(10000.0))
TWO_PI = 2.0 * math.pi
INV_2PI = float(1.0 / TWO_PI)
CW1 = float(np.float32(6.28125))
CW2 = float(np.float32(1.9353628e-3))
CW3 = float(TWO_PI - (np.float64(np.float32(CW1)) + np.float64(np.float32(CW2))))

ENGS = ("sync", "tensor", "vector", "scalar", "gpsimd")


class Plan:

    AUTOSYNC = ("vector", "scalar", "gpsimd")

    def __init__(self):
        self.q = {e: [] for e in ENGS}
        self.cnt = {e: 0 for e in ENGS}
        self.store = 0
        self.pending = {e: 0 for e in ENGS}
        self.pend_dma = {e: 0 for e in ENGS}
        self.prev2 = {e: [0, 0] for e in ENGS}
        self.cnt["gdma"] = 0
        self.bank_read = {}

    def wbank(self, bank):
        br = self.bank_read.get(bank)
        return [br] if br else []

    def rbank(self, bank, eng, mark):
        self.bank_read[bank] = (eng, mark)
        return mark

    def op(self, eng, fn, waits=(), inc=1, bare=False):
        waits = list(waits)
        if bare:
            if self.pend_dma.get(eng):
                waits.append(("gdma", self.pend_dma[eng]))
                self.pend_dma[eng] = 0
            self.q[eng].append((fn, tuple(w for w in waits if w[1] > 0), 0, eng))
            return self.cnt[eng]
        if eng in self.AUTOSYNC:
            dep = max(self.prev2[eng][0], self.pending[eng])
            self.pending[eng] = 0
            if dep > 0:
                waits.append((eng, dep))
            if self.pend_dma.get(eng):
                waits.append(("gdma", self.pend_dma[eng]))
                self.pend_dma[eng] = 0
        self.q[eng].append((fn, tuple(w for w in waits if w[1] > 0), inc, eng))
        self.cnt[eng] += inc
        if inc:
            self.prev2[eng] = [self.prev2[eng][1], self.cnt[eng]]
        return self.cnt[eng]

    def dma(self, eng, out, in_, waits=(), no_pend=False):
        def fn(e):
            return e.dma_start(out=out, in_=in_)
        key = "gdma" if eng == "gpsimd" else eng
        self.q[eng].append((fn, tuple(w for w in waits if w[1] > 0), 16, key))
        self.cnt[key] += 16
        if eng in self.AUTOSYNC and not no_pend:
            self.pend_dma[eng] = self.cnt[key]
        return self.cnt[key]

    def store_dma(self, out, in_, waits=()):
        def fn(e):
            return e.dma_start(out=out, in_=in_)
        self.q["sync"].append((fn, tuple(w for w in waits if w[1] > 0), 16, "store"))
        self.store += 16
        return self.store

    def fence(self, eng):
        val = self.cnt[eng]
        self.q[eng].append(((lambda e: None), ((eng, val),), 0, eng))

    def guard(self, eng, n=2):
        self.pending[eng] = self.cnt[eng]

    def mark(self, eng):
        return self.cnt[eng]


def build_program(n_chunks, debug=False, bass_debug=False, stages=3):
    TT = n_chunks * TC
    AJ = TC + 1
    nc = bass.Bass("TRN2", target_bir_lowering=False, debug=bass_debug)

    x0_h = nc.dram_tensor("x0", [TT, D], F32, kind="ExternalInput").ap()
    w_h = {}
    for nm in ("wrt", "wkt", "wvt", "wtt", "wq", "wk", "wv", "wo"):
        w_h[nm] = nc.dram_tensor(nm, [D, D], F32, kind="ExternalInput").ap()
    wg1_h = nc.dram_tensor("wg1", [D, HD], F32, kind="ExternalInput").ap()
    wg2_h = nc.dram_tensor("wg2", [HD, D], F32, kind="ExternalInput").ap()
    rhsT_h = nc.dram_tensor("rhsT", [D, D], F32, kind="ExternalInput").ap()
    vecs_h = nc.dram_tensor("vecs", [128, 48], F32, kind="ExternalInput").ap()
    ident_h = nc.dram_tensor("ident", [128, 128], F32, kind="ExternalInput").ap()
    tri_h = nc.dram_tensor("trimask", [128, 128], F32, kind="ExternalInput").ap()
    spk_h = nc.dram_tensor("spk_scr", [n_chunks, 128, G * TC], BF16, kind="Internal").ap()
    out_h = nc.dram_tensor("out", [TT, D], F32, kind="ExternalOutput").ap()

    dbg = {}
    if debug:
        for name, dt in [("d_r", F32), ("d_ek", F32), ("d_v", F32), ("d_td", F32),
                         ("d_a", F32), ("d_b", F32), ("d_c5", F32), ("d_mem", F32),
                         ("d_spk", BF16), ("d_x1", F32), ("d_f1", F32), ("d_x2", F32),
                         ("d_qr", BF16), ("d_kr", BF16), ("d_ao", F32),
                         ("d_gate", F32), ("d_x3", F32), ("d_ln2", BF16)]:
            dbg[name] = nc.dram_tensor(name, [128, G * TC], dt, kind="ExternalOutput").ap()

    P = Plan()

    import contextlib
    with contextlib.ExitStack() as stk:
        E = stk.enter_context

        WA = E(nc.sbuf_tensor("WA", [128, 4 * 8 * 1024], BF16))
        WB = WA
        QR = E(nc.sbuf_tensor("QR", [128, 8 * 1024], BF16))
        WGT = E(nc.sbuf_tensor("WGT", [128, 8 * 64 + 1024], BF16))
        AUGW = E(nc.sbuf_tensor("AUGW", [1, 9 * 1024], BF16))
        ROWB = E(nc.sbuf_tensor("ROWB", [1, 5 * (TC + 1)], BF16))
        CARY = E(nc.sbuf_tensor("CARY", [128, 24], F32))
        CS = E(nc.sbuf_tensor("CS", [128, 2 * TC], BF16))
        VEC = E(nc.sbuf_tensor("VEC", [128, 96], F32))
        VECB = E(nc.sbuf_tensor("VECB", [128, 32], BF16))
        VGD = E(nc.sbuf_tensor("VGD", [128, 16], F32))
        VRQ = E(nc.sbuf_tensor("VRQ", [128, 24], F32))
        VIF = E(nc.sbuf_tensor("VIF", [128, 1], F32))
        ONER = E(nc.sbuf_tensor("ONER", [1, 128], F32))
        IDENT = E(nc.sbuf_tensor("IDENT", [128, 128], F32))
        IDENTB = E(nc.sbuf_tensor("IDENTB", [128, 128], BF16))
        TRI = E(nc.sbuf_tensor("TRI", [128, 128], F32))
        ROWS = E(nc.sbuf_tensor("ROWS", [1, 15 * AJ], F32))
        X0TOK = E(nc.sbuf_tensor("x0tok", [128, NSUB * 1024], F32))
        XIN = E(nc.sbuf_tensor("xin", [128, 2 * NSUB * 1024], F32))
        REP = E(nc.sbuf_tensor("REP", [128, 3 * TC], F32))
        NB = G * TC
        fb = {}
        for nm in ("x0cm", "r", "ek", "v", "td", "aA", "bA",
                   "s1", "s2", "mhA", "x1", "x2"):
            fb[nm] = E(nc.sbuf_tensor("f_" + nm, [128, NB], F32))
        fb["gate"] = fb["r"]
        bb = {}
        for nm in ("xa", "spk", "lnb", "qr", "kr", "ktok",
                   "vtok", "esb", "q3"):
            bb[nm] = E(nc.sbuf_tensor("b_" + nm, [128, NB], BF16))
        XRSF = E(nc.sbuf_tensor("f_xrsA", [128, G * (TC + 1)], F32))
        SQB = E(nc.sbuf_tensor("f_sqb", [128, G * TC], F32))
        C5B = E(nc.sbuf_tensor("f_c5b", [128, G * TC], F32))
        SSB = E(nc.sbuf_tensor("ssb", [128, 512], BF16))
        SACC = E(nc.sbuf_tensor("sacc", [128, 512], F32))
        G1B = E(nc.sbuf_tensor("g1b", [64, TC], BF16))

        PP0 = E(nc.psum_tensor("PP0", [128, 512], F32))
        PP1 = E(nc.psum_tensor("PP1", [128, 512], F32))
        PP2 = E(nc.psum_tensor("PP2", [128, 512], F32))
        PP3 = E(nc.psum_tensor("PP3", [128, 512], F32))
        PTF = E(nc.psum_tensor("PTF", [128, 512], F32))
        PTB = E(nc.psum_tensor("PTB", [128, 512], F32))
        PX = E(nc.psum_tensor("PX", [128, 512], F32))
        PS = E(nc.psum_tensor("PS", [128, 512], F32))

        sems = {e: E(nc.semaphore("s_" + e)) for e in ENGS}
        sems["gdma"] = E(nc.semaphore("s_gdma"))
        s_store = E(nc.semaphore("s_st"))
        block = E(nc.Block())


        def _finish():
            P.q["gpsimd"].append(((lambda e: e.wait_ge(s_store, P.store)), (), 0, "gpsimd"))
            def replay(name):
                def run(eng):
                    wmax = {}
                    for fn, waits, inc, semkey in P.q[name]:
                        for weng, val in waits:
                            if wmax.get(weng, -1) >= val:
                                continue
                            wmax[weng] = val
                            sem = s_store if weng == "store" else sems[weng]
                            eng.wait_ge(sem, val)
                        inst = fn(eng)
                        if inc and inst is not None:
                            inst.then_inc(s_store if semkey == "store" else sems[semkey], inc)
                return run
            block.sync(replay("sync"))
            block.tensor(replay("tensor"))
            block.vector(replay("vector"))
            block.scalar(replay("scalar"))
            block.gpsimd(replay("gpsimd"))
            return nc
        def vo(fn, waits=(), inc=1, rb=None, bare=False):
            m = P.op("vector", fn, waits, inc, bare=bare)
            if rb:
                P.rbank(rb, "vector", m)
            return m

        def ao(fn, waits=(), inc=1, rb=None, bare=False):
            m = P.op("scalar", fn, waits, inc, bare=bare)
            if rb:
                P.rbank(rb, "scalar", m)
            return m

        def po(fn, waits=(), inc=1, rb=None, bare=False):
            m = P.op("gpsimd", fn, waits, inc, bare=bare)
            if rb:
                P.rbank(rb, "gpsimd", m)
            return m

        def to(fn, waits=(), inc=1, wb=None):
            w = list(waits)
            if wb:
                w += P.wbank(wb)
            return P.op("tensor", fn, w, inc)
        ld = lambda out, in_, waits=(): P.dma("sync", out, in_, waits)
        gld = lambda out, in_, waits=(): P.dma("gpsimd", out, in_, waits)

        def dstore(out, in_, waits=(), flush=True):
            v = P.store_dma(out, in_, waits)
            if flush:
                for eng in ("vector", "scalar", "gpsimd", "tensor", "sync"):
                    P.q[eng].append(((lambda e: None), (("store", v),), 0, eng))
            return v

        def dflush():
            for eng in ("vector", "scalar", "gpsimd", "tensor", "sync"):
                P.q[eng].append(((lambda e: None), (("store", P.store),), 0, eng))

        def row(slot, n=TC, off=0):
            return ROWS[0:1, slot * AJ + off: slot * AJ + off + n]

        def rowb(slot, n=TC, off=0):
            return ROWB[0:1, slot * AJ + off: slot * AJ + off + n]

        import os as _os
        _kcut = int(_os.environ.get("KCUT", "0"))
        cons_ld = ld(VEC[:, 0:48], vecs_h)
        cons_ld = ld(IDENT[:, :], ident_h)
        cons_ld = ld(TRI[:, :], tri_h)

        idb = vo(lambda e: e.tensor_copy(out=IDENTB[:, :], in_=IDENT[:, :]),
                 waits=[("sync", cons_ld)])
        vo(lambda e: e.tensor_copy(out=VECB[:, 0:1], in_=VEC[:, 40:41]),
           waits=[("sync", cons_ld)])
        vo(lambda e: e.tensor_copy(out=VECB[:, 1:9], in_=VEC[:, 32:40]),
           waits=[("sync", cons_ld)])
        vcb = vo(lambda e: e.tensor_copy(out=VECB[:, 9:25], in_=VEC[:, 0:16]),
                 waits=[("sync", cons_ld)])
        if not (_kcut & 4):
            po(lambda e: e.memset(CARY[:, :], 0.0))
            onr = po(lambda e: e.memset(ONER[0:1, :], 1.0))
            po(lambda e: e.memset(SACC[:, :], 0.0))
            po(lambda e: e.memset(XRSF[:, 0:G * AJ:AJ], 0.0))
            pz = po(lambda e: e.memset(ROWS[0:1, :], 0.0))
            pz = po(lambda e: e.memset(ROWB[0:1, :], 0.0))
            P.guard("gpsimd")
            pz = po(lambda e: e.memset(ROWB[0:1, 4 * AJ + 1:4 * AJ + 1 + TC], 1.0))
        else:
            onr = pz = 0
        if not (_kcut & 2):
            gde = ao(lambda e: e.activation(out=VGD[:, 0:8], in_=VEC[:, 16:24],
                                            func=AF.Exp), waits=[("sync", cons_ld)])
            gd5 = vo(lambda e: e.tensor_scalar(out=VGD[:, 8:16], in0=VGD[:, 0:8],
                                               scalar1=THR, scalar2=None,
                                               op0=ALU.mult, op1=ALU.bypass),
                     waits=[("scalar", gde)])
        else:
            gde = gd5 = 0

        if not (_kcut & 32):
            P.fence("sync")
        _sub = int(_os.environ.get("KSUB", "9"))
        if _sub < 1:
            return _finish()
        _kq = int(_os.environ.get("KQ", "9"))
        tq_ev = 0
        rql = 0
        for i in range(int(_os.environ.get("KQI", str(G)))):
            if not (_kcut & 16):
                rql = P.dma("gpsimd", fb["x1"][:, 0:1024],
                            rhsT_h[i * 128:(i + 1) * 128, :],
                            waits=[("scalar", tq_ev)] if (i > 0 and tq_ev) else (),
                            no_pend=True)
            rq = 0
            if not (_kcut & 8):
                if _kq < 2:
                    continue
                am = vo(lambda e, i=i: e.tensor_reduce(
                    out=VRQ[:, 0 + i:1 + i], in_=fb["x1"][:, 0:1024],
                    axis=AXX, op=ALU.max, apply_absolute_value=True),
                    waits=[("gdma", rql)])
                if _kq < 3:
                    continue
                P.guard("vector")
                am = vo(lambda e, i=i: e.tensor_scalar(
                    out=VRQ[:, 8 + i:9 + i], in0=VRQ[:, 0 + i:1 + i], scalar1=1e-12,
                    scalar2=1.0 / 127.0, op0=ALU.max, op1=ALU.mult))
                P.guard("vector")
                am = vo(lambda e, i=i: e.reciprocal(out=VRQ[:, 16 + i:17 + i],
                                                    in_=VRQ[:, 8 + i:9 + i]))
                if _kq < 4:
                    continue
                qm = ao(lambda e, i=i: e.activation(
                    out=fb["x2"][:, 0:1024], in_=fb["x1"][:, 0:1024],
                    func=AF.Identity, scale=VRQ[:, 16 + i:17 + i]),
                    waits=[("vector", am)])
                if _kq < 5:
                    continue
                rq = vo(lambda e, i=i: e.tensor_scalar(
                    out=fb["x2"][:, 0:1024], in0=fb["x2"][:, 0:1024],
                    scalar1=MAGIC, scalar2=MAGIC, op0=ALU.add, op1=ALU.subtract),
                    waits=[("scalar", qm)])
                P.guard("vector")
                rq = vo(lambda e, i=i: e.tensor_scalar(
                    out=bb["q3"][:, 0:1024], in0=fb["x2"][:, 0:1024],
                    scalar1=127.0, scalar2=-127.0, op0=ALU.min, op1=ALU.max))
                if _kq < 6:
                    continue
            for j in range(int(_os.environ.get("KQT", str(G)))):
                slot = j % 4
                w = []
                if j == 0:
                    if rq:
                        w.append(("vector", rq))
                    w.append(("vector", idb))
                if j >= 4 and tq_ev > 3:
                    w.append(("scalar", tq_ev - 3))
                tqm = to(lambda e, j=j, slot=slot: e.matmul(
                    PTB[:, slot * 128:slot * 128 + 128],
                    bb["q3"][:, j * 128:(j + 1) * 128],
                    IDENTB[:, :], start=True, stop=True), waits=w, wb="PTB")
                tq_ev = ao(lambda e, i=i, j=j, slot=slot: e.activation(
                    out=QR[:, (j * 8 + i) * 128:(j * 8 + i + 1) * 128],
                    in_=PTB[:, slot * 128:slot * 128 + 128], func=AF.Identity),
                    waits=[("tensor", tqm)], rb="PTB")

        wa_done = 0
        if not (_kcut & 1):
            for mi, nm in enumerate(("wrt", "wkt", "wvt", "wtt")):
                for i in range(G):
                    wa_done = gld(WA[:, (mi * 8 + i) * 1024:(mi * 8 + i + 1) * 1024],
                                  w_h[nm][i * 128:(i + 1) * 128, :])
            for i in range(G):
                gld(WGT[:, i * 64:(i + 1) * 64], wg1_h[i * 128:(i + 1) * 128, :])
            wg_done = gld(WGT[0:64, 512:512 + 1024], wg2_h[:, :])
            wa_done = wg_done
        else:
            wg_done = 0

        if _sub < 2:
            return _finish()
        def emit_wsum(w_sb, base, mixcol, slot, ncols=1024, extra_w=()):
            ev = 0
            for half in range(2 if ncols > 512 else 1):
                hw = min(512, ncols)
                mm = 0
                for i in range(G):
                    w = (list(extra_w) + [("sync", cons_ld), ("vector", vcb)]) if i == 0 else []
                    w = list(w) + [("tensor", P.wsum_t)]
                    if i == 0:
                        w.append(("vector", P.wsum_v))
                    lcol = VECB[:, 0:1] if mixcol is None else VECB[:, mixcol + i:mixcol + i + 1]
                    mm = to(lambda e, i=i, half=half, hw=hw, lcol=lcol: e.matmul(
                        PX[0:1, 0:hw], lcol,
                        w_sb[:, base + i * ncols + half * 512:base + i * ncols + half * 512 + hw],
                        start=(i == 0), stop=(i == G - 1)),
                        waits=w, wb="PX")
                    P.wsum_t = mm
                ev = vo(lambda e, half=half, hw=hw: e.tensor_copy(
                    out=AUGW[0:1, slot * 1024 + half * 512:slot * 1024 + half * 512 + hw],
                    in_=PX[0:1, 0:hw]), waits=[("tensor", mm)], rb="PX")
                P.wsum_v = ev
            return ev

        lw = 0
        P.wsum_t = 0
        P.wsum_v = 0
        for m_i in range(4):
            lw = emit_wsum(WA, m_i * 8192, 9, 2 * m_i,
                           extra_w=[("gdma", wa_done)])
            lw = emit_wsum(WA, m_i * 8192, 17, 2 * m_i + 1,
                           extra_w=[("gdma", wa_done)])
        mm = 0
        for i in range(G):
            mm = to(lambda e, i=i: e.matmul(
                PX[0:1, 0:64], VECB[:, 0:1], WGT[:, i * 64:(i + 1) * 64],
                start=(i == 0), stop=(i == G - 1)),
                waits=[("gdma", wg_done), ("vector", lw), ("vector", vcb),
                       ("vector", P.wsum_v)] if i == 0 else (), wb="PX")
        lw = vo(lambda e: e.tensor_copy(out=AUGW[0:1, 8 * 1024:8 * 1024 + 64],
                                        in_=PX[0:1, 0:64]), waits=[("tensor", mm)],
                rb="PX")
        P.wsum_v = lw
        if _sub < 3:
            return _finish()
        invf = ao(lambda e: e.activation(out=VIF[:, 0:1], in_=VEC[:, 42:43],
                                         func=AF.Exp, scale=-LN1E4 / 32.0),
                  waits=[("sync", cons_ld)])

        def emit_trig_pair(tok_base):
            s2 = fb["s2"]
            wn = [s2[:, k * TC:(k + 1) * TC] for k in range(8)]
            pm = po(lambda e: e.iota(wn[0], [[1, TC]], base=tok_base,
                                     channel_multiplier=0,
                                     allow_small_or_imprecise_dtypes=True))
            angm = ao(lambda e: e.activation(out=wn[1], in_=wn[0],
                                             func=AF.Identity,
                                             scale=VIF[:, 0:1]),
                      waits=[("gpsimd", pm), ("scalar", invf)], inc=1)
            marks = []
            for quarter, base_w, dst in ((True, 2, 0), (False, 5, TC)):
                qadd = 0.25 if quarter else 0.0
                wk, wa, wb = wn[base_w], wn[base_w + 1], wn[base_w + 2]
                m = vo(lambda e, wk=wk, qadd=qadd: e.tensor_scalar(
                    out=wk, in0=wn[1], scalar1=INV_2PI, scalar2=qadd,
                    op0=ALU.mult, op1=ALU.add), waits=[("scalar", angm)])
                P.guard("vector")
                m = vo(lambda e, wk=wk: e.tensor_scalar(
                    out=wk, in0=wk, scalar1=MAGIC, scalar2=MAGIC,
                    op0=ALU.add, op1=ALU.subtract))
                P.guard("vector")
                m = vo(lambda e, wk=wk, wa=wa: e.scalar_tensor_tensor(
                    out=wa, in0=wk, scalar=-CW1, in1=wn[1],
                    op0=ALU.mult, op1=ALU.add))
                P.guard("vector")
                m = vo(lambda e, wk=wk, wa=wa, wb=wb: e.scalar_tensor_tensor(
                    out=wb, in0=wk, scalar=-CW2, in1=wa,
                    op0=ALU.mult, op1=ALU.add))
                P.guard("vector")
                m = vo(lambda e, wk=wk, wa=wa, wb=wb: e.scalar_tensor_tensor(
                    out=wa, in0=wk, scalar=-CW3, in1=wb,
                    op0=ALU.mult, op1=ALU.add))
                if quarter:
                    P.guard("vector")
                    m = vo(lambda e, wa=wa: e.tensor_scalar(
                        out=wa, in0=wa, scalar1=math.pi / 2.0, scalar2=None,
                        op0=ALU.add, op1=ALU.bypass))
                marks.append(ao(lambda e, wa=wa, dst=dst: e.activation(
                    out=CS[:, dst:dst + TC], in_=wa, func=AF.Sin),
                    waits=[("vector", m)]))
            return marks[-1]

        def ln_stats(xbuf, slot_m, slot_t, slot_r, rep_slot, first_waits, sqb=None,
                     bslot=5, mt_extra=()):
            sqb = sqb if sqb is not None else fb["s2"]
            mm = 0
            for i in range(G):
                mm = to(lambda e, i=i: e.matmul(
                    PX[0:1, 0:TC], VEC[:, 41:42], xbuf[:, i * TC:(i + 1) * TC],
                    start=(i == 0), stop=(i == G - 1)),
                    waits=first_waits if i == 0 else (), wb="PX")
            sq = 0
            for i in range(G):
                sq = ao(lambda e, i=i, sqb=sqb: e.activation(
                    out=sqb[:, i * TC:(i + 1) * TC],
                    in_=xbuf[:, i * TC:(i + 1) * TC], func=AF.Square),
                    waits=(list(first_waits) + [("tensor", mm)]) if i == 0 else ())
            mm2 = 0
            for i in range(G):
                mm2 = to(lambda e, i=i, sqb=sqb: e.matmul(
                    PX[0:1, 256:256 + TC], VEC[:, 41:42],
                    sqb[:, i * TC:(i + 1) * TC],
                    start=(i == 0), stop=(i == G - 1)),
                    waits=[("scalar", sq)] if i == 0 else (), wb="PX")
            rv = vo(lambda e: e.tensor_copy(out=row(slot_m), in_=PX[0:1, 0:TC]),
                    waits=[("tensor", mm2), ("tensor", mm)], rb="PX")
            P.guard("vector")
            rv = vo(lambda e: e.tensor_tensor(out=row(slot_r), in0=row(slot_m),
                                              in1=row(slot_m), op=ALU.mult))
            P.guard("vector")
            rv = vo(lambda e: e.scalar_tensor_tensor(
                out=row(slot_r), in0=row(slot_r), scalar=-1.0,
                in1=PX[0:1, 256:256 + TC], op0=ALU.mult, op1=ALU.add), rb="PX")
            P.guard("vector")
            rv = vo(lambda e: e.tensor_scalar(
                out=row(slot_r), in0=row(slot_r), scalar1=EPS_LN, scalar2=None,
                op0=ALU.add, op1=ALU.bypass))
            ra = ao(lambda e: e.activation(out=row(slot_r), in_=row(slot_r),
                                           func=AF.Ln), waits=[("vector", rv)])
            P.guard("scalar")
            ra = ao(lambda e: e.activation(out=row(slot_r), in_=row(slot_r),
                                           func=AF.Exp, scale=-0.5))
            mt = vo(lambda e: e.scalar_tensor_tensor(
                out=row(slot_t, off=1), in0=row(slot_m), scalar=-1.0,
                in1=row(slot_r), op0=ALU.mult, op1=ALU.mult),
                waits=[("scalar", ra)])
            P.guard("vector")
            mt = vo(lambda e: e.tensor_copy(out=rowb(bslot, off=1),
                                            in_=row(slot_t, off=1)),
                    waits=list(mt_extra))
            rp = to(lambda e: e.matmul(
                PX[:, 256:256 + TC], ONER[0:1, :], row(slot_r),
                start=True, stop=True),
                waits=[("scalar", ra), ("vector", rv), ("gpsimd", onr),
                       ("vector", P.rep_v)], wb="PX")
            re = vo(lambda e: e.tensor_copy(
                out=REP[:, rep_slot * TC:(rep_slot + 1) * TC],
                in_=PX[:, 256:256 + TC]), waits=[("tensor", rp)], rb="PX")
            P.rep_v = re
            return mt, re

        def replicate(src_row_ap, rep_slot, waits):
            rp = to(lambda e: e.matmul(PX[:, 256:256 + TC], ONER[0:1, :],
                                       src_row_ap, start=True, stop=True),
                    waits=list(waits) + [("gpsimd", onr), ("vector", P.rep_v)],
                    wb="PX")
            re = vo(lambda e: e.tensor_copy(
                out=REP[:, rep_slot * TC:(rep_slot + 1) * TC],
                in_=PX[:, 256:256 + TC]), waits=[("tensor", rp)], rb="PX")
            P.rep_v = re
            return re

        P.fence("sync")
        if stages < 1:
            return _finish()
        spk_stored = []
        P.rep_v = 0
        P.scand_prev = 0
        P.x0ev = 0
        P.last_proj_t = 0
        P.cm_prev = 0
        P.e_evac = 0
        P.o_evac = 0
        P.s_upd = 0
        P.sacc_v = 0
        P.x0cm_free = []
        P.x0ev_h = [0, 0]

        def emit_x0_load(c):
            hb = (c % 2) * NSUB * 1024
            lm = 0
            for s in range(NSUB):
                lm = ld(XIN[:, hb + s * 1024:hb + (s + 1) * 1024],
                        x0_h[c * TC + s * 128:c * TC + (s + 1) * 128, :],
                        waits=[("scalar", P.x0ev_h[c % 2])] if s == 0 else ())
            return lm

        P.lm_pref = {}
        for c in range(n_chunks):
            x0cm = fb["x0cm"]
            hb = (c % 2) * NSUB * 1024
            lm = P.lm_pref.pop(c, None)
            if lm is None:
                lm = emit_x0_load(c)
            if c + 1 < n_chunks:
                P.lm_pref[c + 1] = emit_x0_load(c + 1)
            ev = 0
            for s in range(NSUB):
                for i in range(G):
                    idx = s * G + i
                    w = list(P.x0cm_free) + [("sync", lm)] if idx == 0 else []
                    pb = (PTF, PTB)[idx % 2]
                    off = 128 * ((idx // 2) % 4)
                    tm = to(lambda e, s=s, i=i, pb=pb, off=off, hb=hb: e.transpose(
                        pb[:, off:off + 128],
                        XIN[:, hb + s * 1024 + i * 128:hb + s * 1024 + (i + 1) * 128],
                        IDENT[:, :]), waits=w, wb=("PTF", "PTB")[idx % 2])
                    ev = ao(lambda e, s=s, i=i, pb=pb, off=off, x0cm=x0cm: e.activation(
                        out=x0cm[:, i * TC + s * 128:i * TC + s * 128 + 128],
                        in_=pb[:, off:off + 128],
                        func=AF.Identity), waits=[("tensor", tm)],
                        rb=("PTF", "PTB")[idx % 2])
            P.x0ev_h[c % 2] = ev
            P.x0ev = ev
            x0w = []
            if c > 0:
                xbnd = po(lambda e: e.tensor_copy(
                    out=XRSF[:, 0:G * AJ:AJ],
                    in_=XRSF[:, TC:TC + (G - 1) * AJ + 1:AJ]),
                    waits=[("scalar", P.xam_prev)])
                mbnd = po(lambda e: e.tensor_copy(out=rowb(0, n=1, off=0),
                                                  in_=rowb(0, n=1, off=TC)),
                          waits=[("tensor", P.last_proj_t)])
            else:
                xbnd = P.mark("gpsimd")
                mbnd = xbnd

            mt0, rs0rep = ln_stats(x0cm, 0, 1, 2, 0,
                                   [("scalar", ev)], sqb=SQB,
                                   bslot=0,
                                   mt_extra=[("gpsimd", mbnd),
                                             ("tensor", P.last_proj_t)])
            xrs = XRSF
            xm = 0
            for i in range(G):
                xm = vo(lambda e, i=i, xrs=xrs: e.tensor_tensor(
                    out=xrs[:, i * AJ + 1:i * AJ + 1 + TC],
                    in0=x0cm[:, i * TC:(i + 1) * TC], in1=REP[:, 0:TC],
                    op=ALU.mult),
                    waits=[("vector", rs0rep), ("gpsimd", xbnd),
                           ] if i == 0 else (), bare=(0 < i < G - 1))
            bm = xbnd
            mtb = mbnd
            xam = 0
            for i in range(G):
                xam = ao(lambda e, i=i, xrs=xrs: e.activation(
                    out=bb["lnb"][:, i * TC:(i + 1) * TC],
                    in_=xrs[:, i * AJ + 1:i * AJ + 1 + TC],
                    func=AF.Identity, scale=VEC[:, 0 + i:1 + i]),
                    waits=[("vector", xm), ("gpsimd", bm)] if i == 0 else (),
                    bare=(i > 0))
            for i in range(G):
                xam = ao(lambda e, i=i, xrs=xrs: e.activation(
                    out=bb["q3"][:, i * TC:(i + 1) * TC],
                    in_=xrs[:, i * AJ:i * AJ + TC],
                    func=AF.Identity, scale=VEC[:, 8 + i:9 + i]),
                    bare=(i < G - 1))
            xa2 = 0
            for i in range(G):
                xa2 = po(lambda e, i=i: e.tensor_tensor(
                    out=bb["xa"][:, i * TC:(i + 1) * TC],
                    in0=bb["lnb"][:, i * TC:(i + 1) * TC],
                    in1=bb["q3"][:, i * TC:(i + 1) * TC], op=ALU.add),
                    waits=[("scalar", xam)] if i == 0 else (),
                    bare=(0 < i < G - 1))
            P.xam_prev = xam
            P.x0cm_free = [("vector", xm)]

            projw = [("gpsimd", xa2), ("gpsimd", mtb), ("vector", lw),
                     ("gdma", wa_done), ("vector", mt0)]
            pev = 0
            for m_i, (outb, fn_a) in enumerate((("r", AF.Sigmoid), ("ek", AF.Exp),
                                                ("v", AF.Identity), ("td", AF.Exp))):
                pp = (PP0, PP1, PP2, PP3)[m_i]
                pn = ("PP0", "PP1", "PP2", "PP3")[m_i]
                for j2 in range(RND):
                    mmk = 0
                    for jj in range(SL):
                        j = SL * j2 + jj
                        for i in range(G):
                            w = projw if (m_i == 0 and j2 == 0 and jj == 0 and i == 0) else ()
                            if i == 0 and jj == 0 and j2 > 0:
                                w = [("scalar", pev)]
                            mmk = to(lambda e, m_i=m_i, i=i, j=j, jj=jj, pp=pp: e.matmul(
                                pp[:, jj * TC:jj * TC + TC],
                                WA[:, (m_i * 8 + i) * 1024 + j * 128:(m_i * 8 + i) * 1024 + (j + 1) * 128],
                                bb["xa"][:, i * TC:(i + 1) * TC],
                                start=(i == 0), stop=False), waits=w, wb=pn)
                        mmk = to(lambda e, m_i=m_i, j=j, jj=jj, pp=pp: e.matmul(
                            pp[:, jj * TC:jj * TC + TC],
                            AUGW[0:1, (2 * m_i) * 1024 + j * 128:(2 * m_i) * 1024 + (j + 1) * 128],
                            rowb(0, off=1), start=False, stop=False))
                        mmk = to(lambda e, m_i=m_i, j=j, jj=jj, pp=pp: e.matmul(
                            pp[:, jj * TC:jj * TC + TC],
                            AUGW[0:1, (2 * m_i + 1) * 1024 + j * 128:(2 * m_i + 1) * 1024 + (j + 1) * 128],
                            rowb(0, off=0), start=False, stop=True))
                    for jj in range(SL):
                        j = SL * j2 + jj
                        w = [("tensor", mmk)]
                        if m_i == 0 and j2 == 0 and jj == 0:
                            w.append(("gpsimd", P.cm_prev))
                        pev = ao(lambda e, outb=outb, fn_a=fn_a, j=j, jj=jj, pp=pp: e.activation(
                            out=fb[outb][:, j * TC:(j + 1) * TC],
                            in_=pp[:, jj * TC:jj * TC + TC], func=fn_a),
                            waits=w, rb=pn)
            pdone = pev
            P.last_proj_t = P.mark("tensor")

            ekm = 0
            for i in range(G):
                ekm = vo(lambda e, i=i: e.tensor_scalar(
                    out=fb["ek"][:, i * TC:(i + 1) * TC],
                    in0=fb["ek"][:, i * TC:(i + 1) * TC],
                    scalar1=E10, scalar2=None, op0=ALU.min, op1=ALU.bypass),
                    waits=[("scalar", pdone)] if i == 0 else (),
                    bare=(0 < i < G - 1))
            c5buf = (fb["s2"], C5B)[c % 2]
            ekv = 0
            for i in range(G):
                ekv = po(lambda e, i=i, c5buf=c5buf: e.tensor_tensor(
                    out=c5buf[:, i * TC:(i + 1) * TC],
                    in0=fb["ek"][:, i * TC:(i + 1) * TC],
                    in1=fb["v"][:, i * TC:(i + 1) * TC], op=ALU.mult),
                    waits=[("vector", ekm), ("scalar", pdone)] if i == 0 else (),
                    bare=(0 < i < G - 1))
            a_cur = fb["aA"]
            b_cur = fb["bA"]
            scm = 0
            for i in range(G):
                scm = vo(lambda e, i=i, c5buf=c5buf: e.tensor_tensor_scan(
                    out=fb["aA"][:, i * TC:(i + 1) * TC],
                    data0=fb["td"][:, i * TC:(i + 1) * TC],
                    data1=c5buf[:, i * TC:(i + 1) * TC],
                    initial=CARY[:, i:i + 1], op0=ALU.mult, op1=ALU.add),
                    waits=[("gpsimd", ekv)] if i == 0 else (), bare=(i > 0))
            for i in range(G):
                scm = vo(lambda e, i=i: e.tensor_tensor_scan(
                    out=fb["bA"][:, i * TC:(i + 1) * TC],
                    data0=fb["td"][:, i * TC:(i + 1) * TC],
                    data1=fb["ek"][:, i * TC:(i + 1) * TC],
                    initial=CARY[:, 8 + i:9 + i], op0=ALU.mult, op1=ALU.add),
                    bare=(i < G - 1))
            acar = po(lambda e: e.tensor_copy(
                out=CARY[:, 0:8], in_=fb["aA"][:, TC - 1:TC - 1 + (G - 1) * TC + 1:TC]),
                waits=[("vector", scm)])
            acar = po(lambda e: e.tensor_copy(
                out=CARY[:, 8:16], in_=fb["bA"][:, TC - 1:TC - 1 + (G - 1) * TC + 1:TC]))
            g5 = 0
            for i in range(G):
                g5 = ao(lambda e, i=i: e.activation(
                    out=fb["x1"][:, i * TC:(i + 1) * TC],
                    in_=fb["ek"][:, i * TC:(i + 1) * TC],
                    func=AF.Identity, scale=VGD[:, 8 + i:9 + i]),
                    waits=[("vector", ekm), ("vector", gd5),
                           ("gpsimd", P.cm_prev)] if i == 0 else (),
                    bare=(0 < i < G - 1))
            dn = 0
            for i in range(G):
                dn = vo(lambda e, i=i, b_cur=b_cur: e.scalar_tensor_tensor(
                    out=fb["x2"][:, i * TC:(i + 1) * TC],
                    in0=b_cur[:, i * TC:(i + 1) * TC], scalar=THR,
                    in1=fb["x1"][:, i * TC:(i + 1) * TC],
                    op0=ALU.mult, op1=ALU.add),
                    waits=[("scalar", g5), ("vector", scm)] if i == 0 else (),
                    bare=(i > 0))
            for i in range(G):
                dn = vo(lambda e, i=i: e.tensor_scalar(
                    out=fb["x2"][:, i * TC:(i + 1) * TC],
                    in0=fb["x2"][:, i * TC:(i + 1) * TC],
                    scalar1=THR * EPS_A, scalar2=None, op0=ALU.add, op1=ALU.bypass),
                    bare=(i < G - 1))
            nm = 0
            for i in range(G):
                nm = vo(lambda e, i=i: e.tensor_scalar(
                    out=fb["x1"][:, i * TC:(i + 1) * TC],
                    in0=fb["x1"][:, i * TC:(i + 1) * TC],
                    scalar1=5.0, scalar2=-EPS_A, op0=ALU.mult, op1=ALU.add),
                    bare=(i < G - 1))
            n2 = 0
            for i in range(G):
                n2 = po(lambda e, i=i: e.tensor_tensor(
                    out=fb["x1"][:, i * TC:(i + 1) * TC],
                    in0=fb["x1"][:, i * TC:(i + 1) * TC],
                    in1=fb["v"][:, i * TC:(i + 1) * TC], op=ALU.mult),
                    waits=[("vector", nm)] if i == 0 else (),
                    bare=(0 < i < G - 1))
            n3 = 0
            for i in range(G):
                n3 = po(lambda e, i=i, a_cur=a_cur: e.tensor_tensor(
                    out=fb["x1"][:, i * TC:(i + 1) * TC],
                    in0=fb["x1"][:, i * TC:(i + 1) * TC],
                    in1=a_cur[:, i * TC:(i + 1) * TC], op=ALU.add),
                    bare=(i < G - 1))
            rc = 0
            for i in range(G):
                rc = vo(lambda e, i=i: e.reciprocal(
                    out=fb["s1"][:, i * TC:(i + 1) * TC],
                    in_=fb["x2"][:, i * TC:(i + 1) * TC]),
                    waits=[("gpsimd", n3)] if i == 0 else (),
                    bare=(0 < i < G - 1))
            wkm = 0
            for i in range(G):
                wkm = po(lambda e, i=i: e.tensor_tensor(
                    out=fb["x1"][:, i * TC:(i + 1) * TC],
                    in0=fb["x1"][:, i * TC:(i + 1) * TC],
                    in1=fb["s1"][:, i * TC:(i + 1) * TC], op=ALU.mult),
                    waits=[("vector", rc)] if i == 0 else (),
                    bare=(0 < i < G - 1))
            cm = 0
            for i in range(G):
                cm = po(lambda e, i=i, c5buf=c5buf: e.tensor_tensor(
                    out=c5buf[:, i * TC:(i + 1) * TC],
                    in0=fb["x1"][:, i * TC:(i + 1) * TC],
                    in1=fb["r"][:, i * TC:(i + 1) * TC], op=ALU.mult),
                    bare=(i < G - 1))
            P.cm_prev = cm

            mh_c = fb["mhA"]

            def sl(buf, t, half):
                return buf[:, half * 4 * TC + t:half * 4 * TC + t + 3 * TC + 1:TC]

            for t in range(TC):
                if t == 0:
                    pa = CARY[:, 16:20]
                    pb = CARY[:, 20:24]
                else:
                    pa = sl(mh_c, t - 1, 0)
                    pb = sl(mh_c, t - 1, 1)
                vo(lambda e, t=t, pa=pa, c5buf=c5buf: e.scalar_tensor_tensor(
                    out=sl(fb["s1"], t, 0), in0=pa, scalar=1.0,
                    in1=sl(c5buf, t, 0), op0=ALU.is_gt, op1=ALU.subtract),
                    waits=[("gpsimd", cm)] if t == 0 else (), bare=True)
                vo(lambda e, t=t, pb=pb, c5buf=c5buf: e.scalar_tensor_tensor(
                    out=sl(fb["s1"], t, 1), in0=pb, scalar=1.0,
                    in1=sl(c5buf, t, 1), op0=ALU.is_gt, op1=ALU.subtract),
                    bare=True)
                vo(lambda e, t=t, pa=pa, mh_c=mh_c: e.scalar_tensor_tensor(
                    out=sl(mh_c, t, 0), in0=pa, scalar=BETA,
                    in1=sl(fb["s1"], t, 0), op0=ALU.mult, op1=ALU.subtract),
                    bare=True)
                vo(lambda e, t=t, pb=pb, mh_c=mh_c: e.scalar_tensor_tensor(
                    out=sl(mh_c, t, 1), in0=pb, scalar=BETA,
                    in1=sl(fb["s1"], t, 1), op0=ALU.mult, op1=ALU.subtract),
                    bare=(t < TC - 1))
            scand = P.mark("vector")
            P.scand_prev = scand
            mcar = po(lambda e, mh_c=mh_c: e.tensor_copy(
                out=CARY[:, 16:24],
                in_=mh_c[:, TC - 1:TC - 1 + 7 * TC + 1:TC]),
                waits=[("vector", scand)])
            P.guard("vector")
            spm = vo(lambda e, mh_c=mh_c: e.tensor_single_scalar(
                out=bb["spk"][:, :], in_=mh_c[:, :], scalar=1.0, op=ALU.is_gt),
                waits=[("gdma", spk_stored[-1])] if spk_stored else ())
            sst = P.dma("gpsimd", spk_h[c, :, :], bb["spk"][:, :],
                        waits=[("vector", spm)], no_pend=True)
            spk_stored.append(sst)
            if debug and c == 0:
                dstore(dbg["d_r"], fb["r"][:, :], waits=[("gpsimd", cm)], flush=False)
                dstore(dbg["d_ek"], fb["ek"][:, :], waits=[("gpsimd", cm)], flush=False)
                dstore(dbg["d_v"], fb["v"][:, :], waits=[("gpsimd", cm)], flush=False)
                dstore(dbg["d_td"], fb["td"][:, :], waits=[("gpsimd", cm)], flush=False)
                dstore(dbg["d_a"], a_cur[:, :], waits=[("gpsimd", cm)], flush=False)
                dstore(dbg["d_b"], b_cur[:, :], waits=[("gpsimd", cm)], flush=False)
                dstore(dbg["d_c5"], c5buf[:, :], waits=[("vector", scand)], flush=False)
                dstore(dbg["d_mem"], mh_c[:, :], waits=[("vector", scand)], flush=False)
                dstore(dbg["d_spk"], bb["spk"][:, :], waits=[("vector", spm)])
            if debug and c == 0:
                for eng in ("vector", "scalar", "gpsimd", "tensor"):
                    P.q[eng].append(((lambda e: None), (("store", P.store),), 0, eng))

        ph1_v = P.mark("vector")
        ph1_p = P.mark("gpsimd")
        if stages < 2:
            return _finish()
        P.wsum_v = P.mark("vector")
        P.wsum_t = P.mark("tensor")
        wb_done = 0
        for mi, nm in enumerate(("wq", "wk", "wv", "wo")):
            for i in range(G):
                wb_done = gld(WB[:, (mi * 8 + i) * 1024:(mi * 8 + i + 1) * 1024],
                              w_h[nm][i * 128:(i + 1) * 128, :],
                              waits=[("tensor", P.last_proj_t)] if (mi == 0 and i == 0) else ())
        lw = emit_wsum(WB, 0 * 8192, None, 0, extra_w=[("gdma", wb_done), ("vector", ph1_v)])
        lw = emit_wsum(WB, 1 * 8192, None, 1, extra_w=[("gdma", wb_done)])
        lw = emit_wsum(WB, 2 * 8192, None, 2, extra_w=[("gdma", wb_done)])
        bwe = 0
        for half in range(2):
            bw = 0
            for i in range(G):
                bw = to(lambda e, i=i, half=half: e.matmul(
                    PX[0:1, 0:512], VECB[:, 1 + i:2 + i],
                    WB[:, (3 * 8 + i) * 1024 + half * 512:(3 * 8 + i) * 1024 + half * 512 + 512],
                    start=(i == 0), stop=(i == G - 1)),
                    waits=[("gdma", wb_done), ("vector", lw),
                           ("vector", P.wsum_v)] if (i == 0) else (), wb="PX")
            bwe = vo(lambda e, half=half: e.tensor_copy(
                out=AUGW[0:1, 4 * 1024 + half * 512:4 * 1024 + half * 512 + 512],
                in_=PX[0:1, 0:512]), waits=[("tensor", bw)], rb="PX")
            P.wsum_v = bwe
        gf = 0
        for i in range(G):
            gf = ao(lambda e, i=i: e.activation(
                out=WB[:, (3 * 8 + i) * 1024:(3 * 8 + i + 1) * 1024],
                in_=WB[:, (3 * 8 + i) * 1024:(3 * 8 + i + 1) * 1024],
                func=AF.Identity, scale=VEC[:, 24 + i:25 + i]),
                waits=[("vector", bwe)])
        lw2 = emit_wsum(WB, 3 * 8192, None, 3, extra_w=[("scalar", gf)])
        wbprep_v = P.mark("vector")
        if stages < 3:
            return _finish()

        P.ffn_fo_prev = 0

        def emit_ffn(xin, xout, first_waits, dbg_key=None, c=0):
            mt, rsre = ln_stats(xin, 4, 5, 6, 0, first_waits, bslot=1)
            mre = replicate(row(5, off=1), 2, [("vector", mt),
                                               ("gpsimd", P.ffn_fo_prev)])
            um = 0
            for i in range(G):
                um = vo(lambda e, i=i, xin=xin: e.tensor_tensor(
                    out=fb["s2"][:, i * TC:(i + 1) * TC],
                    in0=xin[:, i * TC:(i + 1) * TC], in1=REP[:, 0:TC],
                    op=ALU.mult), waits=[("vector", rsre)] if i == 0 else ())
            u2 = 0
            for i in range(G):
                u2 = po(lambda e, i=i: e.tensor_tensor(
                    out=fb["s2"][:, i * TC:(i + 1) * TC],
                    in0=fb["s2"][:, i * TC:(i + 1) * TC],
                    in1=REP[:, 2 * TC:3 * TC], op=ALU.add),
                    waits=[("vector", um), ("vector", mre)] if i == 0 else ())
            ab_ = 0
            for i in range(G):
                ab_ = ao(lambda e, i=i: e.activation(
                    out=fb["s1"][:, i * TC:(i + 1) * TC],
                    in_=fb["s2"][:, i * TC:(i + 1) * TC], func=AF.Abs),
                    waits=[("gpsimd", u2)] if i == 0 else ())
            tm = 0
            for step in (4, 2, 1):
                for i in range(step):
                    tm = vo(lambda e, i=i, step=step: e.tensor_tensor(
                        out=fb["s1"][:, i * TC:(i + 1) * TC],
                        in0=fb["s1"][:, i * TC:(i + 1) * TC],
                        in1=fb["s1"][:, (i + step) * TC:(i + step + 1) * TC],
                        op=ALU.max),
                        waits=[("scalar", ab_)] if (step == 4 and i == 0) else ())
                if step > 1:
                    P.guard("vector")
            tt1 = 0
            for s in range(NSUB):
                tt1 = to(lambda e, s=s: e.transpose(
                    PTF[:, s * 128:(s + 1) * 128],
                    fb["s1"][:, s * 128:(s + 1) * 128], IDENT[:, :]),
                    waits=[("vector", tm)] if s == 0 else (), wb="PTF")
            SC = 8 * TC - 8
            sc = 0
            for s in range(NSUB):
                sc = vo(lambda e, s=s: e.tensor_reduce(
                    out=fb["s1"][:, SC + s:SC + s + 1],
                    in_=PTF[:, s * 128:(s + 1) * 128], axis=AXX, op=ALU.max),
                    waits=[("tensor", tt1)] if s == 0 else (), rb="PTF")
            P.guard("vector")
            sc = vo(lambda e: e.tensor_scalar(
                out=fb["s1"][:, SC + 2:SC + 4], in0=fb["s1"][:, SC:SC + 2],
                scalar1=1e-12, scalar2=1.0 / 127.0, op0=ALU.max, op1=ALU.mult))
            P.guard("vector")
            sc = vo(lambda e: e.reciprocal(
                out=fb["s1"][:, SC + 4:SC + 6], in_=fb["s1"][:, SC + 2:SC + 4]))
            P.guard("vector")
            tb = to(lambda e: e.transpose(PTF[0:4, 256:256 + 128],
                                          fb["s1"][:, SC + 2:SC + 6], IDENT[:, :]),
                    waits=[("vector", sc)], wb="PTF")
            r4 = vo(lambda e: e.tensor_copy(out=fb["s1"][0:4, 0:128],
                                            in_=PTF[0:4, 256:256 + 128]),
                    waits=[("tensor", tb)], rb="PTF")
            lsm = to(lambda e: e.matmul(PX[0:1, 0:128], IDENT[0:4, 0:1],
                                        fb["s1"][0:4, 0:128], start=True, stop=True),
                     waits=[("vector", r4)], wb="PX")
            lsm = to(lambda e: e.matmul(PX[0:1, 128:256], IDENT[0:4, 1:2],
                                        fb["s1"][0:4, 0:128], start=True, stop=True))
            lsm = to(lambda e: e.matmul(PX[0:1, 256:384], IDENT[0:4, 2:3],
                                        fb["s1"][0:4, 0:128], start=True, stop=True))
            lsm = to(lambda e: e.matmul(PX[0:1, 384:512], IDENT[0:4, 3:4],
                                        fb["s1"][0:4, 0:128], start=True, stop=True))
            lse = vo(lambda e: e.tensor_copy(out=row(7, n=TC),
                                             in_=PX[0:1, 0:TC]),
                     waits=[("tensor", lsm)], rb="PX")
            ive = vo(lambda e: e.tensor_copy(out=row(8, n=TC),
                                             in_=PX[0:1, 256:256 + TC]), rb="PX")
            ivr = replicate(row(8, n=TC), 2, [("vector", ive)])
            lsr = replicate(row(7, n=TC), 1, [("vector", lse)])
            q1 = 0
            for i in range(G):
                q1 = vo(lambda e, i=i: e.tensor_tensor(
                    out=fb["s2"][:, i * TC:(i + 1) * TC],
                    in0=fb["s2"][:, i * TC:(i + 1) * TC],
                    in1=REP[:, 2 * TC:3 * TC], op=ALU.mult),
                    waits=[("vector", ivr)] if i == 0 else ())
            for i in range(G):
                q1 = vo(lambda e, i=i: e.tensor_scalar(
                    out=fb["s2"][:, i * TC:(i + 1) * TC],
                    in0=fb["s2"][:, i * TC:(i + 1) * TC],
                    scalar1=MAGIC, scalar2=MAGIC, op0=ALU.add, op1=ALU.subtract))
            for i in range(G):
                q1 = vo(lambda e, i=i: e.tensor_scalar(
                    out=bb["q3"][:, i * TC:(i + 1) * TC],
                    in0=fb["s2"][:, i * TC:(i + 1) * TC],
                    scalar1=127.0, scalar2=-127.0, op0=ALU.min, op1=ALU.max))
            fev = 0
            for j2 in range(RND):
                mmk = 0
                for jj in range(SL):
                    j = SL * j2 + jj
                    for i in range(G):
                        w = ()
                        if j2 == 0 and jj == 0 and i == 0:
                            w = [("vector", q1)]
                        elif i == 0 and jj == 0 and j2 > 0:
                            w = [("scalar", fev)]
                        mmk = to(lambda e, i=i, j=j, jj=jj: e.matmul(
                            PP0[:, jj * TC:jj * TC + TC],
                            QR[:, (i * 8 + j) * 128:(i * 8 + j + 1) * 128],
                            bb["q3"][:, i * TC:(i + 1) * TC],
                            start=(i == 0), stop=(i == G - 1)), waits=w, wb="PP0")
                for jj in range(SL):
                    j = SL * j2 + jj
                    fev = ao(lambda e, j=j, jj=jj: e.activation(
                        out=fb["s2"][:, j * TC:(j + 1) * TC],
                        in_=PP0[:, jj * TC:jj * TC + TC],
                        func=AF.Identity, scale=VRQ[:, 8 + j:9 + j]),
                        waits=[("tensor", mmk)], rb="PP0")
            fo = 0
            for i in range(G):
                fo = po(lambda e, i=i: e.tensor_tensor(
                    out=fb["s2"][:, i * TC:(i + 1) * TC],
                    in0=fb["s2"][:, i * TC:(i + 1) * TC],
                    in1=REP[:, TC:2 * TC], op=ALU.mult),
                    waits=[("scalar", fev), ("vector", lsr)] if i == 0 else (),
                    bare=(0 < i < G - 1))
            P.ffn_fo_prev = fo
            if debug and dbg_key and c == 0:
                dstore(dbg[dbg_key], fb["s2"][:, :], waits=[("gpsimd", fo)])
            fo2 = 0
            for i in range(G):
                fo2 = po(lambda e, i=i, xin=xin, xout=xout: e.tensor_tensor(
                    out=xout[:, i * TC:(i + 1) * TC],
                    in0=fb["s2"][:, i * TC:(i + 1) * TC],
                    in1=xin[:, i * TC:(i + 1) * TC], op=ALU.add))
            return fo2

        P.ph2_x1m = 0
        P.ph2_ost = 0
        P.lm_pref = {}
        P.spl_pref = {}
        for c in range(n_chunks):
            x0cm = fb["x0cm"]
            hb = (c % 2) * NSUB * 1024
            lm = P.lm_pref.pop(c, None)
            if lm is None:
                lm = emit_x0_load(c)
            spl = P.spl_pref.pop(c, None)
            if spl is None:
                spl = ld(bb["spk"][:, :], spk_h[c, :, :],
                         waits=[("gdma", spk_stored[-1])] +
                               ([("gpsimd", P.ph2_x1m)] if c > 0 else []))
            ev = 0
            for s in range(NSUB):
                for i in range(G):
                    idx = s * G + i
                    w = []
                    if idx == 0:
                        w = [("sync", spl)]
                        if c == 0:
                            w += [("vector", ph1_v), ("gpsimd", ph1_p)]
                        else:
                            w += [("gpsimd", P.ph2_x1m)]
                    pb = (PTF, PTB)[idx % 2]
                    off = 128 * ((idx // 2) % 4)
                    tm = to(lambda e, s=s, i=i, pb=pb, off=off, hb=hb: e.transpose(
                        pb[:, off:off + 128],
                        XIN[:, hb + s * 1024 + i * 128:hb + s * 1024 + (i + 1) * 128],
                        IDENT[:, :]), waits=w, wb=("PTF", "PTB")[idx % 2])
                    ev = ao(lambda e, s=s, i=i, pb=pb, off=off: e.activation(
                        out=x0cm[:, i * TC + s * 128:i * TC + s * 128 + 128],
                        in_=pb[:, off:off + 128],
                        func=AF.Identity), waits=[("tensor", tm)],
                        rb=("PTF", "PTB")[idx % 2])
            P.x0ev_h[c % 2] = ev
            x1m = 0
            for i in range(G):
                x1m = po(lambda e, i=i: e.tensor_tensor(
                    out=fb["x1"][:, i * TC:(i + 1) * TC],
                    in0=x0cm[:, i * TC:(i + 1) * TC],
                    in1=bb["spk"][:, i * TC:(i + 1) * TC], op=ALU.add),
                    waits=([("scalar", ev), ("sync", spl)] +
                           ([("tensor", P.ph2_ffn2t), ("vector", P.ph2_ffn2v)]
                            if c > 0 else [])) if i == 0 else ())
            P.ph2_x1m = x1m
            if c + 1 < n_chunks:
                P.lm_pref[c + 1] = emit_x0_load(c + 1)
                P.spl_pref[c + 1] = ld(bb["spk"][:, :], spk_h[c + 1, :, :],
                                       waits=[("gpsimd", x1m)])
            if debug and c == 0:
                dstore(dbg["d_x1"], fb["x1"][:, :], waits=[("gpsimd", x1m)])

            P.trigc = emit_trig_pair(c * TC)

            x2m = emit_ffn(fb["x1"], fb["x2"],
                           [("gpsimd", x1m), ("vector", wbprep_v)], "d_f1", c)
            if debug and c == 0:
                dstore(dbg["d_x2"], fb["x2"][:, :], waits=[("gpsimd", x2m)])

            mt2, ln2re = ln_stats(fb["x2"], 9, 10, 11, 0, [("gpsimd", x2m)], bslot=2)
            l2c = 0
            for i in range(G):
                l2c = vo(lambda e, i=i: e.tensor_tensor(
                    out=bb["lnb"][:, i * TC:(i + 1) * TC],
                    in0=fb["x2"][:, i * TC:(i + 1) * TC], in1=REP[:, 0:TC],
                    op=ALU.mult), waits=[("vector", ln2re)] if i == 0 else ())
            if debug and c == 0:
                dstore(dbg["d_ln2"], bb["lnb"][:, :], waits=[("vector", l2c)])

            def proj_b(w_base, fn_a, aug_slot, dstb, pp, pn):
                evp = 0
                for j2 in range(RND):
                    mmk = 0
                    for jj in range(SL):
                        j = SL * j2 + jj
                        for i in range(G):
                            w = ()
                            if j2 == 0 and jj == 0 and i == 0:
                                w = [("vector", l2c), ("vector", mt2),
                                     ("vector", wbprep_v)]
                            elif i == 0 and jj == 0 and j2 > 0:
                                w = [("scalar", evp)]
                            mmk = to(lambda e, i=i, j=j, jj=jj, w_base=w_base, pp=pp: e.matmul(
                                pp[:, jj * TC:jj * TC + TC],
                                WB[:, (w_base + i) * 1024 + j * 128:(w_base + i) * 1024 + (j + 1) * 128],
                                bb["lnb"][:, i * TC:(i + 1) * TC],
                                start=(i == 0), stop=False), waits=w, wb=pn)
                        mmk = to(lambda e, j=j, jj=jj, aug_slot=aug_slot, pp=pp: e.matmul(
                            pp[:, jj * TC:jj * TC + TC],
                            AUGW[0:1, aug_slot * 1024 + j * 128:aug_slot * 1024 + (j + 1) * 128],
                            rowb(2, off=1), start=False, stop=True))
                    for jj in range(SL):
                        j = SL * j2 + jj
                        evp = ao(lambda e, j=j, jj=jj, fn_a=fn_a, dstb=dstb, pp=pp: e.activation(
                            out=dstb[:, j * TC:(j + 1) * TC],
                            in_=pp[:, jj * TC:jj * TC + TC], func=fn_a),
                            waits=[("tensor", mmk)], rb=pn)
                return evp

            qev = proj_b(0, AF.Silu, 0, bb["qr"], PP1, "PP1")
            kev = proj_b(8, AF.Silu, 1, bb["kr"], PP2, "PP2")
            vev = proj_b(16, AF.Identity, 2, bb["xa"], PP3, "PP3")

            g1m = 0
            for i in range(G):
                g1m = to(lambda e, i=i: e.matmul(
                    PX[0:64, 256:256 + TC], WGT[:, i * 64:(i + 1) * 64],
                    bb["lnb"][:, i * TC:(i + 1) * TC],
                    start=(i == 0), stop=False),
                    waits=[("vector", l2c), ("vector", mt2)] if i == 0 else (),
                    wb="PX")
            g1m = to(lambda e: e.matmul(
                PX[0:64, 256:256 + TC], AUGW[0:1, 8 * 1024:8 * 1024 + 64],
                rowb(2, off=1), start=False, stop=True))
            g1e = ao(lambda e: e.activation(out=G1B[:, :], in_=PX[0:64, 256:256 + TC],
                                            func=AF.Identity), waits=[("tensor", g1m)],
                     rb="PX")
            g2m = 0
            for j2 in range(RND):
                for jj in range(SL):
                    j = SL * j2 + jj
                    g2m = to(lambda e, j=j, jj=jj: e.matmul(
                        PP0[:, jj * TC:jj * TC + TC],
                        WGT[0:64, 512 + j * 128:512 + (j + 1) * 128], G1B[:, :],
                        start=True, stop=True),
                        waits=[("scalar", g1e)] if j == 0 else (), wb="PP0")
                for jj in range(SL):
                    j = SL * j2 + jj
                    gev = ao(lambda e, j=j, jj=jj: e.activation(
                        out=fb["gate"][:, j * TC:(j + 1) * TC],
                        in_=PP0[:, jj * TC:jj * TC + TC], func=AF.Sigmoid),
                        waits=[("tensor", g2m)], rb="PP0")
                if j2 < RND - 1:
                    P.q["tensor"].append(((lambda e: None), (("scalar", gev),), 0, "tensor"))
            gated_w = gev
            if debug and c == 0:
                dstore(dbg["d_gate"], fb["gate"][:, :], waits=[("scalar", gated_w)])

            def rope(buf, done):
                rm = 0
                cosr = CS[:, 0:TC]
                sinr = CS[:, TC:2 * TC]
                for g in range(4):
                    rm = vo(lambda e, g=g, buf=buf: e.tensor_tensor(
                        out=bb["esb"][:, (g % 2) * 4 * TC + 0:(g % 2) * 4 * TC + TC],
                        in0=buf[:, g * TC:(g + 1) * TC], in1=cosr, op=ALU.mult),
                        waits=[("scalar", done), ("scalar", P.trigc)] if g == 0 else (),
                        bare=(g > 0))
                    rm = vo(lambda e, g=g, buf=buf: e.tensor_tensor(
                        out=bb["esb"][:, (g % 2) * 4 * TC + TC:(g % 2) * 4 * TC + 2 * TC],
                        in0=buf[:, (4 + g) * TC:(5 + g) * TC], in1=sinr, op=ALU.mult),
                        bare=True)
                    rm = vo(lambda e, g=g, buf=buf: e.tensor_tensor(
                        out=bb["esb"][:, (g % 2) * 4 * TC + 2 * TC:(g % 2) * 4 * TC + 3 * TC],
                        in0=buf[:, g * TC:(g + 1) * TC], in1=sinr, op=ALU.mult),
                        bare=True)
                    rm = vo(lambda e, g=g, buf=buf: e.tensor_tensor(
                        out=bb["esb"][:, (g % 2) * 4 * TC + 3 * TC:(g % 2) * 4 * TC + 4 * TC],
                        in0=buf[:, (4 + g) * TC:(5 + g) * TC], in1=cosr, op=ALU.mult),
                        bare=True)
                    rm = vo(lambda e, g=g, buf=buf: e.tensor_tensor(
                        out=buf[:, g * TC:(g + 1) * TC],
                        in0=bb["esb"][:, (g % 2) * 4 * TC + 0:(g % 2) * 4 * TC + TC],
                        in1=bb["esb"][:, (g % 2) * 4 * TC + TC:(g % 2) * 4 * TC + 2 * TC],
                        op=ALU.subtract), bare=True)
                    rm = vo(lambda e, g=g, buf=buf: e.tensor_tensor(
                        out=buf[:, (4 + g) * TC:(5 + g) * TC],
                        in0=bb["esb"][:, (g % 2) * 4 * TC + 2 * TC:(g % 2) * 4 * TC + 3 * TC],
                        in1=bb["esb"][:, (g % 2) * 4 * TC + 3 * TC:(g % 2) * 4 * TC + 4 * TC],
                        op=ALU.add), bare=(g < 3))
                return rm
            qrm = rope(bb["qr"], qev)
            krm = rope(bb["kr"], kev)
            if debug and c == 0:
                dstore(dbg["d_qr"], bb["qr"][:, :], waits=[("vector", qrm)])
                dstore(dbg["d_kr"], bb["kr"][:, :], waits=[("vector", krm)])

            for s in range(NSUB):
                tev = 0
                for i in range(G):
                    w = [("vector", krm)] if i == 0 else []
                    if i >= 2:
                        w.append(("scalar", tev - 1))
                    tm = to(lambda e, i=i, s=s: e.matmul(
                        PTB[:, (i % 2) * 128:(i % 2) * 128 + 128],
                        bb["kr"][:, i * TC + s * 128:i * TC + s * 128 + 128],
                        IDENTB[:, :], start=True, stop=True), waits=w, wb="PTB")
                    tev = ao(lambda e, i=i, s=s: e.activation(
                        out=bb["ktok"][:, s * 1024 + i * 128:s * 1024 + (i + 1) * 128],
                        in_=PTB[:, (i % 2) * 128:(i % 2) * 128 + 128],
                        func=AF.Identity), waits=[("tensor", tm)], rb="PTB")
                for i in range(G):
                    w = [("scalar", vev)] if i == 0 else []
                    if i >= 2:
                        w.append(("scalar", tev - 1))
                    tm = to(lambda e, i=i, s=s: e.matmul(
                        PTB[:, 256 + (i % 2) * 128:256 + (i % 2) * 128 + 128],
                        bb["xa"][:, i * TC + s * 128:i * TC + s * 128 + 128],
                        IDENTB[:, :], start=True, stop=True), waits=w, wb="PTB")
                    tev = ao(lambda e, i=i, s=s: e.activation(
                        out=bb["vtok"][:, s * 1024 + i * 128:s * 1024 + (i + 1) * 128],
                        in_=PTB[:, 256 + (i % 2) * 128:256 + (i % 2) * 128 + 128],
                        func=AF.Identity), waits=[("tensor", tm)], rb="PTB")
                first_sub = (c == 0 and s == 0)
                if not first_sub:
                    scp = vo(lambda e: e.tensor_copy(out=SSB[:, :], in_=SACC[:, :]),
                             waits=[("vector", P.sacc_v)])
                for h in range(H):
                    glo, pos = h // 4, 32 * (h % 4)
                    eslot = h % 4
                    w = []
                    if h == 0:
                        w += [("vector", qrm), ("scalar", tev)]
                    if h >= 4:
                        w.append(("vector", P.e_evac - 3))
                    em = to(lambda e, h=h, s=s, glo=glo, pos=pos, eslot=eslot: e.matmul(
                        PP1[:, eslot * 128:(eslot + 1) * 128],
                        bb["kr"][pos:pos + 32, glo * TC + s * 128:glo * TC + s * 128 + 128],
                        bb["qr"][pos:pos + 32, glo * TC + s * 128:glo * TC + s * 128 + 128],
                        start=True, stop=False,
                        tile_position=(pos, 0)), waits=w, wb="PP1")
                    em = to(lambda e, h=h, s=s, glo=glo, pos=pos, eslot=eslot: e.matmul(
                        PP1[:, eslot * 128:(eslot + 1) * 128],
                        bb["kr"][pos:pos + 32, (4 + glo) * TC + s * 128:(4 + glo) * TC + s * 128 + 128],
                        bb["qr"][pos:pos + 32, (4 + glo) * TC + s * 128:(4 + glo) * TC + s * 128 + 128],
                        start=False, stop=True, tile_position=(pos, 0)))
                    P.e_evac = vo(lambda e, h=h, eslot=eslot: e.tensor_tensor(
                        out=bb["esb"][:, eslot * 128:(eslot + 1) * 128],
                        in0=PP1[:, eslot * 128:(eslot + 1) * 128],
                        in1=TRI[:, :], op=ALU.mult), waits=[("tensor", em)],
                        rb="PP1")
                    oslot = (h // 2) % 4
                    obase = 64 * (h % 2)
                    evw = [("vector", P.e_evac)]
                    if h >= 8:
                        evw.append(("vector", P.o_evac))
                    evm = to(lambda e, h=h, s=s, oslot=oslot, obase=obase, fs=first_sub: e.matmul(
                        PP3[obase:obase + 64, oslot * 128:(oslot + 1) * 128],
                        bb["vtok"][:, s * 1024 + h * 64:s * 1024 + (h + 1) * 64],
                        bb["esb"][:, (h % 4) * 128:((h % 4) + 1) * 128],
                        start=True, stop=fs,
                        tile_position=(0, obase)), waits=evw, wb="PP3")
                    if not first_sub:
                        evm = to(lambda e, h=h, s=s, glo=glo, pos=pos, oslot=oslot, obase=obase: e.matmul(
                            PP3[obase:obase + 64, oslot * 128:(oslot + 1) * 128],
                            SSB[pos:pos + 32, 64 * glo:64 * glo + 64],
                            bb["qr"][pos:pos + 32, glo * TC + s * 128:glo * TC + s * 128 + 128],
                            start=False, stop=False, tile_position=(pos, obase)),
                            waits=[("vector", scp)] if h == 0 else ())
                        evm = to(lambda e, h=h, s=s, glo=glo, pos=pos, oslot=oslot, obase=obase: e.matmul(
                            PP3[obase:obase + 64, oslot * 128:(oslot + 1) * 128],
                            SSB[pos:pos + 32, 256 + 64 * glo:256 + 64 * glo + 64],
                            bb["qr"][pos:pos + 32, (4 + glo) * TC + s * 128:(4 + glo) * TC + s * 128 + 128],
                            start=False, stop=True, tile_position=(pos, obase)))
                    supw = [("scalar", tev)] if h == 0 else []
                    if h == 0:
                        supw.append(("vector", P.sacc_v))
                    sup = to(lambda e, h=h, s=s, glo=glo, pos=pos: e.matmul(
                        PS[pos:pos + 32, 64 * glo:64 * glo + 64],
                        bb["ktok"][:, s * 1024 + glo * 128 + pos:s * 1024 + glo * 128 + pos + 32],
                        bb["vtok"][:, s * 1024 + h * 64:s * 1024 + (h + 1) * 64],
                        start=True, stop=True, tile_position=(0, pos)), waits=supw,
                        wb="PS")
                    sup = to(lambda e, h=h, s=s, glo=glo, pos=pos: e.matmul(
                        PS[pos:pos + 32, 256 + 64 * glo:256 + 64 * glo + 64],
                        bb["ktok"][:, s * 1024 + (4 + glo) * 128 + pos:s * 1024 + (4 + glo) * 128 + pos + 32],
                        bb["vtok"][:, s * 1024 + h * 64:s * 1024 + (h + 1) * 64],
                        start=True, stop=True, tile_position=(0, pos)), wb="PS")
                    if h % 2 == 1 and h % 8 == 7:
                        for hp in range(h - 7, h + 1, 2):
                            os2 = (hp // 2) % 4
                            P.o_evac = vo(lambda e, hp=hp, os2=os2, s=s: e.tensor_tensor(
                                out=fb["s2"][:, (hp // 2) * TC + s * 128:(hp // 2) * TC + s * 128 + 128],
                                in0=PP3[:, os2 * 128:(os2 + 1) * 128],
                                in1=fb["gate"][:, (hp // 2) * TC + s * 128:(hp // 2) * TC + s * 128 + 128],
                                op=ALU.mult),
                                waits=[("tensor", sup), ("scalar", gated_w)],
                                rb="PP3")
                P.s_upd = P.mark("tensor")
                P.sacc_v = vo(lambda e: e.tensor_tensor(
                    out=SACC[:, :], in0=SACC[:, :], in1=PS[:, :], op=ALU.add),
                    waits=[("tensor", P.s_upd)], rb="PS")
            ao_done = P.o_evac
            if debug and c == 0:
                dstore(dbg["d_ao"], fb["s2"][:, :], waits=[("vector", ao_done)])

            mt3, re3 = ln_stats(fb["s2"], 12, 13, 14, 0, [("vector", ao_done)],
                                sqb=fb["s1"], bslot=3)
            c3 = 0
            for i in range(G):
                c3 = vo(lambda e, i=i: e.tensor_tensor(
                    out=bb["lnb"][:, i * TC:(i + 1) * TC],
                    in0=fb["s2"][:, i * TC:(i + 1) * TC], in1=REP[:, 0:TC],
                    op=ALU.mult), waits=[("vector", re3)] if i == 0 else ())
            wev = 0
            for j2 in range(RND):
                mmk = 0
                for jj in range(SL):
                    j = SL * j2 + jj
                    for i in range(G):
                        w = ()
                        if j2 == 0 and jj == 0 and i == 0:
                            w = [("vector", c3), ("vector", mt3)]
                        elif i == 0 and jj == 0 and j2 > 0:
                            w = [("vector", wev)]
                        mmk = to(lambda e, i=i, j=j, jj=jj: e.matmul(
                            PP2[:, jj * TC:jj * TC + TC],
                            WB[:, (3 * 8 + i) * 1024 + j * 128:(3 * 8 + i) * 1024 + (j + 1) * 128],
                            bb["lnb"][:, i * TC:(i + 1) * TC],
                            start=(i == 0), stop=False), waits=w, wb="PP2")
                    mmk = to(lambda e, j=j, jj=jj: e.matmul(
                        PP2[:, jj * TC:jj * TC + TC],
                        AUGW[0:1, 3 * 1024 + j * 128:3 * 1024 + (j + 1) * 128],
                        rowb(3, off=1), start=False, stop=False))
                    mmk = to(lambda e, j=j, jj=jj: e.matmul(
                        PP2[:, jj * TC:jj * TC + TC],
                        AUGW[0:1, 4 * 1024 + j * 128:4 * 1024 + (j + 1) * 128],
                        rowb(4, off=1), start=False, stop=True))
                for jj in range(SL):
                    j = SL * j2 + jj
                    wev = vo(lambda e, j=j, jj=jj: e.tensor_tensor(
                        out=fb["x1"][:, j * TC:(j + 1) * TC],
                        in0=PP2[:, jj * TC:jj * TC + TC],
                        in1=fb["x2"][:, j * TC:(j + 1) * TC], op=ALU.add),
                        waits=[("tensor", mmk)], rb="PP2")
            if debug and c == 0:
                dstore(dbg["d_x3"], fb["x1"][:, :], waits=[("vector", wev)])

            x4m = emit_ffn(fb["x1"], fb["x2"], [("vector", wev)], None, c)

            P.ph2_ffn2v = P.mark("vector")
            oev = 0
            for s in range(NSUB):
                for i in range(G):
                    idx = s * G + i
                    w = [("gpsimd", x4m)] if idx == 0 else []
                    if idx == 0 and c > 0:
                        w.append(("store", P.ph2_ost))
                    pb = (PTF, PTB)[idx % 2]
                    off = 128 * ((idx // 2) % 4)
                    tm = to(lambda e, s=s, i=i, pb=pb, off=off: e.transpose(
                        pb[:, off:off + 128],
                        fb["x2"][:, i * TC + s * 128:i * TC + s * 128 + 128],
                        IDENT[:, :]), waits=w, wb=("PTF", "PTB")[idx % 2])
                    oev = ao(lambda e, s=s, i=i, pb=pb, off=off: e.activation(
                        out=X0TOK[:, s * 1024 + i * 128:s * 1024 + (i + 1) * 128],
                        in_=pb[:, off:off + 128],
                        func=AF.Identity), waits=[("tensor", tm)],
                        rb=("PTF", "PTB")[idx % 2])
            ost = 0
            for s in range(NSUB):
                ost = P.store_dma(out_h[c * TC + s * 128:c * TC + (s + 1) * 128, :],
                                  X0TOK[:, s * 1024:(s + 1) * 1024],
                                  waits=[("scalar", oev)])
            P.ph2_ost = ost
            P.ph2_ffn2t = P.mark("tensor")

        return _finish()

    return nc


def _perm():
    p = np.zeros(D, dtype=np.int64)
    for h in range(H):
        for j in range(32):
            p[32 * h + j] = 64 * h + j
            p[512 + 32 * h + j] = 64 * h + 32 + j
    return p


def _vec_cols(inputs):
    v = np.zeros((128, 48), dtype=np.float32)

    def col8(x):
        return np.ascontiguousarray(x.reshape(8, 128).T)
    mix = np.asarray(inputs["L0_attn_a_mix"], np.float32)
    v[:, 0:8] = col8(mix)
    v[:, 8:16] = col8(np.float32(1.0) - mix)
    v[:, 16:24] = col8(np.asarray(inputs["L0_attn_a_global_decay"], np.float32))
    v[:, 24:32] = col8(np.asarray(inputs["L0_attn_b_gamma"], np.float32))
    v[:, 32:40] = col8(np.asarray(inputs["L0_attn_b_beta"], np.float32))
    v[:, 40] = 1.0
    v[:, 41] = 1.0 / D
    v[:, 42] = (np.arange(128) % 32).astype(np.float32)
    return v


def kernel(**inputs):
    n_chunks = T // TC
    nc = build_program(n_chunks, debug=False)
    perm = _perm()
    tC = lambda a: np.ascontiguousarray(np.asarray(a, dtype=np.float32))
    base = {
        "wrt": tC(np.asarray(inputs["L0_attn_a_W_R"]).T),
        "wkt": tC(np.asarray(inputs["L0_attn_a_W_K"]).T),
        "wvt": tC(np.asarray(inputs["L0_attn_a_W_V"]).T),
        "wtt": tC(np.asarray(inputs["L0_attn_a_W_td"]).T),
        "wq": tC(np.asarray(inputs["L0_attn_b_Wq"])[:, perm]),
        "wk": tC(np.asarray(inputs["L0_attn_b_Wk"])[:, perm]),
        "wv": tC(inputs["L0_attn_b_Wv"]),
        "wo": tC(inputs["L0_attn_b_Wo"]),
        "wg1": tC(inputs["L0_attn_b_Wg1"]),
        "wg2": tC(inputs["L0_attn_b_Wg2"]),
        "rhsT": tC(np.asarray(inputs["L0_ffn_rhs"]).T),
        "vecs": _vec_cols(inputs),
        "ident": np.eye(128, dtype=np.float32),
        "trimask": np.tril(np.ones((128, 128), np.float32)).T.copy(),
    }
    emb = np.asarray(inputs["L0_attn_a_embeddings"], np.float32)
    in_maps = []
    for core in range(8):
        m = dict(base)
        m["x0"] = np.ascontiguousarray(emb[core % B])
        in_maps.append(m)
    res = run_bass_kernel_spmd(nc, in_maps, core_ids=list(range(8)))
    out = np.stack([np.asarray(res.results[b]["out"]) for b in range(B)], axis=0)
    return out.astype(np.float32)


# version 07
# speedup vs baseline: 1.1119x; 1.1095x over previous
import math
import numpy as np
import ml_dtypes
import concourse.bass as bass
import concourse.mybir as mybir
from concourse.bass_utils import run_bass_kernel_spmd

F32 = mybir.dt.float32
BF16 = mybir.dt.bfloat16
AF = mybir.ActivationFunctionType
ALU = mybir.AluOpType
AXX = mybir.AxisListType.X

B, T, D = 4, 4096, 1024
G = 8
H, HD = 16, 64
TC = 128
NSUB = TC // 128
SL = 512 // TC
RND = G // SL
EPS_LN = 1e-5
EPS_A = 1e-6
BETA = 0.3
THR = 0.2
MAGIC = 12582912.0
E10 = float(np.exp(np.float32(10.0)))
LN1E4 = float(np.log(10000.0))
TWO_PI = 2.0 * math.pi
INV_2PI = float(1.0 / TWO_PI)
CW1 = float(np.float32(6.28125))
CW2 = float(np.float32(1.9353628e-3))
CW3 = float(TWO_PI - (np.float64(np.float32(CW1)) + np.float64(np.float32(CW2))))

ENGS = ("sync", "tensor", "vector", "scalar", "gpsimd")


class Plan:

    AUTOSYNC = ("vector", "scalar", "gpsimd")

    def __init__(self):
        self.q = {e: [] for e in ENGS}
        self.cnt = {e: 0 for e in ENGS}
        self.store = 0
        self.pending = {e: 0 for e in ENGS}
        self.pend_dma = {e: 0 for e in ENGS}
        self.prev2 = {e: [0, 0] for e in ENGS}
        self.cnt["gdma"] = 0
        self.bank_read = {}

    def wbank(self, bank):
        br = self.bank_read.get(bank)
        return [br] if br else []

    def rbank(self, bank, eng, mark):
        self.bank_read[bank] = (eng, mark)
        return mark

    def op(self, eng, fn, waits=(), inc=1, bare=False):
        waits = list(waits)
        if bare:
            if self.pend_dma.get(eng):
                waits.append(("gdma", self.pend_dma[eng]))
                self.pend_dma[eng] = 0
            self.q[eng].append((fn, tuple(w for w in waits if w[1] > 0), 0, eng))
            return self.cnt[eng]
        if eng in self.AUTOSYNC:
            dep = max(self.prev2[eng][0], self.pending[eng])
            self.pending[eng] = 0
            if dep > 0:
                waits.append((eng, dep))
            if self.pend_dma.get(eng):
                waits.append(("gdma", self.pend_dma[eng]))
                self.pend_dma[eng] = 0
        self.q[eng].append((fn, tuple(w for w in waits if w[1] > 0), inc, eng))
        self.cnt[eng] += inc
        if inc:
            self.prev2[eng] = [self.prev2[eng][1], self.cnt[eng]]
        return self.cnt[eng]

    def dma(self, eng, out, in_, waits=(), no_pend=False):
        def fn(e):
            return e.dma_start(out=out, in_=in_)
        key = "gdma" if eng == "gpsimd" else eng
        self.q[eng].append((fn, tuple(w for w in waits if w[1] > 0), 16, key))
        self.cnt[key] += 16
        if eng in self.AUTOSYNC and not no_pend:
            self.pend_dma[eng] = self.cnt[key]
        return self.cnt[key]

    def store_dma(self, out, in_, waits=()):
        def fn(e):
            return e.dma_start(out=out, in_=in_)
        self.q["sync"].append((fn, tuple(w for w in waits if w[1] > 0), 16, "store"))
        self.store += 16
        return self.store

    def fence(self, eng):
        val = self.cnt[eng]
        self.q[eng].append(((lambda e: None), ((eng, val),), 0, eng))

    def guard(self, eng, n=2):
        self.pending[eng] = self.cnt[eng]

    def mark(self, eng):
        return self.cnt[eng]


def build_program(n_chunks, debug=False, bass_debug=False, stages=3):
    TT = n_chunks * TC
    AJ = TC + 1
    nc = bass.Bass("TRN2", target_bir_lowering=False, debug=bass_debug)

    x0_h = nc.dram_tensor("x0", [TT, D], F32, kind="ExternalInput").ap()
    w_h = {}
    for nm in ("wrt", "wkt", "wvt", "wtt", "wq", "wk", "wv", "wo"):
        w_h[nm] = nc.dram_tensor(nm, [D, D], F32, kind="ExternalInput").ap()
    wg1_h = nc.dram_tensor("wg1", [D, HD], F32, kind="ExternalInput").ap()
    wg2_h = nc.dram_tensor("wg2", [HD, D], F32, kind="ExternalInput").ap()
    rhsT_h = nc.dram_tensor("rhsT", [D, D], F32, kind="ExternalInput").ap()
    vecs_h = nc.dram_tensor("vecs", [128, 48], F32, kind="ExternalInput").ap()
    ident_h = nc.dram_tensor("ident", [128, 128], F32, kind="ExternalInput").ap()
    tri_h = nc.dram_tensor("trimask", [128, 128], F32, kind="ExternalInput").ap()
    spk_h = nc.dram_tensor("spk_scr", [n_chunks, 128, G * TC], BF16, kind="Internal").ap()
    out_h = nc.dram_tensor("out", [TT, D], F32, kind="ExternalOutput").ap()

    dbg = {}
    if debug:
        for name, dt in [("d_r", F32), ("d_ek", F32), ("d_v", F32), ("d_td", F32),
                         ("d_a", F32), ("d_b", F32), ("d_c5", F32), ("d_mem", F32),
                         ("d_spk", BF16), ("d_x1", F32), ("d_f1", F32), ("d_x2", F32),
                         ("d_qr", BF16), ("d_kr", BF16), ("d_ao", F32),
                         ("d_gate", F32), ("d_x3", F32), ("d_ln2", BF16)]:
            dbg[name] = nc.dram_tensor(name, [128, G * TC], dt, kind="ExternalOutput").ap()

    P = Plan()

    import contextlib
    with contextlib.ExitStack() as stk:
        E = stk.enter_context

        WA = E(nc.sbuf_tensor("WA", [128, 4 * 8 * 1024], BF16))
        WB = WA
        QR = E(nc.sbuf_tensor("QR", [128, 8 * 1024], BF16))
        WGT = E(nc.sbuf_tensor("WGT", [128, 8 * 64 + 1024], BF16))
        AUGW = E(nc.sbuf_tensor("AUGW", [1, 9 * 1024], BF16))
        ROWB = E(nc.sbuf_tensor("ROWB", [1, 5 * (TC + 1)], BF16))
        CARY = E(nc.sbuf_tensor("CARY", [128, 24], F32))
        CS = E(nc.sbuf_tensor("CS", [128, 2 * TC], BF16))
        VEC = E(nc.sbuf_tensor("VEC", [128, 96], F32))
        VECB = E(nc.sbuf_tensor("VECB", [128, 32], BF16))
        VGD = E(nc.sbuf_tensor("VGD", [128, 16], F32))
        VRQ = E(nc.sbuf_tensor("VRQ", [128, 24], F32))
        VIF = E(nc.sbuf_tensor("VIF", [128, 1], F32))
        ONER = E(nc.sbuf_tensor("ONER", [1, 128], F32))
        IDENT = E(nc.sbuf_tensor("IDENT", [128, 128], F32))
        IDENTB = E(nc.sbuf_tensor("IDENTB", [128, 128], BF16))
        TRI = E(nc.sbuf_tensor("TRI", [128, 128], F32))
        ROWS = E(nc.sbuf_tensor("ROWS", [1, 15 * AJ], F32))
        X0TOK = E(nc.sbuf_tensor("x0tok", [128, NSUB * 1024], F32))
        XIN = E(nc.sbuf_tensor("xin", [128, 2 * NSUB * 1024], F32))
        REP = E(nc.sbuf_tensor("REP", [128, 3 * TC], F32))
        NB = G * TC
        fb = {}
        for nm in ("x0cm", "r", "ek", "v", "td", "aA", "bA",
                   "s1", "s2", "mhA", "x1", "x2"):
            fb[nm] = E(nc.sbuf_tensor("f_" + nm, [128, NB], F32))
        fb["gate"] = fb["r"]
        bb = {}
        for nm in ("xa", "spk", "lnb", "qr", "kr", "ktok",
                   "vtok", "esb", "q3"):
            bb[nm] = E(nc.sbuf_tensor("b_" + nm, [128, NB], BF16))
        XRSF = E(nc.sbuf_tensor("f_xrsA", [128, G * (TC + 1)], F32))
        SQB = E(nc.sbuf_tensor("f_sqb", [128, G * TC], F32))
        C5B = E(nc.sbuf_tensor("f_c5b", [128, G * TC], F32))
        SSB = E(nc.sbuf_tensor("ssb", [128, 512], BF16))
        SACC = E(nc.sbuf_tensor("sacc", [128, 512], F32))
        G1B = E(nc.sbuf_tensor("g1b", [64, TC], BF16))

        PP0 = E(nc.psum_tensor("PP0", [128, 512], F32))
        PP1 = E(nc.psum_tensor("PP1", [128, 512], F32))
        PP2 = E(nc.psum_tensor("PP2", [128, 512], F32))
        PP3 = E(nc.psum_tensor("PP3", [128, 512], F32))
        PTF = E(nc.psum_tensor("PTF", [128, 512], F32))
        PTB = E(nc.psum_tensor("PTB", [128, 512], F32))
        PX = E(nc.psum_tensor("PX", [128, 512], F32))
        PS = E(nc.psum_tensor("PS", [128, 512], F32))

        sems = {e: E(nc.semaphore("s_" + e)) for e in ENGS}
        sems["gdma"] = E(nc.semaphore("s_gdma"))
        s_store = E(nc.semaphore("s_st"))
        block = E(nc.Block())


        def _finish():
            P.q["gpsimd"].append(((lambda e: e.wait_ge(s_store, P.store)), (), 0, "gpsimd"))
            def replay(name):
                def run(eng):
                    wmax = {}
                    for fn, waits, inc, semkey in P.q[name]:
                        for weng, val in waits:
                            if wmax.get(weng, -1) >= val:
                                continue
                            wmax[weng] = val
                            sem = s_store if weng == "store" else sems[weng]
                            eng.wait_ge(sem, val)
                        inst = fn(eng)
                        if inc and inst is not None:
                            inst.then_inc(s_store if semkey == "store" else sems[semkey], inc)
                return run
            block.sync(replay("sync"))
            block.tensor(replay("tensor"))
            block.vector(replay("vector"))
            block.scalar(replay("scalar"))
            block.gpsimd(replay("gpsimd"))
            return nc
        def vo(fn, waits=(), inc=1, rb=None, bare=False):
            m = P.op("vector", fn, waits, inc, bare=bare)
            if rb:
                P.rbank(rb, "vector", m)
            return m

        def ao(fn, waits=(), inc=1, rb=None, bare=False):
            m = P.op("scalar", fn, waits, inc, bare=bare)
            if rb:
                P.rbank(rb, "scalar", m)
            return m

        def po(fn, waits=(), inc=1, rb=None, bare=False):
            m = P.op("gpsimd", fn, waits, inc, bare=bare)
            if rb:
                P.rbank(rb, "gpsimd", m)
            return m

        def to(fn, waits=(), inc=1, wb=None):
            w = list(waits)
            if wb:
                w += P.wbank(wb)
            return P.op("tensor", fn, w, inc)
        ld = lambda out, in_, waits=(): P.dma("sync", out, in_, waits)
        gld = lambda out, in_, waits=(): P.dma("gpsimd", out, in_, waits)

        def dstore(out, in_, waits=(), flush=True):
            v = P.store_dma(out, in_, waits)
            if flush:
                for eng in ("vector", "scalar", "gpsimd", "tensor", "sync"):
                    P.q[eng].append(((lambda e: None), (("store", v),), 0, eng))
            return v

        def dflush():
            for eng in ("vector", "scalar", "gpsimd", "tensor", "sync"):
                P.q[eng].append(((lambda e: None), (("store", P.store),), 0, eng))

        def row(slot, n=TC, off=0):
            return ROWS[0:1, slot * AJ + off: slot * AJ + off + n]

        def rowb(slot, n=TC, off=0):
            return ROWB[0:1, slot * AJ + off: slot * AJ + off + n]

        import os as _os
        _kcut = int(_os.environ.get("KCUT", "0"))
        cons_ld = ld(VEC[:, 0:48], vecs_h)
        cons_ld = ld(IDENT[:, :], ident_h)
        cons_ld = ld(TRI[:, :], tri_h)

        idb = vo(lambda e: e.tensor_copy(out=IDENTB[:, :], in_=IDENT[:, :]),
                 waits=[("sync", cons_ld)])
        vo(lambda e: e.tensor_copy(out=VECB[:, 0:1], in_=VEC[:, 40:41]),
           waits=[("sync", cons_ld)])
        vo(lambda e: e.tensor_copy(out=VECB[:, 1:9], in_=VEC[:, 32:40]),
           waits=[("sync", cons_ld)])
        vcb = vo(lambda e: e.tensor_copy(out=VECB[:, 9:25], in_=VEC[:, 0:16]),
                 waits=[("sync", cons_ld)])
        if not (_kcut & 4):
            po(lambda e: e.memset(CARY[:, :], 0.0))
            onr = po(lambda e: e.memset(ONER[0:1, :], 1.0))
            po(lambda e: e.memset(SACC[:, :], 0.0))
            po(lambda e: e.memset(XRSF[:, 0:G * AJ:AJ], 0.0))
            pz = po(lambda e: e.memset(ROWS[0:1, :], 0.0))
            pz = po(lambda e: e.memset(ROWB[0:1, :], 0.0))
            P.guard("gpsimd")
            pz = po(lambda e: e.memset(ROWB[0:1, 4 * AJ + 1:4 * AJ + 1 + TC], 1.0))
        else:
            onr = pz = 0
        if not (_kcut & 2):
            gde = ao(lambda e: e.activation(out=VGD[:, 0:8], in_=VEC[:, 16:24],
                                            func=AF.Exp), waits=[("sync", cons_ld)])
            gd5 = vo(lambda e: e.tensor_scalar(out=VGD[:, 8:16], in0=VGD[:, 0:8],
                                               scalar1=THR, scalar2=None,
                                               op0=ALU.mult, op1=ALU.bypass),
                     waits=[("scalar", gde)])
        else:
            gde = gd5 = 0

        if not (_kcut & 32):
            P.fence("sync")
        _sub = int(_os.environ.get("KSUB", "9"))
        if _sub < 1:
            return _finish()
        _kq = int(_os.environ.get("KQ", "9"))
        tq_ev = 0
        rql = 0
        for i in range(int(_os.environ.get("KQI", str(G)))):
            if not (_kcut & 16):
                rql = P.dma("gpsimd", fb["x1"][:, 0:1024],
                            rhsT_h[i * 128:(i + 1) * 128, :],
                            waits=[("scalar", tq_ev)] if (i > 0 and tq_ev) else (),
                            no_pend=True)
            rq = 0
            if not (_kcut & 8):
                if _kq < 2:
                    continue
                am = vo(lambda e, i=i: e.tensor_reduce(
                    out=VRQ[:, 0 + i:1 + i], in_=fb["x1"][:, 0:1024],
                    axis=AXX, op=ALU.max, apply_absolute_value=True),
                    waits=[("gdma", rql)])
                if _kq < 3:
                    continue
                P.guard("vector")
                am = vo(lambda e, i=i: e.tensor_scalar(
                    out=VRQ[:, 8 + i:9 + i], in0=VRQ[:, 0 + i:1 + i], scalar1=1e-12,
                    scalar2=1.0 / 127.0, op0=ALU.max, op1=ALU.mult))
                P.guard("vector")
                am = vo(lambda e, i=i: e.reciprocal(out=VRQ[:, 16 + i:17 + i],
                                                    in_=VRQ[:, 8 + i:9 + i]))
                if _kq < 4:
                    continue
                qm = ao(lambda e, i=i: e.activation(
                    out=fb["x2"][:, 0:1024], in_=fb["x1"][:, 0:1024],
                    func=AF.Identity, scale=VRQ[:, 16 + i:17 + i]),
                    waits=[("vector", am)])
                if _kq < 5:
                    continue
                rq = vo(lambda e, i=i: e.tensor_scalar(
                    out=fb["x2"][:, 0:1024], in0=fb["x2"][:, 0:1024],
                    scalar1=MAGIC, scalar2=MAGIC, op0=ALU.add, op1=ALU.subtract),
                    waits=[("scalar", qm)])
                P.guard("vector")
                rq = vo(lambda e, i=i: e.tensor_scalar(
                    out=bb["q3"][:, 0:1024], in0=fb["x2"][:, 0:1024],
                    scalar1=127.0, scalar2=-127.0, op0=ALU.min, op1=ALU.max))
                if _kq < 6:
                    continue
            for j in range(int(_os.environ.get("KQT", str(G)))):
                slot = j % 4
                w = []
                if j == 0:
                    if rq:
                        w.append(("vector", rq))
                    w.append(("vector", idb))
                if j >= 4 and tq_ev > 3:
                    w.append(("scalar", tq_ev - 3))
                tqm = to(lambda e, j=j, slot=slot: e.matmul(
                    PTB[:, slot * 128:slot * 128 + 128],
                    bb["q3"][:, j * 128:(j + 1) * 128],
                    IDENTB[:, :], start=True, stop=True), waits=w, wb="PTB")
                tq_ev = ao(lambda e, i=i, j=j, slot=slot: e.activation(
                    out=QR[:, (j * 8 + i) * 128:(j * 8 + i + 1) * 128],
                    in_=PTB[:, slot * 128:slot * 128 + 128], func=AF.Identity),
                    waits=[("tensor", tqm)], rb="PTB")

        wa_done = 0
        if not (_kcut & 1):
            for mi, nm in enumerate(("wrt", "wkt", "wvt", "wtt")):
                for i in range(G):
                    wa_done = gld(WA[:, (mi * 8 + i) * 1024:(mi * 8 + i + 1) * 1024],
                                  w_h[nm][i * 128:(i + 1) * 128, :])
            for i in range(G):
                gld(WGT[:, i * 64:(i + 1) * 64], wg1_h[i * 128:(i + 1) * 128, :])
            wg_done = gld(WGT[0:64, 512:512 + 1024], wg2_h[:, :])
            wa_done = wg_done
        else:
            wg_done = 0

        if _sub < 2:
            return _finish()
        def emit_wsum(w_sb, base, mixcol, slot, ncols=1024, extra_w=()):
            ev = 0
            for half in range(2 if ncols > 512 else 1):
                hw = min(512, ncols)
                mm = 0
                for i in range(G):
                    w = (list(extra_w) + [("sync", cons_ld), ("vector", vcb)]) if i == 0 else []
                    w = list(w) + [("tensor", P.wsum_t)]
                    if i == 0:
                        w.append(("vector", P.wsum_v))
                    lcol = VECB[:, 0:1] if mixcol is None else VECB[:, mixcol + i:mixcol + i + 1]
                    mm = to(lambda e, i=i, half=half, hw=hw, lcol=lcol: e.matmul(
                        PX[0:1, 0:hw], lcol,
                        w_sb[:, base + i * ncols + half * 512:base + i * ncols + half * 512 + hw],
                        start=(i == 0), stop=(i == G - 1)),
                        waits=w, wb="PX")
                    P.wsum_t = mm
                ev = vo(lambda e, half=half, hw=hw: e.tensor_copy(
                    out=AUGW[0:1, slot * 1024 + half * 512:slot * 1024 + half * 512 + hw],
                    in_=PX[0:1, 0:hw]), waits=[("tensor", mm)], rb="PX")
                P.wsum_v = ev
            return ev

        lw = 0
        P.wsum_t = 0
        P.wsum_v = 0
        for m_i in range(4):
            lw = emit_wsum(WA, m_i * 8192, 9, 2 * m_i,
                           extra_w=[("gdma", wa_done)])
            lw = emit_wsum(WA, m_i * 8192, 17, 2 * m_i + 1,
                           extra_w=[("gdma", wa_done)])
        mm = 0
        for i in range(G):
            mm = to(lambda e, i=i: e.matmul(
                PX[0:1, 0:64], VECB[:, 0:1], WGT[:, i * 64:(i + 1) * 64],
                start=(i == 0), stop=(i == G - 1)),
                waits=[("gdma", wg_done), ("vector", lw), ("vector", vcb),
                       ("vector", P.wsum_v)] if i == 0 else (), wb="PX")
        lw = vo(lambda e: e.tensor_copy(out=AUGW[0:1, 8 * 1024:8 * 1024 + 64],
                                        in_=PX[0:1, 0:64]), waits=[("tensor", mm)],
                rb="PX")
        P.wsum_v = lw
        if _sub < 3:
            return _finish()
        invf = ao(lambda e: e.activation(out=VIF[:, 0:1], in_=VEC[:, 42:43],
                                         func=AF.Exp, scale=-LN1E4 / 32.0),
                  waits=[("sync", cons_ld)])

        def emit_trig_pair(tok_base):
            s2 = fb["s2"]
            wn = [s2[:, k * TC:(k + 1) * TC] for k in range(8)]
            pm = po(lambda e: e.iota(wn[0], [[1, TC]], base=tok_base,
                                     channel_multiplier=0,
                                     allow_small_or_imprecise_dtypes=True))
            angm = ao(lambda e: e.activation(out=wn[1], in_=wn[0],
                                             func=AF.Identity,
                                             scale=VIF[:, 0:1]),
                      waits=[("gpsimd", pm), ("scalar", invf)], inc=1)
            marks = []
            for quarter, base_w, dst in ((True, 2, 0), (False, 5, TC)):
                qadd = 0.25 if quarter else 0.0
                wk, wa, wb = wn[base_w], wn[base_w + 1], wn[base_w + 2]
                m = vo(lambda e, wk=wk, qadd=qadd: e.tensor_scalar(
                    out=wk, in0=wn[1], scalar1=INV_2PI, scalar2=qadd,
                    op0=ALU.mult, op1=ALU.add), waits=[("scalar", angm)])
                P.guard("vector")
                m = vo(lambda e, wk=wk: e.tensor_scalar(
                    out=wk, in0=wk, scalar1=MAGIC, scalar2=MAGIC,
                    op0=ALU.add, op1=ALU.subtract))
                P.guard("vector")
                m = vo(lambda e, wk=wk, wa=wa: e.scalar_tensor_tensor(
                    out=wa, in0=wk, scalar=-CW1, in1=wn[1],
                    op0=ALU.mult, op1=ALU.add))
                P.guard("vector")
                m = vo(lambda e, wk=wk, wa=wa, wb=wb: e.scalar_tensor_tensor(
                    out=wb, in0=wk, scalar=-CW2, in1=wa,
                    op0=ALU.mult, op1=ALU.add))
                P.guard("vector")
                m = vo(lambda e, wk=wk, wa=wa, wb=wb: e.scalar_tensor_tensor(
                    out=wa, in0=wk, scalar=-CW3, in1=wb,
                    op0=ALU.mult, op1=ALU.add))
                if quarter:
                    P.guard("vector")
                    m = vo(lambda e, wa=wa: e.tensor_scalar(
                        out=wa, in0=wa, scalar1=math.pi / 2.0, scalar2=None,
                        op0=ALU.add, op1=ALU.bypass))
                marks.append(ao(lambda e, wa=wa, dst=dst: e.activation(
                    out=CS[:, dst:dst + TC], in_=wa, func=AF.Sin),
                    waits=[("vector", m)]))
            return marks[-1]

        def ln_stats(xbuf, slot_m, slot_t, slot_r, rep_slot, first_waits, sqb=None,
                     bslot=5, mt_extra=()):
            sqb = sqb if sqb is not None else fb["s2"]
            mm = 0
            for i in range(G):
                mm = to(lambda e, i=i: e.matmul(
                    PX[0:1, 0:TC], VEC[:, 41:42], xbuf[:, i * TC:(i + 1) * TC],
                    start=(i == 0), stop=(i == G - 1)),
                    waits=first_waits if i == 0 else (), wb="PX")
            sq = 0
            for i in range(G):
                sq = ao(lambda e, i=i, sqb=sqb: e.activation(
                    out=sqb[:, i * TC:(i + 1) * TC],
                    in_=xbuf[:, i * TC:(i + 1) * TC], func=AF.Square),
                    waits=(list(first_waits) + [("tensor", mm)]) if i == 0 else ())
            mm2 = 0
            for i in range(G):
                mm2 = to(lambda e, i=i, sqb=sqb: e.matmul(
                    PX[0:1, 256:256 + TC], VEC[:, 41:42],
                    sqb[:, i * TC:(i + 1) * TC],
                    start=(i == 0), stop=(i == G - 1)),
                    waits=[("scalar", sq)] if i == 0 else (), wb="PX")
            rv = vo(lambda e: e.tensor_copy(out=row(slot_m), in_=PX[0:1, 0:TC]),
                    waits=[("tensor", mm2), ("tensor", mm)], rb="PX")
            P.guard("vector")
            rv = vo(lambda e: e.tensor_tensor(out=row(slot_r), in0=row(slot_m),
                                              in1=row(slot_m), op=ALU.mult))
            P.guard("vector")
            rv = vo(lambda e: e.scalar_tensor_tensor(
                out=row(slot_r), in0=row(slot_r), scalar=-1.0,
                in1=PX[0:1, 256:256 + TC], op0=ALU.mult, op1=ALU.add), rb="PX")
            P.guard("vector")
            rv = vo(lambda e: e.tensor_scalar(
                out=row(slot_r), in0=row(slot_r), scalar1=EPS_LN, scalar2=None,
                op0=ALU.add, op1=ALU.bypass))
            ra = ao(lambda e: e.activation(out=row(slot_r), in_=row(slot_r),
                                           func=AF.Ln), waits=[("vector", rv)])
            P.guard("scalar")
            ra = ao(lambda e: e.activation(out=row(slot_r), in_=row(slot_r),
                                           func=AF.Exp, scale=-0.5))
            mt = vo(lambda e: e.scalar_tensor_tensor(
                out=row(slot_t, off=1), in0=row(slot_m), scalar=-1.0,
                in1=row(slot_r), op0=ALU.mult, op1=ALU.mult),
                waits=[("scalar", ra)])
            P.guard("vector")
            mt = vo(lambda e: e.tensor_copy(out=rowb(bslot, off=1),
                                            in_=row(slot_t, off=1)),
                    waits=list(mt_extra))
            rp = to(lambda e: e.matmul(
                PX[:, 256:256 + TC], ONER[0:1, :], row(slot_r),
                start=True, stop=True),
                waits=[("scalar", ra), ("vector", rv), ("gpsimd", onr),
                       ("vector", P.rep_v)], wb="PX")
            re = vo(lambda e: e.tensor_copy(
                out=REP[:, rep_slot * TC:(rep_slot + 1) * TC],
                in_=PX[:, 256:256 + TC]), waits=[("tensor", rp)], rb="PX")
            P.rep_v = re
            return mt, re

        def replicate(src_row_ap, rep_slot, waits):
            rp = to(lambda e: e.matmul(PX[:, 256:256 + TC], ONER[0:1, :],
                                       src_row_ap, start=True, stop=True),
                    waits=list(waits) + [("gpsimd", onr), ("vector", P.rep_v)],
                    wb="PX")
            re = vo(lambda e: e.tensor_copy(
                out=REP[:, rep_slot * TC:(rep_slot + 1) * TC],
                in_=PX[:, 256:256 + TC]), waits=[("tensor", rp)], rb="PX")
            P.rep_v = re
            return re

        P.fence("sync")
        if stages < 1:
            return _finish()
        spk_stored = []
        P.rep_v = 0
        P.scand_prev = 0
        P.x0ev = 0
        P.last_proj_t = 0
        P.cm_prev = 0
        P.e_evac = 0
        P.o_evac = 0
        P.s_upd = 0
        P.sacc_v = 0
        P.x0cm_free = []
        P.x0ev_h = [0, 0]

        def emit_x0_load(c):
            hb = (c % 2) * NSUB * 1024
            lm = 0
            for s in range(NSUB):
                lm = ld(XIN[:, hb + s * 1024:hb + (s + 1) * 1024],
                        x0_h[c * TC + s * 128:c * TC + (s + 1) * 128, :],
                        waits=[("scalar", P.x0ev_h[c % 2])] if s == 0 else ())
            return lm

        def p1_front(c):
            x0cm = fb["x0cm"]
            hb = (c % 2) * NSUB * 1024
            lm = emit_x0_load(c)
            ev = 0
            for s in range(NSUB):
                for i in range(G):
                    idx = s * G + i
                    w = list(P.x0cm_free) + [("sync", lm)] if idx == 0 else []
                    pb = (PTF, PTB)[idx % 2]
                    off = 128 * ((idx // 2) % 4)
                    tm = to(lambda e, s=s, i=i, pb=pb, off=off, hb=hb: e.transpose(
                        pb[:, off:off + 128],
                        XIN[:, hb + s * 1024 + i * 128:hb + s * 1024 + (i + 1) * 128],
                        IDENT[:, :]), waits=w, wb=("PTF", "PTB")[idx % 2])
                    ev = ao(lambda e, s=s, i=i, pb=pb, off=off, x0cm=x0cm: e.activation(
                        out=x0cm[:, i * TC + s * 128:i * TC + s * 128 + 128],
                        in_=pb[:, off:off + 128],
                        func=AF.Identity), waits=[("tensor", tm)],
                        rb=("PTF", "PTB")[idx % 2])
            P.x0ev_h[c % 2] = ev
            P.x0ev = ev
            x0w = []
            if c > 0:
                xbnd = po(lambda e: e.tensor_copy(
                    out=XRSF[:, 0:G * AJ:AJ],
                    in_=XRSF[:, TC:TC + (G - 1) * AJ + 1:AJ]),
                    waits=[("scalar", P.xam_prev)])
                mbnd = po(lambda e: e.tensor_copy(out=rowb(0, n=1, off=0),
                                                  in_=rowb(0, n=1, off=TC)),
                          waits=[("tensor", P.last_proj_t)])
            else:
                xbnd = P.mark("gpsimd")
                mbnd = xbnd

            mt0, rs0rep = ln_stats(x0cm, 0, 1, 2, 0,
                                   [("scalar", ev)], sqb=SQB,
                                   bslot=0,
                                   mt_extra=[("gpsimd", mbnd),
                                             ("tensor", P.last_proj_t)])
            xrs = XRSF
            xm = 0
            for i in range(G):
                xm = vo(lambda e, i=i, xrs=xrs: e.tensor_tensor(
                    out=xrs[:, i * AJ + 1:i * AJ + 1 + TC],
                    in0=x0cm[:, i * TC:(i + 1) * TC], in1=REP[:, 0:TC],
                    op=ALU.mult),
                    waits=[("vector", rs0rep), ("gpsimd", xbnd),
                           ] if i == 0 else (), bare=(0 < i < G - 1))
            bm = xbnd
            mtb = mbnd
            xam = 0
            for i in range(G):
                xam = ao(lambda e, i=i, xrs=xrs: e.activation(
                    out=bb["lnb"][:, i * TC:(i + 1) * TC],
                    in_=xrs[:, i * AJ + 1:i * AJ + 1 + TC],
                    func=AF.Identity, scale=VEC[:, 0 + i:1 + i]),
                    waits=[("vector", xm), ("gpsimd", bm)] if i == 0 else (),
                    bare=(i > 0))
            for i in range(G):
                xam = ao(lambda e, i=i, xrs=xrs: e.activation(
                    out=bb["q3"][:, i * TC:(i + 1) * TC],
                    in_=xrs[:, i * AJ:i * AJ + TC],
                    func=AF.Identity, scale=VEC[:, 8 + i:9 + i]),
                    bare=(i < G - 1))
            xa2 = 0
            for i in range(G):
                xa2 = po(lambda e, i=i: e.tensor_tensor(
                    out=bb["xa"][:, i * TC:(i + 1) * TC],
                    in0=bb["lnb"][:, i * TC:(i + 1) * TC],
                    in1=bb["q3"][:, i * TC:(i + 1) * TC], op=ALU.add),
                    waits=[("scalar", xam)] if i == 0 else (),
                    bare=(0 < i < G - 1))
            P.xam_prev = xam
            P.x0cm_free = [("vector", xm)]

            projw = [("gpsimd", xa2), ("gpsimd", mbnd), ("vector", lw),
                     ("gdma", wa_done), ("vector", mt0)]
            pev = 0
            for m_i, (outb, fn_a) in enumerate((("r", AF.Sigmoid), ("ek", AF.Exp),
                                                ("v", AF.Identity), ("td", AF.Exp))):
                pp = (PP0, PP1, PP2, PP3)[m_i]
                pn = ("PP0", "PP1", "PP2", "PP3")[m_i]
                for j2 in range(RND):
                    mmk = 0
                    for jj in range(SL):
                        j = SL * j2 + jj
                        for i in range(G):
                            w = projw if (m_i == 0 and j2 == 0 and jj == 0 and i == 0) else ()
                            if i == 0 and jj == 0 and j2 > 0:
                                w = [("scalar", pev)]
                            mmk = to(lambda e, m_i=m_i, i=i, j=j, jj=jj, pp=pp: e.matmul(
                                pp[:, jj * TC:jj * TC + TC],
                                WA[:, (m_i * 8 + i) * 1024 + j * 128:(m_i * 8 + i) * 1024 + (j + 1) * 128],
                                bb["xa"][:, i * TC:(i + 1) * TC],
                                start=(i == 0), stop=False), waits=w, wb=pn)
                        mmk = to(lambda e, m_i=m_i, j=j, jj=jj, pp=pp: e.matmul(
                            pp[:, jj * TC:jj * TC + TC],
                            AUGW[0:1, (2 * m_i) * 1024 + j * 128:(2 * m_i) * 1024 + (j + 1) * 128],
                            rowb(0, off=1), start=False, stop=False))
                        mmk = to(lambda e, m_i=m_i, j=j, jj=jj, pp=pp: e.matmul(
                            pp[:, jj * TC:jj * TC + TC],
                            AUGW[0:1, (2 * m_i + 1) * 1024 + j * 128:(2 * m_i + 1) * 1024 + (j + 1) * 128],
                            rowb(0, off=0), start=False, stop=True))
                    for jj in range(SL):
                        j = SL * j2 + jj
                        w = [("tensor", mmk)]
                        if m_i == 0 and j2 == 0 and jj == 0:
                            w.append(("gpsimd", P.cm_prev))
                        pev = ao(lambda e, outb=outb, fn_a=fn_a, j=j, jj=jj, pp=pp: e.activation(
                            out=fb[outb][:, j * TC:(j + 1) * TC],
                            in_=pp[:, jj * TC:jj * TC + TC], func=fn_a),
                            waits=w, rb=pn)
            pdone = pev
            P.last_proj_t = P.mark("tensor")
            return pdone

        def p1_wkv(c, pdone):
            ekm = 0
            for i in range(G):
                ekm = vo(lambda e, i=i: e.tensor_scalar(
                    out=fb["ek"][:, i * TC:(i + 1) * TC],
                    in0=fb["ek"][:, i * TC:(i + 1) * TC],
                    scalar1=E10, scalar2=None, op0=ALU.min, op1=ALU.bypass),
                    waits=[("scalar", pdone)] if i == 0 else (),
                    bare=(0 < i < G - 1))
            c5buf = (fb["s2"], C5B)[c % 2]
            ekv = 0
            for i in range(G):
                ekv = po(lambda e, i=i, c5buf=c5buf: e.tensor_tensor(
                    out=c5buf[:, i * TC:(i + 1) * TC],
                    in0=fb["ek"][:, i * TC:(i + 1) * TC],
                    in1=fb["v"][:, i * TC:(i + 1) * TC], op=ALU.mult),
                    waits=[("vector", ekm), ("scalar", pdone)] if i == 0 else (),
                    bare=(0 < i < G - 1))
            a_cur = fb["aA"]
            b_cur = fb["bA"]
            scm = 0
            for i in range(G):
                scm = vo(lambda e, i=i, c5buf=c5buf: e.tensor_tensor_scan(
                    out=fb["aA"][:, i * TC:(i + 1) * TC],
                    data0=fb["td"][:, i * TC:(i + 1) * TC],
                    data1=c5buf[:, i * TC:(i + 1) * TC],
                    initial=CARY[:, i:i + 1], op0=ALU.mult, op1=ALU.add),
                    waits=[("gpsimd", ekv)] if i == 0 else (), bare=(i > 0))
            for i in range(G):
                scm = vo(lambda e, i=i: e.tensor_tensor_scan(
                    out=fb["bA"][:, i * TC:(i + 1) * TC],
                    data0=fb["td"][:, i * TC:(i + 1) * TC],
                    data1=fb["ek"][:, i * TC:(i + 1) * TC],
                    initial=CARY[:, 8 + i:9 + i], op0=ALU.mult, op1=ALU.add),
                    bare=(i < G - 1))
            acar = po(lambda e: e.tensor_copy(
                out=CARY[:, 0:8], in_=fb["aA"][:, TC - 1:TC - 1 + (G - 1) * TC + 1:TC]),
                waits=[("vector", scm)])
            acar = po(lambda e: e.tensor_copy(
                out=CARY[:, 8:16], in_=fb["bA"][:, TC - 1:TC - 1 + (G - 1) * TC + 1:TC]))
            g5 = 0
            for i in range(G):
                g5 = ao(lambda e, i=i: e.activation(
                    out=fb["x1"][:, i * TC:(i + 1) * TC],
                    in_=fb["ek"][:, i * TC:(i + 1) * TC],
                    func=AF.Identity, scale=VGD[:, 8 + i:9 + i]),
                    waits=[("vector", ekm), ("vector", gd5),
                           ("gpsimd", P.cm_prev)] if i == 0 else (),
                    bare=(0 < i < G - 1))
            dn = 0
            for i in range(G):
                dn = vo(lambda e, i=i, b_cur=b_cur: e.scalar_tensor_tensor(
                    out=fb["x2"][:, i * TC:(i + 1) * TC],
                    in0=b_cur[:, i * TC:(i + 1) * TC], scalar=THR,
                    in1=fb["x1"][:, i * TC:(i + 1) * TC],
                    op0=ALU.mult, op1=ALU.add),
                    waits=[("scalar", g5), ("vector", scm)] if i == 0 else (),
                    bare=(i > 0))
            for i in range(G):
                dn = vo(lambda e, i=i: e.tensor_scalar(
                    out=fb["x2"][:, i * TC:(i + 1) * TC],
                    in0=fb["x2"][:, i * TC:(i + 1) * TC],
                    scalar1=THR * EPS_A, scalar2=None, op0=ALU.add, op1=ALU.bypass),
                    bare=(i < G - 1))
            nm = 0
            for i in range(G):
                nm = vo(lambda e, i=i: e.tensor_scalar(
                    out=fb["x1"][:, i * TC:(i + 1) * TC],
                    in0=fb["x1"][:, i * TC:(i + 1) * TC],
                    scalar1=5.0, scalar2=-EPS_A, op0=ALU.mult, op1=ALU.add),
                    bare=(i < G - 1))
            n2 = 0
            for i in range(G):
                n2 = po(lambda e, i=i: e.tensor_tensor(
                    out=fb["x1"][:, i * TC:(i + 1) * TC],
                    in0=fb["x1"][:, i * TC:(i + 1) * TC],
                    in1=fb["v"][:, i * TC:(i + 1) * TC], op=ALU.mult),
                    waits=[("vector", nm)] if i == 0 else (),
                    bare=(0 < i < G - 1))
            n3 = 0
            for i in range(G):
                n3 = po(lambda e, i=i, a_cur=a_cur: e.tensor_tensor(
                    out=fb["x1"][:, i * TC:(i + 1) * TC],
                    in0=fb["x1"][:, i * TC:(i + 1) * TC],
                    in1=a_cur[:, i * TC:(i + 1) * TC], op=ALU.add),
                    bare=(i < G - 1))
            rc = 0
            for i in range(G):
                rc = vo(lambda e, i=i: e.reciprocal(
                    out=fb["s1"][:, i * TC:(i + 1) * TC],
                    in_=fb["x2"][:, i * TC:(i + 1) * TC]),
                    waits=[("gpsimd", n3)] if i == 0 else (),
                    bare=(0 < i < G - 1))
            wkm = 0
            for i in range(G):
                wkm = po(lambda e, i=i: e.tensor_tensor(
                    out=fb["x1"][:, i * TC:(i + 1) * TC],
                    in0=fb["x1"][:, i * TC:(i + 1) * TC],
                    in1=fb["s1"][:, i * TC:(i + 1) * TC], op=ALU.mult),
                    waits=[("vector", rc)] if i == 0 else (),
                    bare=(0 < i < G - 1))
            cm = 0
            for i in range(G):
                cm = po(lambda e, i=i, c5buf=c5buf: e.tensor_tensor(
                    out=c5buf[:, i * TC:(i + 1) * TC],
                    in0=fb["x1"][:, i * TC:(i + 1) * TC],
                    in1=fb["r"][:, i * TC:(i + 1) * TC], op=ALU.mult),
                    bare=(i < G - 1))
            P.cm_prev = cm
            return cm, c5buf, a_cur, b_cur

        def sl(buf, t, half):
            return buf[:, half * 4 * TC + t:half * 4 * TC + t + 3 * TC + 1:TC]

        def p1_scan(c, cm, c5buf, a_cur, b_cur):
            mh_c = fb["mhA"]
            for t in range(TC):
                if t == 0:
                    pa = CARY[:, 16:20]
                    pb = CARY[:, 20:24]
                else:
                    pa = sl(mh_c, t - 1, 0)
                    pb = sl(mh_c, t - 1, 1)
                vo(lambda e, t=t, pa=pa, c5buf=c5buf: e.scalar_tensor_tensor(
                    out=sl(fb["s1"], t, 0), in0=pa, scalar=1.0,
                    in1=sl(c5buf, t, 0), op0=ALU.is_gt, op1=ALU.subtract),
                    waits=[("gpsimd", cm)] if t == 0 else (), bare=True)
                vo(lambda e, t=t, pb=pb, c5buf=c5buf: e.scalar_tensor_tensor(
                    out=sl(fb["s1"], t, 1), in0=pb, scalar=1.0,
                    in1=sl(c5buf, t, 1), op0=ALU.is_gt, op1=ALU.subtract),
                    bare=True)
                vo(lambda e, t=t, pa=pa, mh_c=mh_c: e.scalar_tensor_tensor(
                    out=sl(mh_c, t, 0), in0=pa, scalar=BETA,
                    in1=sl(fb["s1"], t, 0), op0=ALU.mult, op1=ALU.subtract),
                    bare=True)
                vo(lambda e, t=t, pb=pb, mh_c=mh_c: e.scalar_tensor_tensor(
                    out=sl(mh_c, t, 1), in0=pb, scalar=BETA,
                    in1=sl(fb["s1"], t, 1), op0=ALU.mult, op1=ALU.subtract),
                    bare=(t < TC - 1))
            scand = P.mark("vector")
            P.scand_prev = scand
            mcar = po(lambda e, mh_c=mh_c: e.tensor_copy(
                out=CARY[:, 16:24],
                in_=mh_c[:, TC - 1:TC - 1 + 7 * TC + 1:TC]),
                waits=[("vector", scand)])
            P.guard("vector")
            spm = vo(lambda e, mh_c=mh_c: e.tensor_single_scalar(
                out=bb["spk"][:, :], in_=mh_c[:, :], scalar=1.0, op=ALU.is_gt),
                waits=[("gdma", spk_stored[-1])] if spk_stored else ())
            sst = P.dma("gpsimd", spk_h[c, :, :], bb["spk"][:, :],
                        waits=[("vector", spm)], no_pend=True)
            spk_stored.append(sst)
            if debug and c == 0:
                dstore(dbg["d_r"], fb["r"][:, :], waits=[("gpsimd", cm)], flush=False)
                dstore(dbg["d_ek"], fb["ek"][:, :], waits=[("gpsimd", cm)], flush=False)
                dstore(dbg["d_v"], fb["v"][:, :], waits=[("gpsimd", cm)], flush=False)
                dstore(dbg["d_td"], fb["td"][:, :], waits=[("gpsimd", cm)], flush=False)
                dstore(dbg["d_a"], a_cur[:, :], waits=[("gpsimd", cm)], flush=False)
                dstore(dbg["d_b"], b_cur[:, :], waits=[("gpsimd", cm)], flush=False)
                dstore(dbg["d_c5"], c5buf[:, :], waits=[("vector", scand)], flush=False)
                dstore(dbg["d_mem"], mh_c[:, :], waits=[("vector", scand)], flush=False)
                dstore(dbg["d_spk"], bb["spk"][:, :], waits=[("vector", spm)])
                for eng in ("vector", "scalar", "gpsimd", "tensor"):
                    P.q[eng].append(((lambda e: None), (("store", P.store),), 0, eng))

        stW = p1_wkv(0, p1_front(0))
        for c in range(n_chunks):
            if c + 1 < n_chunks:
                stA = p1_front(c + 1)
            p1_scan(c, *stW)
            if c + 1 < n_chunks:
                stW = p1_wkv(c + 1, stA)

        ph1_v = P.mark("vector")
        ph1_p = P.mark("gpsimd")
        if stages < 2:
            return _finish()
        P.wsum_v = P.mark("vector")
        P.wsum_t = P.mark("tensor")
        wb_done = 0
        for mi, nm in enumerate(("wq", "wk", "wv", "wo")):
            for i in range(G):
                wb_done = gld(WB[:, (mi * 8 + i) * 1024:(mi * 8 + i + 1) * 1024],
                              w_h[nm][i * 128:(i + 1) * 128, :],
                              waits=[("tensor", P.last_proj_t)] if (mi == 0 and i == 0) else ())
        lw = emit_wsum(WB, 0 * 8192, None, 0, extra_w=[("gdma", wb_done), ("vector", ph1_v)])
        lw = emit_wsum(WB, 1 * 8192, None, 1, extra_w=[("gdma", wb_done)])
        lw = emit_wsum(WB, 2 * 8192, None, 2, extra_w=[("gdma", wb_done)])
        bwe = 0
        for half in range(2):
            bw = 0
            for i in range(G):
                bw = to(lambda e, i=i, half=half: e.matmul(
                    PX[0:1, 0:512], VECB[:, 1 + i:2 + i],
                    WB[:, (3 * 8 + i) * 1024 + half * 512:(3 * 8 + i) * 1024 + half * 512 + 512],
                    start=(i == 0), stop=(i == G - 1)),
                    waits=[("gdma", wb_done), ("vector", lw),
                           ("vector", P.wsum_v)] if (i == 0) else (), wb="PX")
            bwe = vo(lambda e, half=half: e.tensor_copy(
                out=AUGW[0:1, 4 * 1024 + half * 512:4 * 1024 + half * 512 + 512],
                in_=PX[0:1, 0:512]), waits=[("tensor", bw)], rb="PX")
            P.wsum_v = bwe
        gf = 0
        for i in range(G):
            gf = ao(lambda e, i=i: e.activation(
                out=WB[:, (3 * 8 + i) * 1024:(3 * 8 + i + 1) * 1024],
                in_=WB[:, (3 * 8 + i) * 1024:(3 * 8 + i + 1) * 1024],
                func=AF.Identity, scale=VEC[:, 24 + i:25 + i]),
                waits=[("vector", bwe)])
        lw2 = emit_wsum(WB, 3 * 8192, None, 3, extra_w=[("scalar", gf)])
        wbprep_v = P.mark("vector")
        if stages < 3:
            return _finish()

        P.ffn_fo_prev = 0

        def emit_ffn(xin, xout, first_waits, dbg_key=None, c=0):
            mt, rsre = ln_stats(xin, 4, 5, 6, 0, first_waits, bslot=1)
            mre = replicate(row(5, off=1), 2, [("vector", mt),
                                               ("gpsimd", P.ffn_fo_prev)])
            um = 0
            for i in range(G):
                um = vo(lambda e, i=i, xin=xin: e.tensor_tensor(
                    out=fb["s2"][:, i * TC:(i + 1) * TC],
                    in0=xin[:, i * TC:(i + 1) * TC], in1=REP[:, 0:TC],
                    op=ALU.mult), waits=[("vector", rsre)] if i == 0 else ())
            u2 = 0
            for i in range(G):
                u2 = po(lambda e, i=i: e.tensor_tensor(
                    out=fb["s2"][:, i * TC:(i + 1) * TC],
                    in0=fb["s2"][:, i * TC:(i + 1) * TC],
                    in1=REP[:, 2 * TC:3 * TC], op=ALU.add),
                    waits=[("vector", um), ("vector", mre)] if i == 0 else ())
            ab_ = 0
            for i in range(G):
                ab_ = ao(lambda e, i=i: e.activation(
                    out=fb["s1"][:, i * TC:(i + 1) * TC],
                    in_=fb["s2"][:, i * TC:(i + 1) * TC], func=AF.Abs),
                    waits=[("gpsimd", u2)] if i == 0 else ())
            tm = 0
            for step in (4, 2, 1):
                for i in range(step):
                    tm = vo(lambda e, i=i, step=step: e.tensor_tensor(
                        out=fb["s1"][:, i * TC:(i + 1) * TC],
                        in0=fb["s1"][:, i * TC:(i + 1) * TC],
                        in1=fb["s1"][:, (i + step) * TC:(i + step + 1) * TC],
                        op=ALU.max),
                        waits=[("scalar", ab_)] if (step == 4 and i == 0) else ())
                if step > 1:
                    P.guard("vector")
            tt1 = 0
            for s in range(NSUB):
                tt1 = to(lambda e, s=s: e.transpose(
                    PTF[:, s * 128:(s + 1) * 128],
                    fb["s1"][:, s * 128:(s + 1) * 128], IDENT[:, :]),
                    waits=[("vector", tm)] if s == 0 else (), wb="PTF")
            SC = 8 * TC - 8
            sc = 0
            for s in range(NSUB):
                sc = vo(lambda e, s=s: e.tensor_reduce(
                    out=fb["s1"][:, SC + s:SC + s + 1],
                    in_=PTF[:, s * 128:(s + 1) * 128], axis=AXX, op=ALU.max),
                    waits=[("tensor", tt1)] if s == 0 else (), rb="PTF")
            P.guard("vector")
            sc = vo(lambda e: e.tensor_scalar(
                out=fb["s1"][:, SC + 2:SC + 4], in0=fb["s1"][:, SC:SC + 2],
                scalar1=1e-12, scalar2=1.0 / 127.0, op0=ALU.max, op1=ALU.mult))
            P.guard("vector")
            sc = vo(lambda e: e.reciprocal(
                out=fb["s1"][:, SC + 4:SC + 6], in_=fb["s1"][:, SC + 2:SC + 4]))
            P.guard("vector")
            tb = to(lambda e: e.transpose(PTF[0:4, 256:256 + 128],
                                          fb["s1"][:, SC + 2:SC + 6], IDENT[:, :]),
                    waits=[("vector", sc)], wb="PTF")
            r4 = vo(lambda e: e.tensor_copy(out=fb["s1"][0:4, 0:128],
                                            in_=PTF[0:4, 256:256 + 128]),
                    waits=[("tensor", tb)], rb="PTF")
            lsm = to(lambda e: e.matmul(PX[0:1, 0:128], IDENT[0:4, 0:1],
                                        fb["s1"][0:4, 0:128], start=True, stop=True),
                     waits=[("vector", r4)], wb="PX")
            lsm = to(lambda e: e.matmul(PX[0:1, 128:256], IDENT[0:4, 1:2],
                                        fb["s1"][0:4, 0:128], start=True, stop=True))
            lsm = to(lambda e: e.matmul(PX[0:1, 256:384], IDENT[0:4, 2:3],
                                        fb["s1"][0:4, 0:128], start=True, stop=True))
            lsm = to(lambda e: e.matmul(PX[0:1, 384:512], IDENT[0:4, 3:4],
                                        fb["s1"][0:4, 0:128], start=True, stop=True))
            lse = vo(lambda e: e.tensor_copy(out=row(7, n=TC),
                                             in_=PX[0:1, 0:TC]),
                     waits=[("tensor", lsm)], rb="PX")
            ive = vo(lambda e: e.tensor_copy(out=row(8, n=TC),
                                             in_=PX[0:1, 256:256 + TC]), rb="PX")
            ivr = replicate(row(8, n=TC), 2, [("vector", ive)])
            lsr = replicate(row(7, n=TC), 1, [("vector", lse)])
            q1 = 0
            for i in range(G):
                q1 = vo(lambda e, i=i: e.tensor_tensor(
                    out=fb["s2"][:, i * TC:(i + 1) * TC],
                    in0=fb["s2"][:, i * TC:(i + 1) * TC],
                    in1=REP[:, 2 * TC:3 * TC], op=ALU.mult),
                    waits=[("vector", ivr)] if i == 0 else ())
            for i in range(G):
                q1 = vo(lambda e, i=i: e.tensor_scalar(
                    out=fb["s2"][:, i * TC:(i + 1) * TC],
                    in0=fb["s2"][:, i * TC:(i + 1) * TC],
                    scalar1=MAGIC, scalar2=MAGIC, op0=ALU.add, op1=ALU.subtract))
            for i in range(G):
                q1 = vo(lambda e, i=i: e.tensor_scalar(
                    out=bb["q3"][:, i * TC:(i + 1) * TC],
                    in0=fb["s2"][:, i * TC:(i + 1) * TC],
                    scalar1=127.0, scalar2=-127.0, op0=ALU.min, op1=ALU.max))
            fev = 0
            for j2 in range(RND):
                mmk = 0
                for jj in range(SL):
                    j = SL * j2 + jj
                    for i in range(G):
                        w = ()
                        if j2 == 0 and jj == 0 and i == 0:
                            w = [("vector", q1)]
                        elif i == 0 and jj == 0 and j2 > 0:
                            w = [("scalar", fev)]
                        mmk = to(lambda e, i=i, j=j, jj=jj: e.matmul(
                            PP0[:, jj * TC:jj * TC + TC],
                            QR[:, (i * 8 + j) * 128:(i * 8 + j + 1) * 128],
                            bb["q3"][:, i * TC:(i + 1) * TC],
                            start=(i == 0), stop=(i == G - 1)), waits=w, wb="PP0")
                for jj in range(SL):
                    j = SL * j2 + jj
                    fev = ao(lambda e, j=j, jj=jj: e.activation(
                        out=fb["s2"][:, j * TC:(j + 1) * TC],
                        in_=PP0[:, jj * TC:jj * TC + TC],
                        func=AF.Identity, scale=VRQ[:, 8 + j:9 + j]),
                        waits=[("tensor", mmk)], rb="PP0")
            fo = 0
            for i in range(G):
                fo = po(lambda e, i=i: e.tensor_tensor(
                    out=fb["s2"][:, i * TC:(i + 1) * TC],
                    in0=fb["s2"][:, i * TC:(i + 1) * TC],
                    in1=REP[:, TC:2 * TC], op=ALU.mult),
                    waits=[("scalar", fev), ("vector", lsr)] if i == 0 else (),
                    bare=(0 < i < G - 1))
            P.ffn_fo_prev = fo
            if debug and dbg_key and c == 0:
                dstore(dbg[dbg_key], fb["s2"][:, :], waits=[("gpsimd", fo)])
            fo2 = 0
            for i in range(G):
                fo2 = po(lambda e, i=i, xin=xin, xout=xout: e.tensor_tensor(
                    out=xout[:, i * TC:(i + 1) * TC],
                    in0=fb["s2"][:, i * TC:(i + 1) * TC],
                    in1=xin[:, i * TC:(i + 1) * TC], op=ALU.add))
            return fo2

        P.ph2_x1m = 0
        P.ph2_ost = 0
        P.lm_pref = {}
        P.spl_pref = {}
        for c in range(n_chunks):
            x0cm = fb["x0cm"]
            hb = (c % 2) * NSUB * 1024
            lm = P.lm_pref.pop(c, None)
            if lm is None:
                lm = emit_x0_load(c)
            spl = P.spl_pref.pop(c, None)
            if spl is None:
                spl = ld(bb["spk"][:, :], spk_h[c, :, :],
                         waits=[("gdma", spk_stored[-1])] +
                               ([("gpsimd", P.ph2_x1m)] if c > 0 else []))
            ev = 0
            for s in range(NSUB):
                for i in range(G):
                    idx = s * G + i
                    w = []
                    if idx == 0:
                        w = [("sync", spl)]
                        if c == 0:
                            w += [("vector", ph1_v), ("gpsimd", ph1_p)]
                        else:
                            w += [("gpsimd", P.ph2_x1m)]
                    pb = (PTF, PTB)[idx % 2]
                    off = 128 * ((idx // 2) % 4)
                    tm = to(lambda e, s=s, i=i, pb=pb, off=off, hb=hb: e.transpose(
                        pb[:, off:off + 128],
                        XIN[:, hb + s * 1024 + i * 128:hb + s * 1024 + (i + 1) * 128],
                        IDENT[:, :]), waits=w, wb=("PTF", "PTB")[idx % 2])
                    ev = ao(lambda e, s=s, i=i, pb=pb, off=off: e.activation(
                        out=x0cm[:, i * TC + s * 128:i * TC + s * 128 + 128],
                        in_=pb[:, off:off + 128],
                        func=AF.Identity), waits=[("tensor", tm)],
                        rb=("PTF", "PTB")[idx % 2])
            P.x0ev_h[c % 2] = ev
            x1m = 0
            for i in range(G):
                x1m = po(lambda e, i=i: e.tensor_tensor(
                    out=fb["x1"][:, i * TC:(i + 1) * TC],
                    in0=x0cm[:, i * TC:(i + 1) * TC],
                    in1=bb["spk"][:, i * TC:(i + 1) * TC], op=ALU.add),
                    waits=([("scalar", ev), ("sync", spl)] +
                           ([("tensor", P.ph2_ffn2t), ("vector", P.ph2_ffn2v)]
                            if c > 0 else [])) if i == 0 else ())
            P.ph2_x1m = x1m
            if c + 1 < n_chunks:
                P.lm_pref[c + 1] = emit_x0_load(c + 1)
                P.spl_pref[c + 1] = ld(bb["spk"][:, :], spk_h[c + 1, :, :],
                                       waits=[("gpsimd", x1m)])
            if debug and c == 0:
                dstore(dbg["d_x1"], fb["x1"][:, :], waits=[("gpsimd", x1m)])

            P.trigc = emit_trig_pair(c * TC)

            x2m = emit_ffn(fb["x1"], fb["x2"],
                           [("gpsimd", x1m), ("vector", wbprep_v)], "d_f1", c)
            if debug and c == 0:
                dstore(dbg["d_x2"], fb["x2"][:, :], waits=[("gpsimd", x2m)])

            mt2, ln2re = ln_stats(fb["x2"], 9, 10, 11, 0, [("gpsimd", x2m)], bslot=2)
            l2c = 0
            for i in range(G):
                l2c = vo(lambda e, i=i: e.tensor_tensor(
                    out=bb["lnb"][:, i * TC:(i + 1) * TC],
                    in0=fb["x2"][:, i * TC:(i + 1) * TC], in1=REP[:, 0:TC],
                    op=ALU.mult), waits=[("vector", ln2re)] if i == 0 else ())
            if debug and c == 0:
                dstore(dbg["d_ln2"], bb["lnb"][:, :], waits=[("vector", l2c)])

            def proj_b(w_base, fn_a, aug_slot, dstb, pp, pn):
                evp = 0
                for j2 in range(RND):
                    mmk = 0
                    for jj in range(SL):
                        j = SL * j2 + jj
                        for i in range(G):
                            w = ()
                            if j2 == 0 and jj == 0 and i == 0:
                                w = [("vector", l2c), ("vector", mt2),
                                     ("vector", wbprep_v)]
                            elif i == 0 and jj == 0 and j2 > 0:
                                w = [("scalar", evp)]
                            mmk = to(lambda e, i=i, j=j, jj=jj, w_base=w_base, pp=pp: e.matmul(
                                pp[:, jj * TC:jj * TC + TC],
                                WB[:, (w_base + i) * 1024 + j * 128:(w_base + i) * 1024 + (j + 1) * 128],
                                bb["lnb"][:, i * TC:(i + 1) * TC],
                                start=(i == 0), stop=False), waits=w, wb=pn)
                        mmk = to(lambda e, j=j, jj=jj, aug_slot=aug_slot, pp=pp: e.matmul(
                            pp[:, jj * TC:jj * TC + TC],
                            AUGW[0:1, aug_slot * 1024 + j * 128:aug_slot * 1024 + (j + 1) * 128],
                            rowb(2, off=1), start=False, stop=True))
                    for jj in range(SL):
                        j = SL * j2 + jj
                        evp = ao(lambda e, j=j, jj=jj, fn_a=fn_a, dstb=dstb, pp=pp: e.activation(
                            out=dstb[:, j * TC:(j + 1) * TC],
                            in_=pp[:, jj * TC:jj * TC + TC], func=fn_a),
                            waits=[("tensor", mmk)], rb=pn)
                return evp

            qev = proj_b(0, AF.Silu, 0, bb["qr"], PP1, "PP1")
            kev = proj_b(8, AF.Silu, 1, bb["kr"], PP2, "PP2")
            vev = proj_b(16, AF.Identity, 2, bb["xa"], PP3, "PP3")

            g1m = 0
            for i in range(G):
                g1m = to(lambda e, i=i: e.matmul(
                    PX[0:64, 256:256 + TC], WGT[:, i * 64:(i + 1) * 64],
                    bb["lnb"][:, i * TC:(i + 1) * TC],
                    start=(i == 0), stop=False),
                    waits=[("vector", l2c), ("vector", mt2)] if i == 0 else (),
                    wb="PX")
            g1m = to(lambda e: e.matmul(
                PX[0:64, 256:256 + TC], AUGW[0:1, 8 * 1024:8 * 1024 + 64],
                rowb(2, off=1), start=False, stop=True))
            g1e = ao(lambda e: e.activation(out=G1B[:, :], in_=PX[0:64, 256:256 + TC],
                                            func=AF.Identity), waits=[("tensor", g1m)],
                     rb="PX")
            g2m = 0
            for j2 in range(RND):
                for jj in range(SL):
                    j = SL * j2 + jj
                    g2m = to(lambda e, j=j, jj=jj: e.matmul(
                        PP0[:, jj * TC:jj * TC + TC],
                        WGT[0:64, 512 + j * 128:512 + (j + 1) * 128], G1B[:, :],
                        start=True, stop=True),
                        waits=[("scalar", g1e)] if j == 0 else (), wb="PP0")
                for jj in range(SL):
                    j = SL * j2 + jj
                    gev = ao(lambda e, j=j, jj=jj: e.activation(
                        out=fb["gate"][:, j * TC:(j + 1) * TC],
                        in_=PP0[:, jj * TC:jj * TC + TC], func=AF.Sigmoid),
                        waits=[("tensor", g2m)], rb="PP0")
                if j2 < RND - 1:
                    P.q["tensor"].append(((lambda e: None), (("scalar", gev),), 0, "tensor"))
            gated_w = gev
            if debug and c == 0:
                dstore(dbg["d_gate"], fb["gate"][:, :], waits=[("scalar", gated_w)])

            def rope(buf, done):
                rm = 0
                cosr = CS[:, 0:TC]
                sinr = CS[:, TC:2 * TC]
                for g in range(4):
                    rm = vo(lambda e, g=g, buf=buf: e.tensor_tensor(
                        out=bb["esb"][:, (g % 2) * 4 * TC + 0:(g % 2) * 4 * TC + TC],
                        in0=buf[:, g * TC:(g + 1) * TC], in1=cosr, op=ALU.mult),
                        waits=[("scalar", done), ("scalar", P.trigc)] if g == 0 else (),
                        bare=(g > 0))
                    rm = vo(lambda e, g=g, buf=buf: e.tensor_tensor(
                        out=bb["esb"][:, (g % 2) * 4 * TC + TC:(g % 2) * 4 * TC + 2 * TC],
                        in0=buf[:, (4 + g) * TC:(5 + g) * TC], in1=sinr, op=ALU.mult),
                        bare=True)
                    rm = vo(lambda e, g=g, buf=buf: e.tensor_tensor(
                        out=bb["esb"][:, (g % 2) * 4 * TC + 2 * TC:(g % 2) * 4 * TC + 3 * TC],
                        in0=buf[:, g * TC:(g + 1) * TC], in1=sinr, op=ALU.mult),
                        bare=True)
                    rm = vo(lambda e, g=g, buf=buf: e.tensor_tensor(
                        out=bb["esb"][:, (g % 2) * 4 * TC + 3 * TC:(g % 2) * 4 * TC + 4 * TC],
                        in0=buf[:, (4 + g) * TC:(5 + g) * TC], in1=cosr, op=ALU.mult),
                        bare=True)
                    rm = vo(lambda e, g=g, buf=buf: e.tensor_tensor(
                        out=buf[:, g * TC:(g + 1) * TC],
                        in0=bb["esb"][:, (g % 2) * 4 * TC + 0:(g % 2) * 4 * TC + TC],
                        in1=bb["esb"][:, (g % 2) * 4 * TC + TC:(g % 2) * 4 * TC + 2 * TC],
                        op=ALU.subtract), bare=True)
                    rm = vo(lambda e, g=g, buf=buf: e.tensor_tensor(
                        out=buf[:, (4 + g) * TC:(5 + g) * TC],
                        in0=bb["esb"][:, (g % 2) * 4 * TC + 2 * TC:(g % 2) * 4 * TC + 3 * TC],
                        in1=bb["esb"][:, (g % 2) * 4 * TC + 3 * TC:(g % 2) * 4 * TC + 4 * TC],
                        op=ALU.add), bare=(g < 3))
                return rm
            qrm = rope(bb["qr"], qev)
            krm = rope(bb["kr"], kev)
            if debug and c == 0:
                dstore(dbg["d_qr"], bb["qr"][:, :], waits=[("vector", qrm)])
                dstore(dbg["d_kr"], bb["kr"][:, :], waits=[("vector", krm)])

            for s in range(NSUB):
                tev = 0
                for i in range(G):
                    w = [("vector", krm)] if i == 0 else []
                    if i >= 2:
                        w.append(("scalar", tev - 1))
                    tm = to(lambda e, i=i, s=s: e.matmul(
                        PTB[:, (i % 2) * 128:(i % 2) * 128 + 128],
                        bb["kr"][:, i * TC + s * 128:i * TC + s * 128 + 128],
                        IDENTB[:, :], start=True, stop=True), waits=w, wb="PTB")
                    tev = ao(lambda e, i=i, s=s: e.activation(
                        out=bb["ktok"][:, s * 1024 + i * 128:s * 1024 + (i + 1) * 128],
                        in_=PTB[:, (i % 2) * 128:(i % 2) * 128 + 128],
                        func=AF.Identity), waits=[("tensor", tm)], rb="PTB")
                for i in range(G):
                    w = [("scalar", vev)] if i == 0 else []
                    if i >= 2:
                        w.append(("scalar", tev - 1))
                    tm = to(lambda e, i=i, s=s: e.matmul(
                        PTB[:, 256 + (i % 2) * 128:256 + (i % 2) * 128 + 128],
                        bb["xa"][:, i * TC + s * 128:i * TC + s * 128 + 128],
                        IDENTB[:, :], start=True, stop=True), waits=w, wb="PTB")
                    tev = ao(lambda e, i=i, s=s: e.activation(
                        out=bb["vtok"][:, s * 1024 + i * 128:s * 1024 + (i + 1) * 128],
                        in_=PTB[:, 256 + (i % 2) * 128:256 + (i % 2) * 128 + 128],
                        func=AF.Identity), waits=[("tensor", tm)], rb="PTB")
                first_sub = (c == 0 and s == 0)
                if not first_sub:
                    scp = vo(lambda e: e.tensor_copy(out=SSB[:, :], in_=SACC[:, :]),
                             waits=[("vector", P.sacc_v)])
                for h in range(H):
                    glo, pos = h // 4, 32 * (h % 4)
                    eslot = h % 4
                    w = []
                    if h == 0:
                        w += [("vector", qrm), ("scalar", tev)]
                    if h >= 4:
                        w.append(("vector", P.e_evac - 3))
                    em = to(lambda e, h=h, s=s, glo=glo, pos=pos, eslot=eslot: e.matmul(
                        PP1[:, eslot * 128:(eslot + 1) * 128],
                        bb["kr"][pos:pos + 32, glo * TC + s * 128:glo * TC + s * 128 + 128],
                        bb["qr"][pos:pos + 32, glo * TC + s * 128:glo * TC + s * 128 + 128],
                        start=True, stop=False,
                        tile_position=(pos, 0)), waits=w, wb="PP1")
                    em = to(lambda e, h=h, s=s, glo=glo, pos=pos, eslot=eslot: e.matmul(
                        PP1[:, eslot * 128:(eslot + 1) * 128],
                        bb["kr"][pos:pos + 32, (4 + glo) * TC + s * 128:(4 + glo) * TC + s * 128 + 128],
                        bb["qr"][pos:pos + 32, (4 + glo) * TC + s * 128:(4 + glo) * TC + s * 128 + 128],
                        start=False, stop=True, tile_position=(pos, 0)))
                    P.e_evac = vo(lambda e, h=h, eslot=eslot: e.tensor_tensor(
                        out=bb["esb"][:, eslot * 128:(eslot + 1) * 128],
                        in0=PP1[:, eslot * 128:(eslot + 1) * 128],
                        in1=TRI[:, :], op=ALU.mult), waits=[("tensor", em)],
                        rb="PP1")
                    oslot = (h // 2) % 4
                    obase = 64 * (h % 2)
                    evw = [("vector", P.e_evac)]
                    if h >= 8:
                        evw.append(("vector", P.o_evac))
                    evm = to(lambda e, h=h, s=s, oslot=oslot, obase=obase, fs=first_sub: e.matmul(
                        PP3[obase:obase + 64, oslot * 128:(oslot + 1) * 128],
                        bb["vtok"][:, s * 1024 + h * 64:s * 1024 + (h + 1) * 64],
                        bb["esb"][:, (h % 4) * 128:((h % 4) + 1) * 128],
                        start=True, stop=fs,
                        tile_position=(0, obase)), waits=evw, wb="PP3")
                    if not first_sub:
                        evm = to(lambda e, h=h, s=s, glo=glo, pos=pos, oslot=oslot, obase=obase: e.matmul(
                            PP3[obase:obase + 64, oslot * 128:(oslot + 1) * 128],
                            SSB[pos:pos + 32, 64 * glo:64 * glo + 64],
                            bb["qr"][pos:pos + 32, glo * TC + s * 128:glo * TC + s * 128 + 128],
                            start=False, stop=False, tile_position=(pos, obase)),
                            waits=[("vector", scp)] if h == 0 else ())
                        evm = to(lambda e, h=h, s=s, glo=glo, pos=pos, oslot=oslot, obase=obase: e.matmul(
                            PP3[obase:obase + 64, oslot * 128:(oslot + 1) * 128],
                            SSB[pos:pos + 32, 256 + 64 * glo:256 + 64 * glo + 64],
                            bb["qr"][pos:pos + 32, (4 + glo) * TC + s * 128:(4 + glo) * TC + s * 128 + 128],
                            start=False, stop=True, tile_position=(pos, obase)))
                    supw = [("scalar", tev)] if h == 0 else []
                    if h == 0:
                        supw.append(("vector", P.sacc_v))
                    sup = to(lambda e, h=h, s=s, glo=glo, pos=pos: e.matmul(
                        PS[pos:pos + 32, 64 * glo:64 * glo + 64],
                        bb["ktok"][:, s * 1024 + glo * 128 + pos:s * 1024 + glo * 128 + pos + 32],
                        bb["vtok"][:, s * 1024 + h * 64:s * 1024 + (h + 1) * 64],
                        start=True, stop=True, tile_position=(0, pos)), waits=supw,
                        wb="PS")
                    sup = to(lambda e, h=h, s=s, glo=glo, pos=pos: e.matmul(
                        PS[pos:pos + 32, 256 + 64 * glo:256 + 64 * glo + 64],
                        bb["ktok"][:, s * 1024 + (4 + glo) * 128 + pos:s * 1024 + (4 + glo) * 128 + pos + 32],
                        bb["vtok"][:, s * 1024 + h * 64:s * 1024 + (h + 1) * 64],
                        start=True, stop=True, tile_position=(0, pos)), wb="PS")
                    if h % 2 == 1 and h % 8 == 7:
                        for hp in range(h - 7, h + 1, 2):
                            os2 = (hp // 2) % 4
                            P.o_evac = vo(lambda e, hp=hp, os2=os2, s=s: e.tensor_tensor(
                                out=fb["s2"][:, (hp // 2) * TC + s * 128:(hp // 2) * TC + s * 128 + 128],
                                in0=PP3[:, os2 * 128:(os2 + 1) * 128],
                                in1=fb["gate"][:, (hp // 2) * TC + s * 128:(hp // 2) * TC + s * 128 + 128],
                                op=ALU.mult),
                                waits=[("tensor", sup), ("scalar", gated_w)],
                                rb="PP3")
                P.s_upd = P.mark("tensor")
                P.sacc_v = vo(lambda e: e.tensor_tensor(
                    out=SACC[:, :], in0=SACC[:, :], in1=PS[:, :], op=ALU.add),
                    waits=[("tensor", P.s_upd)], rb="PS")
            ao_done = P.o_evac
            if debug and c == 0:
                dstore(dbg["d_ao"], fb["s2"][:, :], waits=[("vector", ao_done)])

            mt3, re3 = ln_stats(fb["s2"], 12, 13, 14, 0, [("vector", ao_done)],
                                sqb=fb["s1"], bslot=3)
            c3 = 0
            for i in range(G):
                c3 = vo(lambda e, i=i: e.tensor_tensor(
                    out=bb["lnb"][:, i * TC:(i + 1) * TC],
                    in0=fb["s2"][:, i * TC:(i + 1) * TC], in1=REP[:, 0:TC],
                    op=ALU.mult), waits=[("vector", re3)] if i == 0 else ())
            wev = 0
            for j2 in range(RND):
                mmk = 0
                for jj in range(SL):
                    j = SL * j2 + jj
                    for i in range(G):
                        w = ()
                        if j2 == 0 and jj == 0 and i == 0:
                            w = [("vector", c3), ("vector", mt3)]
                        elif i == 0 and jj == 0 and j2 > 0:
                            w = [("vector", wev)]
                        mmk = to(lambda e, i=i, j=j, jj=jj: e.matmul(
                            PP2[:, jj * TC:jj * TC + TC],
                            WB[:, (3 * 8 + i) * 1024 + j * 128:(3 * 8 + i) * 1024 + (j + 1) * 128],
                            bb["lnb"][:, i * TC:(i + 1) * TC],
                            start=(i == 0), stop=False), waits=w, wb="PP2")
                    mmk = to(lambda e, j=j, jj=jj: e.matmul(
                        PP2[:, jj * TC:jj * TC + TC],
                        AUGW[0:1, 3 * 1024 + j * 128:3 * 1024 + (j + 1) * 128],
                        rowb(3, off=1), start=False, stop=False))
                    mmk = to(lambda e, j=j, jj=jj: e.matmul(
                        PP2[:, jj * TC:jj * TC + TC],
                        AUGW[0:1, 4 * 1024 + j * 128:4 * 1024 + (j + 1) * 128],
                        rowb(4, off=1), start=False, stop=True))
                for jj in range(SL):
                    j = SL * j2 + jj
                    wev = vo(lambda e, j=j, jj=jj: e.tensor_tensor(
                        out=fb["x1"][:, j * TC:(j + 1) * TC],
                        in0=PP2[:, jj * TC:jj * TC + TC],
                        in1=fb["x2"][:, j * TC:(j + 1) * TC], op=ALU.add),
                        waits=[("tensor", mmk)], rb="PP2")
            if debug and c == 0:
                dstore(dbg["d_x3"], fb["x1"][:, :], waits=[("vector", wev)])

            x4m = emit_ffn(fb["x1"], fb["x2"], [("vector", wev)], None, c)

            P.ph2_ffn2v = P.mark("vector")
            oev = 0
            for s in range(NSUB):
                for i in range(G):
                    idx = s * G + i
                    w = [("gpsimd", x4m)] if idx == 0 else []
                    if idx == 0 and c > 0:
                        w.append(("store", P.ph2_ost))
                    pb = (PTF, PTB)[idx % 2]
                    off = 128 * ((idx // 2) % 4)
                    tm = to(lambda e, s=s, i=i, pb=pb, off=off: e.transpose(
                        pb[:, off:off + 128],
                        fb["x2"][:, i * TC + s * 128:i * TC + s * 128 + 128],
                        IDENT[:, :]), waits=w, wb=("PTF", "PTB")[idx % 2])
                    oev = ao(lambda e, s=s, i=i, pb=pb, off=off: e.activation(
                        out=X0TOK[:, s * 1024 + i * 128:s * 1024 + (i + 1) * 128],
                        in_=pb[:, off:off + 128],
                        func=AF.Identity), waits=[("tensor", tm)],
                        rb=("PTF", "PTB")[idx % 2])
            ost = 0
            for s in range(NSUB):
                ost = P.store_dma(out_h[c * TC + s * 128:c * TC + (s + 1) * 128, :],
                                  X0TOK[:, s * 1024:(s + 1) * 1024],
                                  waits=[("scalar", oev)])
            P.ph2_ost = ost
            P.ph2_ffn2t = P.mark("tensor")

        return _finish()

    return nc


def _perm():
    p = np.zeros(D, dtype=np.int64)
    for h in range(H):
        for j in range(32):
            p[32 * h + j] = 64 * h + j
            p[512 + 32 * h + j] = 64 * h + 32 + j
    return p


def _vec_cols(inputs):
    v = np.zeros((128, 48), dtype=np.float32)

    def col8(x):
        return np.ascontiguousarray(x.reshape(8, 128).T)
    mix = np.asarray(inputs["L0_attn_a_mix"], np.float32)
    v[:, 0:8] = col8(mix)
    v[:, 8:16] = col8(np.float32(1.0) - mix)
    v[:, 16:24] = col8(np.asarray(inputs["L0_attn_a_global_decay"], np.float32))
    v[:, 24:32] = col8(np.asarray(inputs["L0_attn_b_gamma"], np.float32))
    v[:, 32:40] = col8(np.asarray(inputs["L0_attn_b_beta"], np.float32))
    v[:, 40] = 1.0
    v[:, 41] = 1.0 / D
    v[:, 42] = (np.arange(128) % 32).astype(np.float32)
    return v


def kernel(**inputs):
    n_chunks = T // TC
    nc = build_program(n_chunks, debug=False)
    perm = _perm()
    tC = lambda a: np.ascontiguousarray(np.asarray(a, dtype=np.float32))
    base = {
        "wrt": tC(np.asarray(inputs["L0_attn_a_W_R"]).T),
        "wkt": tC(np.asarray(inputs["L0_attn_a_W_K"]).T),
        "wvt": tC(np.asarray(inputs["L0_attn_a_W_V"]).T),
        "wtt": tC(np.asarray(inputs["L0_attn_a_W_td"]).T),
        "wq": tC(np.asarray(inputs["L0_attn_b_Wq"])[:, perm]),
        "wk": tC(np.asarray(inputs["L0_attn_b_Wk"])[:, perm]),
        "wv": tC(inputs["L0_attn_b_Wv"]),
        "wo": tC(inputs["L0_attn_b_Wo"]),
        "wg1": tC(inputs["L0_attn_b_Wg1"]),
        "wg2": tC(inputs["L0_attn_b_Wg2"]),
        "rhsT": tC(np.asarray(inputs["L0_ffn_rhs"]).T),
        "vecs": _vec_cols(inputs),
        "ident": np.eye(128, dtype=np.float32),
        "trimask": np.tril(np.ones((128, 128), np.float32)).T.copy(),
    }
    emb = np.asarray(inputs["L0_attn_a_embeddings"], np.float32)
    in_maps = []
    for core in range(8):
        m = dict(base)
        m["x0"] = np.ascontiguousarray(emb[core % B])
        in_maps.append(m)
    res = run_bass_kernel_spmd(nc, in_maps, core_ids=list(range(8)))
    out = np.stack([np.asarray(res.results[b]["out"]) for b in range(B)], axis=0)
    return out.astype(np.float32)
